# version 15
# baseline (speedup 1.0000x reference)
import math
import numpy as np
import ml_dtypes
import concourse.bass as bass
import concourse.mybir as mybir
from concourse.bass_utils import run_bass_kernel_spmd

F32 = mybir.dt.float32
BF16 = mybir.dt.bfloat16
AF = mybir.ActivationFunctionType
ALU = mybir.AluOpType
AX = mybir.AxisListType

ENGINES = ("tensor", "vector", "scalar", "gpsimd", "sync")

REAL_CFG = dict(T=2048, E=2048, NH=16, HHD=64, VD=128, QHD=192, NOPE=128,
                NE=16, I1=1408, TOPK=4, TU=1024, TAU=256)

NEG_BIG = -1.0e30


class Sem:
    def __init__(self, h):
        self.h = h
        self.n = 0


class KB:
    def __init__(self, cfg):
        self.cfg = dict(cfg)
        c = self.cfg
        c["EC"] = c["E"] // 128
        c["TT"] = c["T"] // 128
        c["IC"] = c["I1"] // 128
        c["SHI"] = 2 * c["I1"]
        c["NU"] = c["T"] // c["TU"]
        c["NTAU"] = c["T"] // c["TAU"]
        c["LI"] = 0.8 - 0.6 * math.exp(-0.3 * 12)
        assert c["E"] == c["NH"] * 2 * c["HHD"] == c["NH"] * c["VD"]
        assert c["T"] % 512 == 0 and c["TU"] % 512 == 0 and c["TAU"] % 128 == 0
        self.acts = {e: [] for e in ENGINES}
        self._exits = []
        self.S = {}
        self.cast_thr = {}
        self.psA_evac = None
        self.ptr_free = None
        self.ptr2_free = None
        self.psO_free = 0
        self.v_ones = 0
        self.wfull_done = None

    def sem(self, name):
        if name not in self.S:
            cm = self.nc.semaphore(name)
            h = cm.__enter__()
            self._exits.append(cm)
            self.S[name] = Sem(h)
        return self.S[name]

    def emit(self, eng, fn, waits=(), inc=None, k=1):
        waits = list(waits)
        if eng == "vector":
            s = self.S["sv"]
            waits.append((s, s.n))
            assert inc is None or inc is s
            inc, k = s, 1
        elif eng == "scalar":
            s = self.S["sa"]
            waits.append((s, s.n))
            assert inc is None or inc is s
            inc, k = s, 1
        wl = [(s.h, v) for (s, v) in waits if v]
        tgt = None
        if inc is not None:
            inc.n += k
            tgt = inc.n

        def run(e, wl=wl, fn=fn, inc=inc, k=k):
            for h, v in wl:
                e.wait_ge(h, v)
            r = fn(e)
            if inc is not None:
                r.then_inc(inc.h, k)
        self.acts[eng].append(run)
        return tgt

    def mm(self, out, lhsT, rhs, start, stop, waits=(), inc=None):
        return self.emit(
            "tensor",
            lambda e, o=out, l=lhsT, r=rhs, s=start, p=stop:
                e.matmul(o, l, r, start=s, stop=p),
            waits, inc)

    def tr(self, out, in_, ident, waits=(), inc=None):
        return self.emit(
            "tensor",
            lambda e, o=out, i=in_, d=ident: e.transpose(o, i, d),
            waits, inc)

    def dma(self, eng, out, in_, sem, waits=(), transpose=False, **kw):
        def fn(e, o=out, i=in_, t=transpose, kw=dict(kw)):
            return e.dma_start(o, i, transpose=t, **kw)
        self.emit(eng, fn, waits, inc=sem, k=16)
        return (sem, sem.n)

    def barrier(self, exclude=()):
        waits = tuple((s.h, s.n) for nm, s in self.S.items()
                      if s.n > 0 and nm not in exclude)
        for eng in ENGINES:
            def fn(e, wl=waits):
                for h, v in wl:
                    e.wait_ge(h, v)
            self.acts[eng].append(fn)

    def alloc_sbuf(self, name, shape, dt):
        cm = self.nc.sbuf_tensor(name, shape, dt)
        return cm.__enter__(), cm

    def alloc_psum(self, name, shape, dt):
        cm = self.nc.psum_tensor(name, shape, dt)
        return cm.__enter__(), cm

    def build(self):
        c = self.cfg
        T, E, EC = c["T"], c["E"], c["EC"]
        NH, HHD, VD, QHD, NOPE = c["NH"], c["HHD"], c["VD"], c["QHD"], c["NOPE"]
        NE, I1, SHI = c["NE"], c["I1"], c["SHI"]

        IC = c["IC"]
        nc = bass.Bass("TRN2", target_bir_lowering=False)
        self.nc = nc
        dram = lambda n, sh, dt=F32, kind="ExternalInput": nc.dram_tensor(n, sh, dt, kind=kind).ap()

        xq = dram("xq", [T, E])
        xt = dram("xt", [128, EC * T])
        wq_t = dram("wq_t", [NH, 128, E]); wk_t = dram("wk_t", [NH, 128, E])
        wv_t = dram("wv_t", [NH, 128, E]); wo_t = dram("wo_t", [EC, 128, E])
        lq1 = dram("lq1", [HHD]); lk1 = dram("lk1", [HHD])
        lq2 = dram("lq2", [HHD]); lk2 = dram("lk2", [HHD])
        subw = dram("subw", [2 * HHD])
        ln1w = dram("ln1w", [E]); ln2w = dram("ln2w", [E])
        qn_t = dram("qn_t", [NH, 128, E]); kw_t = dram("kw_t", [NH, 128, E])
        vw_t = dram("vw_t", [NH, 128, E]); ow_t = dram("ow_t", [EC, 128, E])
        gw_t = dram("gw_t", [128, EC * NE])
        w1_t = dram("w1_t", [NE, 2 * IC, 128, E])
        w2_t = dram("w2_t", [NE, EC, 128, I1])
        shgu_t = dram("shgu_t", [4 * IC, 128, E])
        shdn_t = dram("shdn_t", [2, EC, 128, I1])
        identb = dram("identb", [128, 128], BF16)
        identf = dram("identf", [128, 128], F32)
        cmask = dram("cmask", [128, 128], F32)
        out = dram("out", [T, E], F32, kind="ExternalOutput")

        idram = lambda n, sh, dt: nc.dram_tensor(n, sh, dt, kind="Internal").ap()
        wq_bf = idram("wq_bf", [NH, 128, E], BF16)
        wk_bf = idram("wk_bf", [NH, 128, E], BF16)
        wv_bf = idram("wv_bf", [NH, 128, E], BF16)
        wo_bf = idram("wo_bf", [EC, 128, E], BF16)
        qn_bf = idram("qn_bf", [NH, 128, E], BF16)
        kw_bf = idram("kw_bf", [NH, 128, E], BF16)
        vw_bf = idram("vw_bf", [NH, 128, E], BF16)
        ow_bf = idram("ow_bf", [EC, 128, E], BF16)
        gw_bf = idram("gw_bf", [128, EC * NE], BF16)
        w1_bf = idram("w1_bf", [NE, 2 * IC, 128, E], BF16)
        w2_bf = idram("w2_bf", [NE, EC, 128, I1], BF16)
        shgu_bf = idram("shgu_bf", [4 * IC, 128, E], BF16)
        shdn_bf = idram("shdn_bf", [2, EC, 128, I1], BF16)
        res1 = idram("res1", [T, E], F32)
        res2 = idram("res2", [T, E], F32)
        wfull_d = idram("wfull_d", [NE, T], F32)

        sp = self.sem("sp"); sv = self.sem("sv"); sa = self.sem("sa")

        sb = self.alloc_sbuf
        xT, _ = sb("xT", [128, EC * T], BF16)
        identb_s, _ = sb("identb_s", [128, 128], BF16)
        identf_s, _ = sb("identf_s", [128, 128], F32)
        cmask_s, _ = sb("cmask_s", [128, 128], F32)
        onesb, _ = sb("onesb", [1, 128], BF16)
        onesf, _ = sb("onesf", [1, 128], F32)
        lam_bc, _ = sb("lam_bc", [128, 1], F32)
        g_bc, _ = sb("g_bc", [128, 2 * HHD], F32)
        ln1w_bf, _ = sb("ln1w_bf", [1, E], BF16)
        ln2w_bf, _ = sb("ln2w_bf", [1, E], BF16)
        lnv, _ = sb("lnv", [1, 6 * HHD], F32)
        epsv, _ = sb("epsv", [128, 4], F32)
        self.epsv = epsv
        lprod, _ = sb("lprod", [1, 2 * HHD], F32)
        sml, _ = sb("sml", [1, 16], F32)

        xT3 = xT.ap().rearrange("p (c t) -> p c t", c=EC)

        def flat8k(ap):
            sh = list(ap.shape)
            v = ap
            while len(v.shape) > 2:
                n = len(v.shape)
                names = " ".join(chr(97 + i) for i in range(n))
                tail = " ".join(chr(97 + i) for i in range(n - 2, n))
                head = " ".join(chr(97 + i) for i in range(n - 2))
                v = v.rearrange(f"{names} -> {head} ({tail})".strip())
            a, b = v.shape
            assert (a * b) % 8192 == 0, sh
            if b % 8192 == 0 and b > 8192:
                return v.rearrange("a (r q) -> (a r) q", q=8192)
            if b == 8192:
                return v
            return v.rearrange("a b -> (a b)").rearrange("(r q) -> r q", q=8192)

        s_xT = self.sem("c_xT")
        self.w_xT = self.dma(
            "gpsimd", xT.ap().rearrange("p (a q) -> p a q", q=8192),
            xt.rearrange("p (a q) -> p a q", q=8192), s_xT, max_dma_last_dim=8192)

        groups = dict(
            c_qkv=[("Wq", wq_bf, wq_t), ("Wk", wk_bf, wk_t), ("Wv", wv_bf, wv_t)],
            c_p2=[("Wo", wo_bf, wo_t), ("qw", qn_bf, qn_t), ("kw", kw_bf, kw_t),
                  ("vw", vw_bf, vw_t), ("ow", ow_bf, ow_t), ("gw", gw_bf, gw_t),
                  ("ln1w", ln1w_bf[:], ln1w.rearrange("(a e) -> a e", a=1)),
                  ("ln2w", ln2w_bf[:], ln2w.rearrange("(a e) -> a e", a=1))],
            c_moe1=[(f"w1_{e}", w1_bf[e], w1_t[e]) for e in range(NE // 2)] +
                   [(f"w2_{e}", w2_bf[e], w2_t[e]) for e in range(NE // 2)],
            c_moe2=[(f"w1_{e}", w1_bf[e], w1_t[e]) for e in range(NE // 2, NE)] +
                   [(f"w2_{e}", w2_bf[e], w2_t[e]) for e in range(NE // 2, NE)] +
                   [("shgu", shgu_bf, shgu_t), ("shdn", shdn_bf, shdn_t)],
        )
        for gname, items in groups.items():
            gs = self.sem(gname)
            for name, dst, src in items:
                if name in ("ln1w", "ln2w", "gw"):
                    self.dma("gpsimd", dst, src, gs, max_dma_last_dim=8192)
                else:
                    self.dma("gpsimd", flat8k(dst), flat8k(src), gs,
                             max_dma_last_dim=8192)
            for name, dst, src in items:
                self.cast_thr[name] = (gs, gs.n)

        r1f = lambda ap: ap.rearrange("(a h) -> a h", a=1)
        s_cn = self.sem("c_const")
        self.dma("sync", identb_s[:], identb, s_cn)
        self.dma("sync", identf_s[:], identf, s_cn)
        self.dma("sync", cmask_s[:], cmask, s_cn)
        self.dma("sync", lnv[:, 0:HHD], r1f(lq1), s_cn)
        self.dma("sync", lnv[:, HHD:2 * HHD], r1f(lk1), s_cn)
        self.dma("sync", lnv[:, 2 * HHD:3 * HHD], r1f(lq2), s_cn)
        self.dma("sync", lnv[:, 3 * HHD:4 * HHD], r1f(lk2), s_cn)
        self.dma("sync", lnv[:, 4 * HHD:6 * HHD], r1f(subw), s_cn)
        w_consts = (s_cn, s_cn.n)
        self.w_consts = w_consts

        self.emit("vector", lambda e: e.memset(onesb[:], 1.0))
        self.emit("vector", lambda e: e.memset(epsv[:, 0:1], 1e-5))
        self.emit("vector", lambda e: e.memset(epsv[:, 1:2], 1e-6))
        self.emit("vector", lambda e: e.memset(onesf[:], 1.0), inc=sv)
        self.v_ones = sv.n
        self.emit("vector", lambda e: e.tensor_tensor(lprod[:, 0:HHD], lnv[:, 0:HHD],
                                                      lnv[:, HHD:2 * HHD], ALU.mult),
                  waits=[w_consts])
        self.emit("vector", lambda e: e.tensor_tensor(lprod[:, HHD:2 * HHD], lnv[:, 2 * HHD:3 * HHD],
                                                      lnv[:, 3 * HHD:4 * HHD], ALU.mult))
        self.emit("vector", lambda e: e.tensor_reduce(sml[:, 0:1], lprod[:, 0:HHD], AX.X, ALU.add))
        self.emit("vector", lambda e: e.tensor_reduce(sml[:, 1:2], lprod[:, HHD:2 * HHD], AX.X, ALU.add),
                  inc=sv)
        v_ls = sv.n
        self.emit("scalar", lambda e: e.activation(sml[:, 2:3], sml[:, 0:1], AF.Exp),
                  waits=[(sv, v_ls)])
        self.emit("scalar", lambda e: e.activation(sml[:, 3:4], sml[:, 1:2], AF.Exp), inc=sa)
        a_le = sa.n
        self.emit("vector", lambda e: e.tensor_tensor(sml[:, 4:5], sml[:, 2:3], sml[:, 3:4], ALU.subtract),
                  waits=[(sa, a_le)])
        self.emit("vector", lambda e: e.tensor_scalar(sml[:, 5:6], sml[:, 4:5], c["LI"], None, ALU.add),
                  inc=sv)
        v_lam = sv.n

        psP, psP_cm = self.alloc_psum("psP", [128, 512], F32)
        self.mm(psP[:, 0:1], onesf[:], sml[:, 5:6], True, True,
                waits=[(sv, v_lam)], inc=sp)
        p_lam = sp.n
        self.emit("vector", lambda e: e.tensor_copy(lam_bc[:], psP[:, 0:1]),
                  waits=[(sp, p_lam)], inc=sv)
        v_lbc = sv.n
        self.mm(psP[:, 128:128 + 2 * HHD], onesf[:], lnv[:, 4 * HHD:6 * HHD], True, True,
                waits=[(sv, v_lbc)], inc=sp)
        p_g = sp.n
        self.emit("vector", lambda e: e.tensor_scalar(g_bc[:], psP[:, 128:128 + 2 * HHD],
                                                      1.0 - c["LI"], None, ALU.mult),
                  waits=[(sp, p_g)], inc=sv)
        psP_cm.__exit__(None, None, None)
        self.barrier(exclude=("c_xT", "c_qkv", "c_p2", "c_moe1", "c_moe2"))

        PH = c.get("PH", 6)
        aoT, aoT_cm = sb("aoT", [128, EC * T], BF16)
        aoT3 = aoT.ap().rearrange("p (c t) -> p c t", c=EC)

        if PH >= 1:
            self.attn_phase(True, xT3, aoT3, (wq_bf, wk_bf, wv_bf), ("Wq", "Wk", "Wv"),
                            HHD ** -0.5, identb_s, identf_s, cmask_s, g_bc, lam_bc)
        if PH >= 2:
            self.proj_residual_epilogue("p1", aoT3, wo_bf, "Wo", xq, res1,
                                        ln1w_bf, "ln1w", xT3, identb_s, identf_s, onesb)
        if PH >= 3:
            self.attn_phase(False, xT3, aoT3, (qn_bf, kw_bf, vw_bf), ("qw", "kw", "vw"),
                            QHD ** -0.5, identb_s, identf_s, cmask_s, None, None)
        if PH >= 4:
            self.proj_residual_epilogue("p2", aoT3, ow_bf, "ow", res1, res2,
                                        ln2w_bf, "ln2w", xT3, identb_s, identf_s, onesb)
        aoT_cm.__exit__(None, None, None)

        if PH >= 5:
            self.gate_phase(xT3, gw_bf, wfull_d, identf_s)

        if PH >= 6:
            self.moe_phase(xT3, w1_bf, w2_bf, shgu_bf, shdn_bf, wfull_d, res1, res2,
                           out, onesf, identf_s)


        with nc.Block() as block:
            for eng in ENGINES:
                acts = self.acts[eng]
                if not acts:
                    continue
                def mk(acts):
                    def body(e):
                        for a in acts:
                            a(e)
                    return body
                getattr(block, eng)(mk(acts))
        for cm in self._exits:
            cm.__exit__(None, None, None)
        return nc

    def attn_phase(self, is_diff, xT3, aoT3, w_srcs, cast_keys,
                   scale, identb_s, identf_s, cmask_s, g_bc, lam_bc):
        c = self.cfg
        T, EC, TT, NH = c["T"], c["EC"], c["TT"], c["NH"]
        sp, sv, sa = (self.S[k] for k in ("sp", "sv", "sa"))
        pre = "d" if is_diff else "m"
        cms = []
        def sba(name, shape, dt):
            h, cm = self.alloc_sbuf(pre + name, shape, dt)
            cms.append(cm)
            return h
        qT = sba("qT", [128, T], BF16)
        kT = sba("kT", [128, T], BF16)
        vfm = sba("vfm", [128, T], BF16)
        v_tm = sba("v_tm", [128, T], BF16)
        Pa = sba("Pa", [128, T], BF16)
        Pb = sba("Pb", [128, T], BF16)
        PT4 = [sba(f"PT{j}", [128, T], BF16) for j in range(4 if is_diff else 2)]
        wb = [[sba(f"w{w}b{j}", [128, EC * 128], BF16) for j in range(2)] for w in range(3)]
        stat = sba("stat", [128, 64], F32)
        Ocmb = sba("Ocmb", [128, 128], F32)
        Osqt = sba("Osqt", [128, 128], F32)
        On = [sba(f"On{j}", [128, 128], BF16) for j in range(2)]

        psS, psS_cm = self.alloc_psum(pre + "psS", [128, 2048], F32)
        psO, psO_cm = self.alloc_psum(pre + "psO", [128, 512], F32)
        ptrb, ptrb_cm = self.alloc_psum(pre + "ptrb", [128, 2048], BF16)
        ptr2, ptr2_cm = self.alloc_psum(pre + "ptr2", [128, 1024], BF16)
        pcms = [psS_cm, psO_cm, ptrb_cm, ptr2_cm]

        v_tm3 = v_tm.ap().rearrange("p (c d) -> p c d", c=TT)
        NB = T // 512

        wsem = [[self.sem(f"{pre}wl{w}{j}") for j in range(2)] for w in range(3)]
        wload = [[None, None] for _ in range(3)]
        wfree = [[0, 0] for _ in range(3)]

        def issue_load(h):
            slot = h % 2
            for widx in range(3):
                waits = [self.cast_thr[cast_keys[widx]]]
                if wfree[widx][slot]:
                    waits.append((sp, wfree[widx][slot]))
                wload[widx][slot] = self.dma(
                    "sync", wb[widx][slot][:], w_srcs[widx][h], wsem[widx][slot],
                    waits=waits)

        halves = ((0, 0), (1, 64)) if is_diff else ((0, 0),)
        kdim = 64 if is_diff else 128
        last_hl = halves[-1][0]
        ZB = lambda zp, hl, k: (zp * 2 + hl) * 4 + k
        SSQ, LNV = 60, 61
        RSTD = lambda par: 56 + par

        psS_free = [None, None]
        ptrb_free = [None, None]
        ptr2_free = [None]
        psO_free = [None, None]
        z_free = {}
        ptr_done = {}
        On_free = [None, None]
        ocmb_free = [None]
        rstd_free = [None, None]
        ckp = [0]
        tpp = [0]

        issue_load(0)
        for h in range(NH):
            slot = h % 2
            if h + 1 < NH:
                issue_load(h + 1)

            evs = []
            proj_evac = [None] * NB
            for widx, dst in ((0, qT), (1, kT), (2, vfm)):
                wb3 = wb[widx][slot].ap().rearrange("p (c r) -> p c r", c=EC)
                for kc in range(EC):
                    for nb in range(NB):
                        w = []
                        if kc == 0:
                            if nb == 0:
                                w.append(wload[widx][slot])
                                if widx == 0:
                                    w.append(self.w_xT)
                                    for f in psS_free:
                                        if f:
                                            w.append(f)
                            if proj_evac[nb]:
                                w.append(proj_evac[nb])
                        self.mm(psS[:, nb * 512:(nb + 1) * 512], wb3[:, kc, :],
                                xT3[:, kc, nb * 512:(nb + 1) * 512],
                                start=(kc == 0), stop=(kc == EC - 1),
                                waits=w,
                                inc=sp if kc == EC - 1 else None)
                p_all = sp.n
                for nb in range(NB):
                    self.emit("vector",
                              lambda e, d=dst, nb=nb:
                              e.tensor_copy(d[:, nb * 512:(nb + 1) * 512],
                                            psS[:, nb * 512:(nb + 1) * 512]),
                              waits=[(sp, p_all - NB + 1 + nb)], inc=sv)
                    proj_evac[nb] = (sv, sv.n)
                wfree[widx][slot] = sp.n
                evs.append(sv.n)
            v_qk = evs[1]
            v_vfm = evs[2]
            psS_free = [(sv, evs[2]), (sv, evs[2])]

            for grp in range((TT + 7) // 8):
                tiles = list(range(grp * 8, min(TT, grp * 8 + 8)))
                for j, ct in enumerate(tiles):
                    w = []
                    if j == 0:
                        w.append((sv, v_vfm))
                        if ptr2_free[0]:
                            w.append(ptr2_free[0])
                    self.tr(ptr2[:, j * 128:(j + 1) * 128], vfm[:, ct * 128:(ct + 1) * 128],
                            identb_s[:], waits=w, inc=sp if j == len(tiles) - 1 else None)
                p_tr = sp.n
                g0, gn = tiles[0], len(tiles)
                self.emit("vector",
                          lambda e, g0=g0, gn=gn:
                          e.tensor_copy(v_tm[:, g0 * 128:(g0 + gn) * 128], ptr2[:, 0:gn * 128]),
                          waits=[(sp, p_tr)], inc=sv)
                ptr2_free[0] = (sv, sv.n)
            v_vtm = sv.n

            def emit_S(b):
                ti, pbase = b["ti"], b["pbase"]
                L = 128 * (ti + 1)
                b["chunks"] = []
                for c0 in range(0, L, 1024):
                    c1 = min(L, c0 + 1024)
                    p = ckp[0] & 1
                    ckp[0] += 1
                    base = p * 1024
                    diag = c1 == L
                    first = True

                    def fw():
                        nonlocal first
                        w = []
                        if first:
                            first = False
                            if psS_free[p]:
                                w.append(psS_free[p])
                            if ti == 0:
                                w.append((sv, v_qk))
                        return w
                    mend = c1 - 128 if diag else c1
                    n0s = list(range(c0, mend, 512))
                    if diag:
                        self.tr(psS[:, base + L - 128 - c0:base + L - c0],
                                cmask_s[:], identf_s[:], waits=fw())
                        self.emit(
                            "tensor",
                            lambda e, o=psS[:, base + L - 128 - c0:base + L - c0],
                            l=qT[pbase:pbase + kdim, ti * 128:(ti + 1) * 128],
                            r=kT[pbase:pbase + kdim, L - 128:L]:
                            e.matmul(o, l, r, start=False, stop=True,
                                     skip_group_check=True),
                            fw(), inc=None if n0s else sp)
                    for n0 in n0s:
                        n1 = min(mend, n0 + 512)
                        self.mm(psS[:, base + n0 - c0:base + n1 - c0],
                                qT[pbase:pbase + kdim, ti * 128:(ti + 1) * 128],
                                kT[pbase:pbase + kdim, n0:n1],
                                True, True, waits=fw(),
                                inc=sp if n1 == mend else None)
                    p_sc = sp.n
                    b["chunks"].append(dict(p=p, base=base, c0=c0, c1=c1,
                                            p_sc=p_sc, v_mask=0))

            def emit_exp(b):
                hl, zp, P_sb = b["hl"], b["zp"], b["P"]
                b["exp"] = []
                for ci, ch in enumerate(b["chunks"]):
                    w = [(sp, ch["p_sc"])]
                    if ch["v_mask"]:
                        w.append((sv, ch["v_mask"]))
                    if ci == 0:
                        if z_free.get((zp, hl)):
                            w.append((sv, z_free[(zp, hl)]))
                        if ptr_done.get(b["pk"]):
                            w.append((sp, ptr_done[b["pk"]]))
                    zc = ZB(zp, hl, ci)
                    self.emit("scalar",
                              lambda e, P=P_sb, ch=ch, zc=zc, sk=scale:
                              e.activation(P[:, ch["c0"]:ch["c1"]],
                                           psS[:, ch["base"]:ch["base"] + ch["c1"] - ch["c0"]],
                                           AF.Exp, scale=sk,
                                           accum_out=stat[:, zc:zc + 1]),
                              waits=w, inc=sa)
                    b["exp"].append(sa.n)
                    psS_free[ch["p"]] = (sa, sa.n)

            def emit_T(b):
                ti = b["ti"]
                P_sb, PT_sb = b["P"], b["PT"]
                nt = ti + 1
                b["evacs"] = []
                for g0 in range(0, nt, 8):
                    gn = min(8, nt - g0)
                    tp = tpp[0] & 1
                    tpp[0] += 1
                    tbase = tp * 1024
                    for j in range(gn):
                        ct = g0 + j
                        w = []
                        if j == 0:
                            w.append((sa, b["exp"][g0 // 8]))
                            if ptrb_free[tp]:
                                w.append(ptrb_free[tp])
                        self.tr(ptrb[:, tbase + j * 128:tbase + (j + 1) * 128],
                                P_sb[:, ct * 128:(ct + 1) * 128], identb_s[:],
                                waits=w, inc=sp if j == gn - 1 else None)
                    p_tr = sp.n
                    if g0 == 0:
                        self.emit("scalar",
                                  lambda e, PT=PT_sb, gn=gn, tbase=tbase:
                                  e.copy(PT[:, 0:gn * 128],
                                         ptrb[:, tbase:tbase + gn * 128]),
                                  waits=[(sp, p_tr)], inc=sa)
                        ptrb_free[tp] = (sa, sa.n)
                        b["evacs"].append((sa, sa.n))
                    else:
                        self.emit("vector",
                                  lambda e, PT=PT_sb, g0=g0, gn=gn, tbase=tbase:
                                  e.tensor_copy(PT[:, g0 * 128:(g0 + gn) * 128],
                                                ptrb[:, tbase:tbase + gn * 128]),
                                  waits=[(sp, p_tr)], inc=sv)
                        ptrb_free[tp] = (sv, sv.n)
                        b["evacs"].append((sv, sv.n))
                ptr_done[b["pk"]] = sp.n

            def emit_AV(b):
                ti, hl, par = b["ti"], b["hl"], b["par"]
                PT_sb = b["PT"]
                nt = ti + 1
                ob = par * 256 + hl * 128
                for ct in range(nt):
                    w = []
                    if ct % 8 == 0:
                        w.append(b["evacs"][ct // 8])
                    if ct == 0:
                        w.append((sv, v_vtm))
                        if psO_free[par]:
                            w.append(psO_free[par])
                    self.mm(psO[:, ob:ob + 128],
                            PT_sb[:, ct * 128:(ct + 1) * 128], v_tm3[:, ct, :],
                            start=(ct == 0), stop=(ct == nt - 1),
                            waits=w, inc=sp if ct == nt - 1 else None)
                b["av"] = sp.n

            def emit_combine(tb):
                ti, par, zp = tb[0]["ti"], tb[0]["par"], tb[0]["zp"]
                wz = [(sa, tb[-1]["exp"][-1])]
                for b in tb:
                    z0 = ZB(zp, b["hl"], 0)
                    if len(b["chunks"]) == 2:
                        self.emit("vector",
                                  lambda e, z0=z0: e.tensor_tensor(
                                      stat[:, z0:z0 + 1], stat[:, z0:z0 + 1],
                                      stat[:, z0 + 1:z0 + 2], ALU.add),
                                  waits=wz)
                        wz = []
                    self.emit("vector",
                              lambda e, z0=z0: e.reciprocal(stat[:, z0 + 2:z0 + 3],
                                                            stat[:, z0:z0 + 1]),
                              waits=wz)
                    wz = []
                    z_free[(zp, b["hl"])] = sv.n
                ob = par * 256
                if is_diff:
                    r1 = ZB(zp, 0, 2)
                    r2 = ZB(zp, 1, 2)
                    cf = ZB(zp, 1, 3)
                    self.emit("vector",
                              lambda e, r2=r2, cf=cf: e.tensor_scalar(
                                  stat[:, cf:cf + 1], stat[:, r2:r2 + 1],
                                  lam_bc[:], -1.0, ALU.mult, ALU.mult))
                    w = [(sp, tb[-1]["av"])]
                    if ocmb_free[0]:
                        w.append(ocmb_free[0])
                    self.emit("vector",
                              lambda e, ob=ob, r1=r1: e.tensor_scalar(
                                  Ocmb[:], psO[:, ob:ob + 128],
                                  stat[:, r1:r1 + 1], None, ALU.mult),
                              waits=w)
                    self.emit("vector",
                              lambda e, ob=ob, cf=cf: e.scalar_tensor_tensor(
                                  Ocmb[:], psO[:, ob + 128:ob + 256],
                                  stat[:, cf:cf + 1], Ocmb[:], ALU.mult, ALU.add),
                              inc=sv)
                    psO_free[par] = (sv, sv.n)
                    v_oc = sv.n
                    self.emit("scalar",
                              lambda e: e.activation(Osqt[:], Ocmb[:], AF.Square,
                                                     accum_out=stat[:, SSQ:SSQ + 1]),
                              waits=[(sv, v_oc)])
                    self.emit("scalar",
                              lambda e: e.activation(stat[:, LNV:LNV + 1],
                                                     stat[:, SSQ:SSQ + 1], AF.Ln,
                                                     bias=self.epsv[:, 0:1],
                                                     scale=1.0 / 128.0))
                    w = [rstd_free[par]] if rstd_free[par] else []
                    self.emit("scalar",
                              lambda e, par=par: e.activation(
                                  stat[:, RSTD(par):RSTD(par) + 1],
                                  stat[:, LNV:LNV + 1], AF.Exp, scale=-0.5),
                              waits=w, inc=sa)
                    a_rstd = sa.n
                    ocmb_free[0] = (sa, a_rstd)
                    w = [(sa, a_rstd)]
                    if On_free[par]:
                        w.append(On_free[par])
                    self.emit("vector",
                              lambda e, par=par: e.scalar_tensor_tensor(
                                  On[par][:], Ocmb[:], stat[:, RSTD(par):RSTD(par) + 1],
                                  g_bc[:], ALU.mult, ALU.mult),
                              waits=w, inc=sv)
                    rstd_free[par] = (sv, sv.n)
                else:
                    r1 = ZB(zp, 0, 2)
                    w = [(sp, tb[0]["av"])]
                    if On_free[par]:
                        w.append(On_free[par])
                    self.emit("vector",
                              lambda e, ob=ob, r1=r1, par=par: e.tensor_scalar(
                                  On[par][:], psO[:, ob:ob + 128],
                                  stat[:, r1:r1 + 1], None, ALU.mult),
                              waits=w, inc=sv)
                    psO_free[par] = (sv, sv.n)
                v_on = sv.n
                w = [(sv, v_on)]
                if ptr2_free[0]:
                    w.append(ptr2_free[0])
                self.tr(ptr2[:, 0:128], On[par][:], identb_s[:], waits=w, inc=sp)
                p_ot = sp.n
                On_free[par] = (sp, p_ot)
                self.emit("vector",
                          lambda e, h=h, ti=ti: e.tensor_copy(
                              aoT3[:, h, ti * 128:(ti + 1) * 128], ptr2[:, 0:128]),
                          waits=[(sp, p_ot)], inc=sv)
                ptr2_free[0] = (sv, sv.n)

            pend = []
            avq = []
            tile_blocks = []

            def flush_T():
                b0 = pend.pop(0)
                emit_T(b0)
                avq.append(b0)

            def flush_AV():
                b0 = avq.pop(0)
                emit_AV(b0)
                tile_blocks.append(b0)
                if b0["hl"] == last_hl:
                    emit_combine(tile_blocks)
                    tile_blocks.clear()

            for ti in range(TT):
                for (hl, pbase) in halves:
                    if is_diff:
                        P_sb = (Pa, Pb)[hl]
                        PT_sb = PT4[hl * 2 + (ti & 1)]
                    else:
                        P_sb = (Pa, Pb)[ti & 1]
                        PT_sb = PT4[ti & 1]
                    b = dict(ti=ti, hl=hl, pbase=pbase, P=P_sb, PT=PT_sb, par=ti & 1,
                             zp=ti % 3, pk=(hl if is_diff else ti & 1))
                    emit_S(b)
                    while len(pend) >= 2:
                        flush_T()
                    while len(avq) >= 1:
                        flush_AV()
                    emit_exp(b)
                    pend.append(b)
            while pend:
                flush_T()
            while avq:
                flush_AV()

        self.barrier()
        for cm in reversed(cms + pcms):
            cm.__exit__(None, None, None)

    def proj_residual_epilogue(self, mode, aoT3, w_bf, cast_key, resid_src, spill,
                               lnw_bf, lnw_key, xT3, identb_s, identf_s, onesb):
        c = self.cfg
        T, E, EC, TAU, NTAU = c["T"], c["E"], c["EC"], c["TAU"], c["NTAU"]
        sp, sv, sa = (self.S[k] for k in ("sp", "sv", "sa"))
        cms = []
        def sba(name, shape, dt):
            h, cm = self.alloc_sbuf(mode + name, shape, dt)
            cms.append(cm)
            return h
        wob = [sba(f"wob{j}", [128, EC * 128], BF16) for j in range(3)]
        tmpy = [sba(f"tmpy{j}", [128, 512], F32) for j in range(2)]
        h1blk = sba("h1blk", [128, 2 * E], F32)
        xload = [sba(f"xload{j}", [128, E], F32) for j in range(2)]
        h1n = sba("h1n", [128, E], F32)
        xlnt = sba("xlnt", [128, E], BF16)
        nq = E // 512
        stats = sba("stats", [128, 6 * nq + 8], F32)
        lnwbc = sba("lnwbc", [128, E], BF16)
        psA, psA_cm = self.alloc_psum(mode + "psA", [128, 2048], F32)
        psT, psT_cm = self.alloc_psum(mode + "psT", [128, 1024], F32)
        ptrb, ptrb_cm = self.alloc_psum(mode + "ptrb", [128, 1024], BF16)
        pcms = [psA_cm, psT_cm, ptrb_cm]
        S = 6 * nq
        NT2 = TAU // 128
        psT_free = [None, None]
        ptrb_free = None

        wosem = [self.sem(f"{mode}wo{j}") for j in range(3)]
        xlsem = [self.sem(f"{mode}xl{j}") for j in range(2)]
        spsem = [self.sem(f"{mode}sp{j}") for j in range(2)]

        for nb in range(nq):
            w = []
            if nb == 0:
                w.append(self.cast_thr[lnw_key])
                w.append((sv, self.v_ones))
            self.mm(psA[:, nb * 512:(nb + 1) * 512], onesb[:],
                    lnw_bf[:, nb * 512:(nb + 1) * 512], True, True,
                    waits=w, inc=sp if nb == nq - 1 else None)
        p_lnw = sp.n
        for nb in range(nq):
            self.emit("vector",
                      lambda e, nb=nb: e.tensor_copy(lnwbc[:, nb * 512:(nb + 1) * 512],
                                                     psA[:, nb * 512:(nb + 1) * 512]),
                      waits=[(sp, p_lnw)] if nb == 0 else (),
                      inc=sv if nb == nq - 1 else None)
        psA_free = (sv, sv.n)

        wo_done = [None] * 3
        wo_free = [0] * 3
        xl_free = [0, 0]
        xl_done = [None, None]
        tmpy_free = [0, 0]
        h1n_free = None
        hb_free = [None, None]
        xlnt_free = 0

        pairs = [(tau, m) for tau in range(NTAU) for m in range(EC)]

        def load_wo(idx):
            tau, m = pairs[idx]
            slot = idx % 3
            waits = [self.cast_thr[cast_key]]
            if wo_free[slot]:
                waits.append((sp, wo_free[slot]))
            wo_done[slot] = self.dma(
                "sync", wob[slot][:], w_bf[m], wosem[slot], waits=waits)

        load_wo(0)
        load_wo(1)
        for tau in range(NTAU):
            for tt in range(NT2):
                slot = tt % 2
                r0 = tau * TAU + tt * 128
                w = [(sv, xl_free[slot])] if xl_free[slot] else []
                xl_done[slot] = self.dma("sync", xload[slot][:],
                                         resid_src[r0:r0 + 128, :], xlsem[slot], waits=w)

            for pi in range(EC // 2):
                m0 = 2 * pi
                m1 = m0 + 1
                idx0 = tau * EC + m0
                s0, s1 = idx0 % 3, (idx0 + 1) % 3
                wb30 = wob[s0].ap().rearrange("p (c r) -> p c r", c=EC)
                wb31 = wob[s1].ap().rearrange("p (c r) -> p c r", c=EC)
                t0 = tau * TAU
                for kc in range(EC):
                    w0, w1 = [], []
                    if kc == 0:
                        w0.append(wo_done[s0])
                        w1.append(wo_done[s1])
                        if psA_free:
                            w0.append(psA_free)
                    self.mm(psA[:, 0:TAU], wb30[:, kc, :], aoT3[:, kc, t0:t0 + TAU],
                            start=(kc == 0), stop=(kc == EC - 1),
                            waits=w0, inc=sp if kc == EC - 1 else None)
                    self.mm(psA[:, 512:512 + TAU], wb31[:, kc, :], aoT3[:, kc, t0:t0 + TAU],
                            start=(kc == 0), stop=(kc == EC - 1),
                            waits=w1, inc=sp if kc == EC - 1 else None)
                p_y1 = sp.n
                wo_free[s0] = p_y1 - 1
                wo_free[s1] = p_y1
                for pf in (idx0 + 2, idx0 + 3):
                    if pf < len(pairs):
                        load_wo(pf)
                for mi, (pv, off) in enumerate(((p_y1 - 1, 0), (p_y1, 512))):
                    w = [(sp, pv)]
                    if tmpy_free[mi]:
                        w.append((sv, tmpy_free[mi]))
                    self.emit("vector",
                              lambda e, sl=mi, off=off: e.tensor_copy(
                                  tmpy[sl][:, 0:TAU], psA[:, off:off + TAU]),
                              waits=w, inc=sv)
                v_y = sv.n
                psA_free = (sv, v_y)
                for tt in range(NT2):
                    for mi, m in enumerate((m0, m1)):
                        tp = (tt * 2 + mi) & 1
                        wtr = [(sv, v_y)] if (tt == 0 and mi == 0) else []
                        if psT_free[tp]:
                            wtr.append(psT_free[tp])
                        self.tr(psT[:, tp * 512:tp * 512 + 128],
                                tmpy[mi][:, tt * 128:(tt + 1) * 128],
                                identf_s[:], waits=wtr, inc=sp)
                        p_tr = sp.n
                        wadd = [(sp, p_tr), xl_done[tt % 2]]
                        if hb_free[tt % 2]:
                            wadd.append(hb_free[tt % 2])
                        self.emit("vector",
                                  lambda e, tt=tt, m=m, sl=tt % 2, tp=tp:
                                  e.tensor_tensor(h1blk[:, tt * E + m * 128:tt * E + (m + 1) * 128],
                                                  psT[:, tp * 512:tp * 512 + 128],
                                                  xload[sl][:, m * 128:(m + 1) * 128],
                                                  ALU.add),
                                  waits=wadd, inc=sv)
                        psT_free[tp] = (sv, sv.n)
                tmpy_free[0] = sv.n
                tmpy_free[1] = sv.n
            v_h1 = sv.n

            for tt in range(NT2):
                hb = h1blk.ap()[:, tt * E:(tt + 1) * E]
                r0 = tau * TAU + tt * 128

                if mode == "p1":
                    for q in range(nq):
                        self.emit("vector",
                                  lambda e, q=q, hb=hb: e.bn_stats(stats[:, q * 6:(q + 1) * 6],
                                                                   hb[:, q * 512:(q + 1) * 512]),
                                  waits=[(sv, v_h1)] if q == 0 else ())
                    self.emit("vector", lambda e: e.bn_aggr(stats[:, S:S + 2], stats[:, 0:S]),
                              inc=sv)
                    v_mv = sv.n
                    self.emit("scalar", lambda e: e.activation(stats[:, S + 2:S + 3],
                                                               stats[:, S + 1:S + 2], AF.Ln,
                                                               bias=self.epsv[:, 0:1]),
                              waits=[(sv, v_mv)])
                    self.emit("scalar", lambda e: e.activation(stats[:, S + 3:S + 4],
                                                               stats[:, S + 2:S + 3], AF.Exp,
                                                               scale=-0.5),
                              inc=sa)
                    a_rstd = sa.n
                    w = [(sa, a_rstd)]
                    if h1n_free:
                        w.append(h1n_free)
                    self.emit("vector",
                              lambda e, hb=hb: e.tensor_scalar(h1n[:], hb, stats[:, S:S + 1],
                                                               stats[:, S + 3:S + 4],
                                                               ALU.subtract, ALU.mult),
                              waits=w, inc=sv)
                    v_h1n = sv.n
                    h1n_free = self.dma("sync", spill[r0:r0 + 128, :], h1n[:],
                                        spsem[0], waits=[(sv, v_h1n)])
                    rms_src = h1n.ap()[:, :]
                    v_src = v_h1n
                else:
                    hb_free[tt] = self.dma("sync", spill[r0:r0 + 128, :], hb,
                                           spsem[tt % 2], waits=[(sv, v_h1)])
                    rms_src = hb
                    v_src = v_h1

                for q in range(nq):
                    self.emit("vector",
                              lambda e, q=q, rs=rms_src: e.bn_stats(stats[:, q * 6:(q + 1) * 6],
                                                                    rs[:, q * 512:(q + 1) * 512]),
                              waits=[(sv, v_src)] if q == 0 else ())
                self.emit("vector", lambda e: e.bn_aggr(stats[:, S:S + 2], stats[:, 0:S]))
                self.emit("vector", lambda e: e.tensor_tensor(stats[:, S + 4:S + 5],
                                                              stats[:, S:S + 1],
                                                              stats[:, S:S + 1], ALU.mult))
                self.emit("vector", lambda e: e.tensor_tensor(stats[:, S + 5:S + 6],
                                                              stats[:, S + 4:S + 5],
                                                              stats[:, S + 1:S + 2], ALU.add),
                          inc=sv)
                v_msq = sv.n
                self.emit("scalar", lambda e: e.activation(stats[:, S + 6:S + 7],
                                                           stats[:, S + 5:S + 6], AF.Ln,
                                                           bias=self.epsv[:, 1:2]),
                          waits=[(sv, v_msq)])
                self.emit("scalar", lambda e: e.activation(stats[:, S + 7:S + 8],
                                                           stats[:, S + 6:S + 7], AF.Exp,
                                                           scale=-0.5),
                          inc=sa)
                a_rms = sa.n
                w = [(sa, a_rms)]
                if xlnt_free:
                    w.append((sv, xlnt_free))
                self.emit("vector",
                          lambda e, rs=rms_src: e.scalar_tensor_tensor(xlnt[:], rs,
                                                                       stats[:, S + 7:S + 8],
                                                                       lnwbc[:], ALU.mult, ALU.mult),
                          waits=w, inc=sv)
                v_xln = sv.n
                for g0 in range(0, EC, 8):
                    gn = min(8, EC - g0)
                    for j in range(gn):
                        cc = g0 + j
                        w = []
                        if j == 0:
                            w.append((sv, v_xln))
                            if ptrb_free:
                                w.append(ptrb_free)
                        self.tr(ptrb[:, j * 128:(j + 1) * 128], xlnt[:, cc * 128:(cc + 1) * 128],
                                identb_s[:], waits=w, inc=sp if j == gn - 1 else None)
                    p_tr = sp.n
                    for j in range(gn):
                        cc = g0 + j
                        self.emit("vector",
                                  lambda e, cc=cc, j=j, r0=r0:
                                  e.tensor_copy(xT3[:, cc, r0:r0 + 128],
                                                ptrb[:, j * 128:(j + 1) * 128]),
                                  waits=[(sp, p_tr)] if j == 0 else (),
                                  inc=sv if j == gn - 1 else None)
                    ptrb_free = (sv, sv.n)
                xlnt_free = sv.n
                xl_free[tt % 2] = sv.n

        self.barrier()
        for cm in reversed(cms + pcms):
            cm.__exit__(None, None, None)

    def gate_phase(self, xT3, gw_bf, wfull_d, identf_s):
        c = self.cfg
        T, E, EC, TT, NE, TOPK = c["T"], c["E"], c["EC"], c["TT"], c["NE"], c["TOPK"]
        sp, sv, sa = (self.S[k] for k in ("sp", "sv", "sa"))
        cms = []
        def sba(name, shape, dt):
            h, cm = self.alloc_sbuf(name, shape, dt)
            cms.append(cm)
            return h
        gwT = sba("gwT", [128, EC * NE], BF16)
        logT = sba("logT", [NE, T], F32)
        probs = sba("probs", [128, NE], F32)
        work = sba("work", [128, NE], F32)
        eqm = sba("eqm", [128, NE], F32)
        gst = sba("gstat", [128, 8], F32)
        wfT = sba("wfT", [NE, T], F32)
        psT, psT_cm = self.alloc_psum("gpsT", [128, 512], F32)
        pcms = [psT_cm]

        s_gw = self.sem("g_gw")
        w_gw = self.dma("sync", gwT[:], gw_bf, s_gw, waits=[self.cast_thr["gw"]])
        gwT3 = gwT.ap().rearrange("p (c r) -> p c r", c=EC)

        psT_free = None
        for nb in range(T // 512):
            for kc in range(EC):
                w = []
                if kc == 0:
                    w.append(w_gw)
                    if psT_free:
                        w.append(psT_free)
                self.mm(psT[0:NE, 0:512], gwT3[:, kc, :],
                        xT3[:, kc, nb * 512:(nb + 1) * 512],
                        start=(kc == 0), stop=(kc == EC - 1),
                        waits=w, inc=sp if kc == EC - 1 else None)
            p_l = sp.n
            self.emit("vector",
                      lambda e, nb=nb: e.tensor_copy(logT[:, nb * 512:(nb + 1) * 512],
                                                     psT[0:NE, 0:512]),
                      waits=[(sp, p_l)], inc=sv)
            psT_free = (sv, sv.n)
        v_log = sv.n

        eqm_free = 0
        for ti in range(TT):
            w = [(sv, v_log)] if ti == 0 else []
            if psT_free:
                w.append(psT_free)
            self.tr(psT[:, 0:NE], logT[:, ti * 128:(ti + 1) * 128],
                    identf_s[0:NE, 0:NE], waits=w, inc=sp)
            p_t = sp.n
            self.emit("vector", lambda e: e.tensor_copy(work[:], psT[:, 0:NE]),
                      waits=[(sp, p_t)], inc=sv)
            psT_free = (sv, sv.n)
            self.emit("vector", lambda e: e.tensor_reduce(gst[:, 0:1], work[:], AX.X,
                                                          ALU.max, negate=True), inc=sv)
            v_m = sv.n
            self.emit("scalar", lambda e: e.activation(probs[:], work[:], AF.Exp,
                                                       bias=gst[:, 0:1],
                                                       accum_out=gst[:, 1:2]),
                      waits=[(sv, v_m)], inc=sa)
            a_e = sa.n
            self.emit("vector", lambda e: e.reciprocal(gst[:, 2:3], gst[:, 1:2]),
                      waits=[(sa, a_e)])
            self.emit("vector", lambda e: e.tensor_scalar(probs[:], probs[:], gst[:, 2:3],
                                                          None, ALU.mult))
            self.emit("vector", lambda e: e.tensor_copy(work[:], probs[:]))
            self.emit("vector", lambda e: e.memset(gst[:, 3:4], 0.0))
            first_eq = True
            for it in range(TOPK):
                self.emit("vector", lambda e: e.tensor_reduce(gst[:, 4:5], work[:], AX.X, ALU.max))
                self.emit("vector", lambda e: e.tensor_tensor(gst[:, 3:4], gst[:, 3:4],
                                                              gst[:, 4:5], ALU.add))
                if it < TOPK - 1:
                    w = [(sp, eqm_free)] if (first_eq and eqm_free) else []
                    first_eq = False
                    self.emit("vector", lambda e: e.tensor_scalar(eqm[:], work[:], gst[:, 4:5],
                                                                  None, ALU.is_equal), waits=w)
                    self.emit("vector", lambda e: e.scalar_tensor_tensor(work[:], eqm[:], -2.0,
                                                                         work[:], ALU.mult, ALU.add))
            w = [(sp, eqm_free)] if (first_eq and eqm_free) else []
            self.emit("vector", lambda e: e.tensor_scalar(eqm[:], probs[:], gst[:, 4:5],
                                                          None, ALU.is_ge), waits=w)
            self.emit("vector", lambda e: e.reciprocal(gst[:, 5:6], gst[:, 3:4]))
            self.emit("vector", lambda e: e.tensor_tensor(eqm[:], eqm[:], probs[:], ALU.mult))
            self.emit("vector", lambda e: e.tensor_scalar(eqm[:], eqm[:], gst[:, 5:6],
                                                          None, ALU.mult), inc=sv)
            v_wf = sv.n
            w = [(sv, v_wf)]
            if psT_free:
                w.append(psT_free)
            self.tr(psT[0:NE, 0:128], eqm[:], identf_s[:], waits=w, inc=sp)
            p_t2 = sp.n
            eqm_free = p_t2
            self.emit("vector",
                      lambda e, ti=ti: e.tensor_copy(wfT[:, ti * 128:(ti + 1) * 128],
                                                     psT[0:NE, 0:128]),
                      waits=[(sp, p_t2)], inc=sv)
            psT_free = (sv, sv.n)
        v_wfT = sv.n
        s_wf = self.sem("g_wf")
        self.wfull_done = self.dma("sync", wfull_d, wfT[:], s_wf, waits=[(sv, v_wfT)])

        self.barrier()
        for cm in reversed(cms + pcms):
            cm.__exit__(None, None, None)

    def moe_phase(self, xT3, w1_bf, w2_bf, shgu_bf, shdn_bf, wfull_d, res1, res2,
                  out, onesf, identf_s):
        c = self.cfg
        T, E, EC = c["T"], c["E"], c["EC"]
        NE, I1, IC, SHI = c["NE"], c["I1"], c["IC"], c["SHI"]
        TU, NU = c["TU"], c["NU"]
        sp, sv, sa = (self.S[k] for k in ("sp", "sv", "sa"))
        NB = TU // 512

        specs = []
        for e in range(NE):
            specs.append(dict(kind="reg", e=e, gt0=0, ut0=IC, wsrc=w1_bf[e],
                              w2src=w2_bf[e], ck1=f"w1_{e}", ck2=f"w2_{e}"))
        specs.append(dict(kind="sh", e=NE, gt0=0, ut0=2 * IC, wsrc=shgu_bf,
                          w2src=shdn_bf[0], ck1="shgu", ck2="shdn"))
        specs.append(dict(kind="sh", e=NE + 1, gt0=IC, ut0=3 * IC, wsrc=shgu_bf,
                          w2src=shdn_bf[1], ck1="shgu", ck2="shdn"))

        cms = []
        def sba(name, shape, dt):
            h, cm = self.alloc_sbuf(name, shape, dt)
            cms.append(cm)
            return h
        routed = sba("routed", [128, EC * TU], F32)
        psG, psG_cm = self.alloc_psum("psG", [128, 1024], F32)
        psU, psU_cm = self.alloc_psum("psU", [128, 1024], F32)
        psD, psD_cm = self.alloc_psum("psD", [128, 1024], F32)
        psT, psT_cm = self.alloc_psum("mpsT", [128, 512], F32)
        pcms = [psG_cm, psU_cm, psD_cm, psT_cm]

        routed3 = routed.ap().rearrange("p (c t) -> p c t", c=EC)

        sW1g = [self.sem(f"mw1g{j}") for j in range(2)]
        sW1u = [self.sem(f"mw1u{j}") for j in range(2)]
        sW2 = [self.sem(f"mw2{j}") for j in range(2)]
        sWfr = self.sem("mwfr")

        w1g_done = [None, None]; w1g_free = [0, 0]
        w1u_done = [None, None]; w1u_free = [0, 0]
        w2_done = [None, None]; w2_free = [0, 0]
        gst_free = [0, 0]; ust_free = [0, 0]
        wbc_free = [0, 0]
        wfrow_free = 0
        psG_free = None; psU_free = None; psD_free = None
        psT_free = None
        act_free = 0

        for u in range(NU):
            t0 = u * TU
            mcms = []
            def mba(name, shape, dt, u=u, mcms=mcms):
                h, cm = self.alloc_sbuf(f"u{u}" + name, shape, dt)
                mcms.append(cm)
                return h
            act = mba("act", [128, IC * TU], BF16)
            w1g = [mba(f"w1g{j}", [128, EC * 128], BF16) for j in range(2)]
            w1u = [mba(f"w1u{j}", [128, EC * 128], BF16) for j in range(2)]
            w2b = [mba(f"w2b{j}", [128, IC * 128], BF16) for j in range(2)]
            gstp = [mba(f"gstp{j}", [128, TU], BF16) for j in range(2)]
            ustp = [mba(f"ustp{j}", [128, TU], BF16) for j in range(2)]
            wbc = [mba(f"wbc{j}", [128, TU], F32) for j in range(2)]
            wfrow = mba("wfrow", [1, TU], F32)
            act3 = act.ap().rearrange("p (c t) -> p c t", c=IC)

            for si, spec in enumerate(specs):
                eslot = spec["e"] % 2
                v_wbc = None
                if spec["kind"] == "reg":
                    w = [self.wfull_done]
                    if wfrow_free:
                        w.append((sp, wfrow_free))
                    g_wfr = self.dma("sync", wfrow[:],
                                     wfull_d[spec["e"]:spec["e"] + 1, t0:t0 + TU],
                                     sWfr, waits=w)
                    for nb in range(NB):
                        w = [g_wfr]
                        if nb == 0 and wbc_free[eslot]:
                            w.append((sv, wbc_free[eslot]))
                        if psT_free:
                            w.append(psT_free)
                        self.mm(psT[:, 0:512], onesf[:], wfrow[:, nb * 512:(nb + 1) * 512],
                                True, True, waits=w, inc=sp)
                        p_b = sp.n
                        self.emit("vector",
                                  lambda e_, nb=nb, sl=eslot:
                                  e_.tensor_copy(wbc[sl][:, nb * 512:(nb + 1) * 512],
                                                 psT[:, 0:512]),
                                  waits=[(sp, p_b)], inc=sv)
                        psT_free = (sv, sv.n)
                    wfrow_free = sp.n
                    v_wbc = sv.n

                for cc in range(IC):
                    slotc = cc % 2
                    w = [self.cast_thr[spec["ck1"]]]
                    if w1g_free[slotc]:
                        w.append((sp, w1g_free[slotc]))
                    w1g_done[slotc] = self.dma(
                        "sync", w1g[slotc][:], spec["wsrc"][spec["gt0"] + cc],
                        sW1g[slotc], waits=w)
                    w = [self.cast_thr[spec["ck1"]]]
                    if w1u_free[slotc]:
                        w.append((sp, w1u_free[slotc]))
                    w1u_done[slotc] = self.dma(
                        "sync", w1u[slotc][:], spec["wsrc"][spec["ut0"] + cc],
                        sW1u[slotc], waits=w)

                    wg3 = w1g[slotc].ap().rearrange("p (c r) -> p c r", c=EC)
                    wu3 = w1u[slotc].ap().rearrange("p (c r) -> p c r", c=EC)
                    for kc in range(EC):
                        for ps, w3, done_, free_ in ((psG, wg3, w1g_done, psG_free),
                                                     (psU, wu3, w1u_done, psU_free)):
                            for nb in range(NB):
                                w = []
                                if kc == 0 and nb == 0:
                                    w.append(done_[slotc])
                                    if free_:
                                        w.append(free_)
                                    if ps is psG and act_free:
                                        w.append((sp, act_free))
                                last = (kc == EC - 1 and nb == NB - 1 and ps is psU)
                                self.mm(ps[:, nb * 512:(nb + 1) * 512], w3[:, kc, :],
                                        xT3[:, kc, t0 + nb * 512:t0 + (nb + 1) * 512],
                                        start=(kc == 0), stop=(kc == EC - 1),
                                        waits=w, inc=sp if last else None)
                    p_um = sp.n
                    p_gm = p_um
                    w1g_free[slotc] = p_um
                    w1u_free[slotc] = p_um

                    w = [(sp, p_gm)]
                    if gst_free[slotc]:
                        w.append((sv, gst_free[slotc]))
                    self.emit("scalar",
                              lambda e_, sl=slotc: e_.activation(gstp[sl][:], psG[:, 0:c['TU']],
                                                                 AF.Silu),
                              waits=w, inc=sa)
                    a_g = sa.n
                    psG_free = (sa, a_g)
                    w = [(sp, p_um)]
                    if v_wbc is not None:
                        w.append((sv, v_wbc))
                    if ust_free[slotc]:
                        w.append((sv, ust_free[slotc]))
                    if spec["kind"] == "reg":
                        self.emit("vector",
                                  lambda e_, sl=eslot, s2=slotc:
                                  e_.tensor_tensor(ustp[s2][:], psU[:, 0:c['TU']], wbc[sl][:],
                                                   ALU.mult),
                                  waits=w, inc=sv)
                    else:
                        self.emit("vector",
                                  lambda e_, s2=slotc: e_.tensor_copy(ustp[s2][:], psU[:, 0:c['TU']]),
                                  waits=w, inc=sv)
                    v_u = sv.n
                    psU_free = (sv, v_u)
                    w = [(sa, a_g), (sv, v_u)]
                    self.emit("vector",
                              lambda e_, cc=cc, s2=slotc:
                              e_.tensor_tensor(act3[:, cc, :], gstp[s2][:], ustp[s2][:], ALU.mult),
                              waits=w, inc=sv)
                    gst_free[slotc] = sv.n
                    ust_free[slotc] = sv.n
                act_done = sv.n
                if spec["kind"] == "reg":
                    wbc_free[eslot] = act_done

                for m in range(EC):
                    slotm = m % 2
                    w = [self.cast_thr[spec["ck2"]]]
                    if w2_free[slotm]:
                        w.append((sp, w2_free[slotm]))
                    w2_done[slotm] = self.dma(
                        "sync", w2b[slotm][:], spec["w2src"][m],
                        sW2[slotm], waits=w)
                    w23 = w2b[slotm].ap().rearrange("p (c r) -> p c r", c=IC)
                    for kc in range(IC):
                        for nb in range(NB):
                            w = []
                            if kc == 0 and nb == 0:
                                w.append(w2_done[slotm])
                                w.append((sv, act_done))
                                if psD_free:
                                    w.append(psD_free)
                            last = (kc == IC - 1 and nb == NB - 1)
                            self.mm(psD[:, nb * 512:(nb + 1) * 512], w23[:, kc, :],
                                    act3[:, kc, nb * 512:(nb + 1) * 512],
                                    start=(kc == 0), stop=(kc == IC - 1),
                                    waits=w, inc=sp if last else None)
                    p_d = sp.n
                    w2_free[slotm] = p_d
                    if m == EC - 1:
                        act_free = p_d
                    if si == 0:
                        self.emit("vector",
                                  lambda e_, m=m: e_.tensor_copy(routed3[:, m, :], psD[:, 0:c['TU']]),
                                  waits=[(sp, p_d)], inc=sv)
                    else:
                        self.emit("vector",
                                  lambda e_, m=m: e_.tensor_tensor(routed3[:, m, :],
                                                                   routed3[:, m, :],
                                                                   psD[:, 0:c['TU']],
                                                                   ALU.add),
                                  waits=[(sp, p_d)], inc=sv)
                    psD_free = (sv, sv.n)
            v_routed = sv.n

            self.barrier()
            for cm in reversed(mcms):
                cm.__exit__(None, None, None)
            psT_free = self.final_epilogue(u, routed3, res1, res2, out, identf_s,
                                           psT, psT_free, v_routed)

        self.barrier()
        for cm in reversed(cms + pcms):
            cm.__exit__(None, None, None)

    def final_epilogue(self, u, routed3, res1, res2, out, identf_s, psT,
                       psT_free, v_routed):
        c = self.cfg
        T, E, EC, TU = c["T"], c["E"], c["EC"], c["TU"]
        sp, sv, sa = (self.S[k] for k in ("sp", "sv", "sa"))
        cms = []
        def sba(name, shape, dt):
            h, cm = self.alloc_sbuf(f"f{u}" + name, shape, dt)
            cms.append(cm)
            return h
        r1l = [sba(f"r1l{j}", [128, E], F32) for j in range(2)]
        r2l = [sba(f"r2l{j}", [128, E], F32) for j in range(2)]
        ftile = sba("ftile", [128, E], F32)
        ot = [sba(f"ot{j}", [128, E], F32) for j in range(2)]
        nq = E // 512
        fstats = sba("fstats", [128, 6 * nq + 4], F32)
        S = 6 * nq

        sR1 = [self.sem(f"fr1{j}") for j in range(2)]
        sR2 = [self.sem(f"fr2{j}") for j in range(2)]
        sOut = [self.sem(f"out{j}") for j in range(2)]

        NT = TU // 128
        r1_done = [None, None]; r2_done = [None, None]
        r1_free = [0, 0]; r2_free = [0, 0]
        ot_free = [None, None]
        ftile_free = 0

        for tt in range(NT):
            slot = tt % 2
            r0 = u * TU + tt * 128
            w = [(sv, r1_free[slot])] if r1_free[slot] else []
            r1_done[slot] = self.dma("sync", r1l[slot][:], res1[r0:r0 + 128, :],
                                     sR1[slot], waits=w)
            w = [(sv, r2_free[slot])] if r2_free[slot] else []
            r2_done[slot] = self.dma("sync", r2l[slot][:], res2[r0:r0 + 128, :],
                                     sR2[slot], waits=w)

            for q in range(nq):
                w = []
                if q == 0:
                    w.append(r1_done[slot])
                    w.append(r2_done[slot])
                    if ftile_free:
                        w.append((sv, ftile_free))
                self.emit("vector",
                          lambda e, q=q, sl=slot:
                          e.tensor_tensor(ftile[:, q * 512:(q + 1) * 512],
                                          r1l[sl][:, q * 512:(q + 1) * 512],
                                          r2l[sl][:, q * 512:(q + 1) * 512], ALU.add),
                          waits=w, inc=sv if q == nq - 1 else None)
            r1_free[slot] = sv.n
            r2_free[slot] = sv.n
            v_f = sv.n

            for g0 in range(0, EC, 4):
                gn = min(4, EC - g0)
                for j in range(gn):
                    m = g0 + j
                    w = []
                    if j == 0:
                        w.append((sv, v_routed))
                        if psT_free:
                            w.append(psT_free)
                    self.tr(psT[:, j * 128:(j + 1) * 128],
                            routed3[:, m, tt * 128:(tt + 1) * 128],
                            identf_s[:], waits=w, inc=sp if j == gn - 1 else None)
                p_t = sp.n
                self.emit("vector",
                          lambda e, g0=g0, gn=gn:
                          e.tensor_tensor(ftile[:, g0 * 128:(g0 + gn) * 128],
                                          ftile[:, g0 * 128:(g0 + gn) * 128],
                                          psT[:, 0:gn * 128], ALU.add),
                          waits=[(sp, p_t), (sv, v_f)], inc=sv)
                psT_free = (sv, sv.n)
            v_sum = sv.n

            for q in range(nq):
                self.emit("vector",
                          lambda e, q=q: e.bn_stats(fstats[:, q * 6:(q + 1) * 6],
                                                    ftile[:, q * 512:(q + 1) * 512]),
                          waits=[(sv, v_sum)] if q == 0 else ())
            self.emit("vector", lambda e: e.bn_aggr(fstats[:, S:S + 2], fstats[:, 0:S]),
                      inc=sv)
            v_mv = sv.n
            self.emit("scalar", lambda e: e.activation(fstats[:, S + 2:S + 3],
                                                       fstats[:, S + 1:S + 2], AF.Ln,
                                                       bias=self.epsv[:, 0:1]),
                      waits=[(sv, v_mv)])
            self.emit("scalar", lambda e: e.activation(fstats[:, S + 3:S + 4],
                                                       fstats[:, S + 2:S + 3], AF.Exp,
                                                       scale=-0.5),
                      inc=sa)
            a_r = sa.n
            w = [(sa, a_r)]
            if ot_free[slot]:
                w.append(ot_free[slot])
            self.emit("vector",
                      lambda e, sl=slot: e.tensor_scalar(ot[sl][:], ftile[:],
                                                         fstats[:, S:S + 1],
                                                         fstats[:, S + 3:S + 4],
                                                         ALU.subtract, ALU.mult),
                      waits=w, inc=sv)
            v_ot = sv.n
            ftile_free = sv.n
            ot_free[slot] = self.dma("sync", out[r0:r0 + 128, :], ot[slot][:],
                                     sOut[slot], waits=[(sv, v_ot)])

        w0 = (sOut[0].h, sOut[0].n)
        w1 = (sOut[1].h, sOut[1].n)
        def fwait(e, w0=w0, w1=w1):
            if w0[1]:
                e.wait_ge(w0[0], w0[1])
            if w1[1]:
                e.wait_ge(w1[0], w1[1])
        self.acts["gpsimd"].append(fwait)

        self.barrier()
        for cm in reversed(cms):
            cm.__exit__(None, None, None)
        return psT_free


def make_consts(cfg):
    return dict(identb=np.eye(128, dtype=ml_dtypes.bfloat16),
                identf=np.eye(128, dtype=np.float32),
                cmask=np.tril(np.full((128, 128), NEG_BIG, np.float32), k=-1))


def _tile_rows(w):
    R, C = w.shape
    return np.ascontiguousarray(
        w.reshape(R // 128, 128, C // 128, 128).transpose(0, 3, 2, 1).reshape(R // 128, 128, C))


def _tile_x(x):
    T, E = x.shape
    return np.ascontiguousarray(x.reshape(T, E // 128, 128).transpose(2, 1, 0).reshape(128, -1))


def prep_weights(inputs):
    c = REAL_CFG
    NH, QHD, NOPE, VD, NE, I1 = c["NH"], c["QHD"], c["NOPE"], c["VD"], c["NE"], c["I1"]
    qw = np.asarray(inputs["L0_ffn_q_w"])
    kvw = np.asarray(inputs["L0_ffn_kv_w"])
    qn = np.ascontiguousarray(qw.reshape(NH, QHD, -1)[:, :NOPE].reshape(NH * NOPE, -1))
    kw = np.ascontiguousarray(kvw[:NH * NOPE])
    vw = np.ascontiguousarray(kvw[NH * NOPE:NH * (NOPE + VD)])
    w1 = np.asarray(inputs["L0_ffn_w1"])
    w2 = np.asarray(inputs["L0_ffn_w2"])
    shdn = np.asarray(inputs["L0_ffn_sh_down_w"])
    gw = np.asarray(inputs["L0_ffn_gate_w"])
    E = 2048
    d = dict(
        wq_t=_tile_rows(np.asarray(inputs["L0_attn_Wq"])),
        wk_t=_tile_rows(np.asarray(inputs["L0_attn_Wk"])),
        wv_t=_tile_rows(np.asarray(inputs["L0_attn_Wv"])),
        wo_t=_tile_rows(np.asarray(inputs["L0_attn_Wo"])),
        qn_t=_tile_rows(qn), kw_t=_tile_rows(kw), vw_t=_tile_rows(vw),
        ow_t=_tile_rows(np.asarray(inputs["L0_ffn_o_w"])),
        gw_t=np.ascontiguousarray(gw.reshape(NE, E // 128, 128).transpose(2, 1, 0).reshape(128, -1)),
        w1_t=np.stack([_tile_rows(w1[e]) for e in range(NE)]),
        w2_t=np.stack([_tile_rows(w2[e]) for e in range(NE)]),
        shgu_t=_tile_rows(np.asarray(inputs["L0_ffn_sh_gu_w"])),
        shdn_t=np.stack([_tile_rows(np.ascontiguousarray(shdn[:, h * I1:(h + 1) * I1]))
                         for h in range(2)]),
    )
    return d


_BUILD_CACHE = {}


def build_nc(cfg=None, key="real"):
    if key in _BUILD_CACHE:
        return _BUILD_CACHE[key]
    nc = KB(cfg or REAL_CFG).build()
    _BUILD_CACHE[key] = nc
    return nc


def kernel(**inputs):
    _ = inputs.get("L0_ffn_positions")
    nc = build_nc()
    consts = make_consts(REAL_CFG)
    base = dict(
        lq1=inputs["L0_attn_lambda_q1"], lk1=inputs["L0_attn_lambda_k1"],
        lq2=inputs["L0_attn_lambda_q2"], lk2=inputs["L0_attn_lambda_k2"],
        subw=inputs["L0_attn_subln_weight"],
        ln1w=inputs["L0_ffn_ln1_w"], ln2w=inputs["L0_ffn_ln2_w"],
        **prep_weights(inputs), **consts)
    x = inputs["L0_attn_query"]
    in_maps = [dict(base, xq=np.ascontiguousarray(x[b]), xt=_tile_x(x[b]))
               for b in range(x.shape[0])]
    res = run_bass_kernel_spmd(nc, in_maps, core_ids=list(range(len(in_maps))))
    return np.stack([res.results[b]["out"] for b in range(len(in_maps))], axis=0)


# version 16
# speedup vs baseline: 1.0077x; 1.0077x over previous
import math
import numpy as np
import ml_dtypes
import concourse.bass as bass
import concourse.mybir as mybir
from concourse.bass_utils import run_bass_kernel_spmd

F32 = mybir.dt.float32
BF16 = mybir.dt.bfloat16
AF = mybir.ActivationFunctionType
ALU = mybir.AluOpType
AX = mybir.AxisListType

ENGINES = ("tensor", "vector", "scalar", "gpsimd", "sync")

REAL_CFG = dict(T=2048, E=2048, NH=16, HHD=64, VD=128, QHD=192, NOPE=128,
                NE=16, I1=1408, TOPK=4, TU=1024, TAU=256)

NEG_BIG = -1.0e30


class Sem:
    def __init__(self, h):
        self.h = h
        self.n = 0


class KB:
    def __init__(self, cfg):
        self.cfg = dict(cfg)
        c = self.cfg
        c["EC"] = c["E"] // 128
        c["TT"] = c["T"] // 128
        c["IC"] = c["I1"] // 128
        c["SHI"] = 2 * c["I1"]
        c["NU"] = c["T"] // c["TU"]
        c["NTAU"] = c["T"] // c["TAU"]
        c["LI"] = 0.8 - 0.6 * math.exp(-0.3 * 12)
        assert c["E"] == c["NH"] * 2 * c["HHD"] == c["NH"] * c["VD"]
        assert c["T"] % 512 == 0 and c["TU"] % 512 == 0 and c["TAU"] % 128 == 0
        self.acts = {e: [] for e in ENGINES}
        self._exits = []
        self.S = {}
        self.cast_thr = {}
        self.psA_evac = None
        self.ptr_free = None
        self.ptr2_free = None
        self.psO_free = 0
        self.v_ones = 0
        self.wfull_done = None

    def sem(self, name):
        if name not in self.S:
            cm = self.nc.semaphore(name)
            h = cm.__enter__()
            self._exits.append(cm)
            self.S[name] = Sem(h)
        return self.S[name]

    def emit(self, eng, fn, waits=(), inc=None, k=1):
        waits = list(waits)
        if eng == "vector":
            s = self.S["sv"]
            waits.append((s, s.n))
            assert inc is None or inc is s
            inc, k = s, 1
        elif eng == "scalar":
            s = self.S["sa"]
            waits.append((s, s.n))
            assert inc is None or inc is s
            inc, k = s, 1
        wl = [(s.h, v) for (s, v) in waits if v]
        tgt = None
        if inc is not None:
            inc.n += k
            tgt = inc.n

        def run(e, wl=wl, fn=fn, inc=inc, k=k):
            for h, v in wl:
                e.wait_ge(h, v)
            r = fn(e)
            if inc is not None:
                r.then_inc(inc.h, k)
        self.acts[eng].append(run)
        return tgt

    def mm(self, out, lhsT, rhs, start, stop, waits=(), inc=None):
        return self.emit(
            "tensor",
            lambda e, o=out, l=lhsT, r=rhs, s=start, p=stop:
                e.matmul(o, l, r, start=s, stop=p),
            waits, inc)

    def tr(self, out, in_, ident, waits=(), inc=None):
        return self.emit(
            "tensor",
            lambda e, o=out, i=in_, d=ident: e.transpose(o, i, d),
            waits, inc)

    def dma(self, eng, out, in_, sem, waits=(), transpose=False, **kw):
        def fn(e, o=out, i=in_, t=transpose, kw=dict(kw)):
            return e.dma_start(o, i, transpose=t, **kw)
        self.emit(eng, fn, waits, inc=sem, k=16)
        return (sem, sem.n)

    def barrier(self, exclude=()):
        waits = tuple((s.h, s.n) for nm, s in self.S.items()
                      if s.n > 0 and nm not in exclude)
        for eng in ENGINES:
            def fn(e, wl=waits):
                for h, v in wl:
                    e.wait_ge(h, v)
            self.acts[eng].append(fn)

    def alloc_sbuf(self, name, shape, dt):
        cm = self.nc.sbuf_tensor(name, shape, dt)
        return cm.__enter__(), cm

    def alloc_psum(self, name, shape, dt):
        cm = self.nc.psum_tensor(name, shape, dt)
        return cm.__enter__(), cm

    def build(self):
        c = self.cfg
        T, E, EC = c["T"], c["E"], c["EC"]
        NH, HHD, VD, QHD, NOPE = c["NH"], c["HHD"], c["VD"], c["QHD"], c["NOPE"]
        NE, I1, SHI = c["NE"], c["I1"], c["SHI"]

        IC = c["IC"]
        nc = bass.Bass("TRN2", target_bir_lowering=False)
        self.nc = nc
        dram = lambda n, sh, dt=F32, kind="ExternalInput": nc.dram_tensor(n, sh, dt, kind=kind).ap()

        xq = dram("xq", [T, E])
        xt = dram("xt", [128, EC * T])
        wq_t = dram("wq_t", [NH, 128, E]); wk_t = dram("wk_t", [NH, 128, E])
        wv_t = dram("wv_t", [NH, 128, E]); wo_t = dram("wo_t", [EC, 128, E])
        lq1 = dram("lq1", [HHD]); lk1 = dram("lk1", [HHD])
        lq2 = dram("lq2", [HHD]); lk2 = dram("lk2", [HHD])
        subw = dram("subw", [2 * HHD])
        ln1w = dram("ln1w", [E]); ln2w = dram("ln2w", [E])
        qn_t = dram("qn_t", [NH, 128, E]); kw_t = dram("kw_t", [NH, 128, E])
        vw_t = dram("vw_t", [NH, 128, E]); ow_t = dram("ow_t", [EC, 128, E])
        gw_t = dram("gw_t", [128, EC * NE])
        w1_t = dram("w1_t", [NE, 2 * IC, 128, E])
        w2_t = dram("w2_t", [NE, EC, 128, I1])
        shgu_t = dram("shgu_t", [4 * IC, 128, E])
        shdn_t = dram("shdn_t", [2, EC, 128, I1])
        identb = dram("identb", [128, 128], BF16)
        identf = dram("identf", [128, 128], F32)
        cmask = dram("cmask", [128, 128], F32)
        out = dram("out", [T, E], F32, kind="ExternalOutput")

        idram = lambda n, sh, dt: nc.dram_tensor(n, sh, dt, kind="Internal").ap()
        wq_bf = idram("wq_bf", [NH, 128, E], BF16)
        wk_bf = idram("wk_bf", [NH, 128, E], BF16)
        wv_bf = idram("wv_bf", [NH, 128, E], BF16)
        wo_bf = idram("wo_bf", [EC, 128, E], BF16)
        qn_bf = idram("qn_bf", [NH, 128, E], BF16)
        kw_bf = idram("kw_bf", [NH, 128, E], BF16)
        vw_bf = idram("vw_bf", [NH, 128, E], BF16)
        ow_bf = idram("ow_bf", [EC, 128, E], BF16)
        gw_bf = idram("gw_bf", [128, EC * NE], BF16)
        w1_bf = idram("w1_bf", [NE, 2 * IC, 128, E], BF16)
        w2_bf = idram("w2_bf", [NE, EC, 128, I1], BF16)
        shgu_bf = idram("shgu_bf", [4 * IC, 128, E], BF16)
        shdn_bf = idram("shdn_bf", [2, EC, 128, I1], BF16)
        res1 = idram("res1", [T, E], F32)
        res2 = idram("res2", [T, E], F32)
        wfull_d = idram("wfull_d", [NE, T], F32)

        sp = self.sem("sp"); sv = self.sem("sv"); sa = self.sem("sa")

        sb = self.alloc_sbuf
        xT, _ = sb("xT", [128, EC * T], BF16)
        identb_s, _ = sb("identb_s", [128, 128], BF16)
        identf_s, _ = sb("identf_s", [128, 128], F32)
        cmask_s, _ = sb("cmask_s", [128, 128], F32)
        onesb, _ = sb("onesb", [1, 128], BF16)
        onesf, _ = sb("onesf", [1, 128], F32)
        lam_bc, _ = sb("lam_bc", [128, 1], F32)
        g_bc, _ = sb("g_bc", [128, 2 * HHD], F32)
        ln1w_bf, _ = sb("ln1w_bf", [1, E], BF16)
        ln2w_bf, _ = sb("ln2w_bf", [1, E], BF16)
        lnv, _ = sb("lnv", [1, 6 * HHD], F32)
        epsv, _ = sb("epsv", [128, 4], F32)
        self.epsv = epsv
        lprod, _ = sb("lprod", [1, 2 * HHD], F32)
        sml, _ = sb("sml", [1, 16], F32)

        xT3 = xT.ap().rearrange("p (c t) -> p c t", c=EC)

        def flat8k(ap):
            sh = list(ap.shape)
            v = ap
            while len(v.shape) > 2:
                n = len(v.shape)
                names = " ".join(chr(97 + i) for i in range(n))
                tail = " ".join(chr(97 + i) for i in range(n - 2, n))
                head = " ".join(chr(97 + i) for i in range(n - 2))
                v = v.rearrange(f"{names} -> {head} ({tail})".strip())
            a, b = v.shape
            assert (a * b) % 8192 == 0, sh
            if b % 8192 == 0 and b > 8192:
                return v.rearrange("a (r q) -> (a r) q", q=8192)
            if b == 8192:
                return v
            return v.rearrange("a b -> (a b)").rearrange("(r q) -> r q", q=8192)

        s_xT = self.sem("c_xT")
        self.w_xT = self.dma(
            "gpsimd", xT.ap().rearrange("p (a q) -> p a q", q=8192),
            xt.rearrange("p (a q) -> p a q", q=8192), s_xT, max_dma_last_dim=8192)

        groups = dict(
            c_qkv=[("Wq", wq_bf, wq_t), ("Wk", wk_bf, wk_t), ("Wv", wv_bf, wv_t)],
            c_p2=[("Wo", wo_bf, wo_t), ("qw", qn_bf, qn_t), ("kw", kw_bf, kw_t),
                  ("vw", vw_bf, vw_t), ("ow", ow_bf, ow_t), ("gw", gw_bf, gw_t),
                  ("ln1w", ln1w_bf[:], ln1w.rearrange("(a e) -> a e", a=1)),
                  ("ln2w", ln2w_bf[:], ln2w.rearrange("(a e) -> a e", a=1))],
            c_moe1=[(f"w1_{e}", w1_bf[e], w1_t[e]) for e in range(NE // 2)] +
                   [(f"w2_{e}", w2_bf[e], w2_t[e]) for e in range(NE // 2)],
            c_moe2=[(f"w1_{e}", w1_bf[e], w1_t[e]) for e in range(NE // 2, NE)] +
                   [(f"w2_{e}", w2_bf[e], w2_t[e]) for e in range(NE // 2, NE)] +
                   [("shgu", shgu_bf, shgu_t), ("shdn", shdn_bf, shdn_t)],
        )
        for gname, items in groups.items():
            gs = self.sem(gname)
            for name, dst, src in items:
                if name in ("ln1w", "ln2w", "gw"):
                    self.dma("gpsimd", dst, src, gs, max_dma_last_dim=8192)
                else:
                    self.dma("gpsimd", flat8k(dst), flat8k(src), gs,
                             max_dma_last_dim=8192)
            for name, dst, src in items:
                self.cast_thr[name] = (gs, gs.n)

        r1f = lambda ap: ap.rearrange("(a h) -> a h", a=1)
        s_cn = self.sem("c_const")
        self.dma("sync", identb_s[:], identb, s_cn)
        self.dma("sync", identf_s[:], identf, s_cn)
        self.dma("sync", cmask_s[:], cmask, s_cn)
        self.dma("sync", lnv[:, 0:HHD], r1f(lq1), s_cn)
        self.dma("sync", lnv[:, HHD:2 * HHD], r1f(lk1), s_cn)
        self.dma("sync", lnv[:, 2 * HHD:3 * HHD], r1f(lq2), s_cn)
        self.dma("sync", lnv[:, 3 * HHD:4 * HHD], r1f(lk2), s_cn)
        self.dma("sync", lnv[:, 4 * HHD:6 * HHD], r1f(subw), s_cn)
        w_consts = (s_cn, s_cn.n)
        self.w_consts = w_consts

        self.emit("vector", lambda e: e.memset(onesb[:], 1.0))
        self.emit("vector", lambda e: e.memset(epsv[:, 0:1], 1e-5))
        self.emit("vector", lambda e: e.memset(epsv[:, 1:2], 1e-6))
        self.emit("vector", lambda e: e.memset(onesf[:], 1.0), inc=sv)
        self.v_ones = sv.n
        self.emit("vector", lambda e: e.tensor_tensor(lprod[:, 0:HHD], lnv[:, 0:HHD],
                                                      lnv[:, HHD:2 * HHD], ALU.mult),
                  waits=[w_consts])
        self.emit("vector", lambda e: e.tensor_tensor(lprod[:, HHD:2 * HHD], lnv[:, 2 * HHD:3 * HHD],
                                                      lnv[:, 3 * HHD:4 * HHD], ALU.mult))
        self.emit("vector", lambda e: e.tensor_reduce(sml[:, 0:1], lprod[:, 0:HHD], AX.X, ALU.add))
        self.emit("vector", lambda e: e.tensor_reduce(sml[:, 1:2], lprod[:, HHD:2 * HHD], AX.X, ALU.add),
                  inc=sv)
        v_ls = sv.n
        self.emit("scalar", lambda e: e.activation(sml[:, 2:3], sml[:, 0:1], AF.Exp),
                  waits=[(sv, v_ls)])
        self.emit("scalar", lambda e: e.activation(sml[:, 3:4], sml[:, 1:2], AF.Exp), inc=sa)
        a_le = sa.n
        self.emit("vector", lambda e: e.tensor_tensor(sml[:, 4:5], sml[:, 2:3], sml[:, 3:4], ALU.subtract),
                  waits=[(sa, a_le)])
        self.emit("vector", lambda e: e.tensor_scalar(sml[:, 5:6], sml[:, 4:5], c["LI"], None, ALU.add),
                  inc=sv)
        v_lam = sv.n

        psP, psP_cm = self.alloc_psum("psP", [128, 512], F32)
        self.mm(psP[:, 0:1], onesf[:], sml[:, 5:6], True, True,
                waits=[(sv, v_lam)], inc=sp)
        p_lam = sp.n
        self.emit("vector", lambda e: e.tensor_copy(lam_bc[:], psP[:, 0:1]),
                  waits=[(sp, p_lam)], inc=sv)
        v_lbc = sv.n
        self.mm(psP[:, 128:128 + 2 * HHD], onesf[:], lnv[:, 4 * HHD:6 * HHD], True, True,
                waits=[(sv, v_lbc)], inc=sp)
        p_g = sp.n
        self.emit("vector", lambda e: e.tensor_scalar(g_bc[:], psP[:, 128:128 + 2 * HHD],
                                                      1.0 - c["LI"], None, ALU.mult),
                  waits=[(sp, p_g)], inc=sv)
        psP_cm.__exit__(None, None, None)
        self.barrier(exclude=("c_xT", "c_qkv", "c_p2", "c_moe1", "c_moe2"))

        PH = c.get("PH", 6)
        aoT, aoT_cm = sb("aoT", [128, EC * T], BF16)
        aoT3 = aoT.ap().rearrange("p (c t) -> p c t", c=EC)

        if PH >= 1:
            self.attn_phase(True, xT3, aoT3, (wq_bf, wk_bf, wv_bf), ("Wq", "Wk", "Wv"),
                            HHD ** -0.5, identb_s, identf_s, cmask_s, g_bc, lam_bc)
        if PH >= 2:
            self.proj_residual_epilogue("p1", aoT3, wo_bf, "Wo", xq, res1,
                                        ln1w_bf, "ln1w", xT3, identb_s, identf_s, onesb)
        if PH >= 3:
            self.attn_phase(False, xT3, aoT3, (qn_bf, kw_bf, vw_bf), ("qw", "kw", "vw"),
                            QHD ** -0.5, identb_s, identf_s, cmask_s, None, None)
        if PH >= 4:
            self.proj_residual_epilogue("p2", aoT3, ow_bf, "ow", res1, res2,
                                        ln2w_bf, "ln2w", xT3, identb_s, identf_s, onesb)
        aoT_cm.__exit__(None, None, None)

        if PH >= 5:
            self.gate_phase(xT3, gw_bf, wfull_d, identf_s)

        if PH >= 6:
            self.moe_phase(xT3, w1_bf, w2_bf, shgu_bf, shdn_bf, wfull_d, res1, res2,
                           out, onesf, identf_s)


        with nc.Block() as block:
            for eng in ENGINES:
                acts = self.acts[eng]
                if not acts:
                    continue
                def mk(acts):
                    def body(e):
                        for a in acts:
                            a(e)
                    return body
                getattr(block, eng)(mk(acts))
        for cm in self._exits:
            cm.__exit__(None, None, None)
        return nc

    def attn_phase(self, is_diff, xT3, aoT3, w_srcs, cast_keys,
                   scale, identb_s, identf_s, cmask_s, g_bc, lam_bc):
        c = self.cfg
        T, EC, TT, NH = c["T"], c["EC"], c["TT"], c["NH"]
        sp, sv, sa = (self.S[k] for k in ("sp", "sv", "sa"))
        pre = "d" if is_diff else "m"
        cms = []
        def sba(name, shape, dt):
            h, cm = self.alloc_sbuf(pre + name, shape, dt)
            cms.append(cm)
            return h
        qT = sba("qT", [128, T], BF16)
        kT = sba("kT", [128, T], BF16)
        vfm = sba("vfm", [128, T], BF16)
        v_tm = sba("v_tm", [128, T], BF16)
        Pa = sba("Pa", [128, T], BF16)
        Pb = sba("Pb", [128, T], BF16)
        PT4 = [sba(f"PT{j}", [128, T], BF16) for j in range(4 if is_diff else 2)]
        wb = [[sba(f"w{w}b{j}", [128, EC * 128], BF16) for j in range(2)] for w in range(3)]
        stat = sba("stat", [128, 64], F32)
        Ocmb = sba("Ocmb", [128, 128], F32)
        Osqt = sba("Osqt", [128, 128], F32)
        On = [sba(f"On{j}", [128, 128], BF16) for j in range(2)]

        psS, psS_cm = self.alloc_psum(pre + "psS", [128, 2048], F32)
        psO, psO_cm = self.alloc_psum(pre + "psO", [128, 512], F32)
        ptrb, ptrb_cm = self.alloc_psum(pre + "ptrb", [128, 2048], BF16)
        ptr2, ptr2_cm = self.alloc_psum(pre + "ptr2", [128, 1024], BF16)
        pcms = [psS_cm, psO_cm, ptrb_cm, ptr2_cm]

        v_tm3 = v_tm.ap().rearrange("p (c d) -> p c d", c=TT)
        NB = T // 512

        wsem = [[self.sem(f"{pre}wl{w}{j}") for j in range(2)] for w in range(3)]
        wload = [[None, None] for _ in range(3)]
        wfree = [[0, 0] for _ in range(3)]

        def issue_load(h):
            slot = h % 2
            for widx in range(3):
                waits = [self.cast_thr[cast_keys[widx]]]
                if wfree[widx][slot]:
                    waits.append((sp, wfree[widx][slot]))
                wload[widx][slot] = self.dma(
                    "sync", wb[widx][slot][:], w_srcs[widx][h], wsem[widx][slot],
                    waits=waits)

        halves = ((0, 0), (1, 64)) if is_diff else ((0, 0),)
        kdim = 64 if is_diff else 128
        last_hl = halves[-1][0]
        ZB = lambda zp, hl, k: (zp * 2 + hl) * 4 + k
        SSQ, LNV = 60, 61
        RSTD = lambda par: 56 + par

        psS_free = [None, None]
        ptrb_free = [None, None]
        ptr2_free = [None]
        psO_free = [None, None]
        z_free = {}
        ptr_done = {}
        On_free = [None, None]
        ocmb_free = [None]
        rstd_free = [None, None]
        ckp = [0]
        tpp = [0]

        issue_load(0)
        for h in range(NH):
            slot = h % 2
            if h + 1 < NH:
                issue_load(h + 1)

            evs = []
            proj_evac = [None] * NB
            for widx, dst in ((0, qT), (1, kT), (2, vfm)):
                wb3 = wb[widx][slot].ap().rearrange("p (c r) -> p c r", c=EC)
                for kc in range(EC):
                    for nb in range(NB):
                        w = []
                        if kc == 0:
                            if nb == 0:
                                w.append(wload[widx][slot])
                                if widx == 0:
                                    w.append(self.w_xT)
                                    for f in psS_free:
                                        if f:
                                            w.append(f)
                            if proj_evac[nb]:
                                w.append(proj_evac[nb])
                        self.mm(psS[:, nb * 512:(nb + 1) * 512], wb3[:, kc, :],
                                xT3[:, kc, nb * 512:(nb + 1) * 512],
                                start=(kc == 0), stop=(kc == EC - 1),
                                waits=w,
                                inc=sp if kc == EC - 1 else None)
                p_all = sp.n
                for nb in range(NB):
                    self.emit("vector",
                              lambda e, d=dst, nb=nb:
                              e.tensor_copy(d[:, nb * 512:(nb + 1) * 512],
                                            psS[:, nb * 512:(nb + 1) * 512]),
                              waits=[(sp, p_all - NB + 1 + nb)], inc=sv)
                    proj_evac[nb] = (sv, sv.n)
                wfree[widx][slot] = sp.n
                evs.append(sv.n)
            v_qk = evs[1]
            v_vfm = evs[2]
            psS_free = [(sv, evs[2]), (sv, evs[2])]

            for grp in range((TT + 7) // 8):
                tiles = list(range(grp * 8, min(TT, grp * 8 + 8)))
                for j, ct in enumerate(tiles):
                    w = []
                    if j == 0:
                        w.append((sv, v_vfm))
                        if ptr2_free[0]:
                            w.append(ptr2_free[0])
                    self.tr(ptr2[:, j * 128:(j + 1) * 128], vfm[:, ct * 128:(ct + 1) * 128],
                            identb_s[:], waits=w, inc=sp if j == len(tiles) - 1 else None)
                p_tr = sp.n
                g0, gn = tiles[0], len(tiles)
                self.emit("vector",
                          lambda e, g0=g0, gn=gn:
                          e.tensor_copy(v_tm[:, g0 * 128:(g0 + gn) * 128], ptr2[:, 0:gn * 128]),
                          waits=[(sp, p_tr)], inc=sv)
                ptr2_free[0] = (sv, sv.n)
            v_vtm = sv.n

            def emit_S(b):
                ti, pbase = b["ti"], b["pbase"]
                L = 128 * (ti + 1)
                b["chunks"] = []
                for c0 in range(0, L, 1024):
                    c1 = min(L, c0 + 1024)
                    p = ckp[0] & 1
                    ckp[0] += 1
                    base = p * 1024
                    diag = c1 == L
                    first = True

                    def fw():
                        nonlocal first
                        w = []
                        if first:
                            first = False
                            if psS_free[p]:
                                w.append(psS_free[p])
                            if ti == 0:
                                w.append((sv, v_qk))
                        return w
                    for n0 in range(c0, c1, 512):
                        n1 = min(c1, n0 + 512)
                        self.mm(psS[:, base + n0 - c0:base + n1 - c0],
                                qT[pbase:pbase + kdim, ti * 128:(ti + 1) * 128],
                                kT[pbase:pbase + kdim, n0:n1],
                                True, True, waits=fw(),
                                inc=sp if n1 == c1 else None)
                    p_sc = sp.n
                    v_mask = 0
                    if diag:
                        off = base + ti * 128 - c0
                        self.emit("vector",
                                  lambda e, off=off: e.tensor_tensor(
                                      psS[:, off:off + 128], psS[:, off:off + 128],
                                      cmask_s[:], ALU.add),
                                  waits=[(sp, p_sc)])
                        v_mask = sv.n
                    b["chunks"].append(dict(p=p, base=base, c0=c0, c1=c1,
                                            p_sc=p_sc, v_mask=v_mask))

            def emit_exp(b):
                hl, zp, P_sb = b["hl"], b["zp"], b["P"]
                L = 128 * (b["ti"] + 1)
                parts = []
                for ch in b["chunks"]:
                    e1 = min(ch["c1"], L - 128)
                    if e1 > ch["c0"]:
                        parts.append((ch, ch["c0"], e1, 0))
                dch = b["chunks"][-1]
                parts.append((dch, L - 128, L, dch["v_mask"]))
                b["exp"] = []
                nz = 0
                for ci, (ch, e0, e1, vm) in enumerate(parts):
                    w = [(sp, ch["p_sc"])]
                    if vm:
                        w.append((sv, vm))
                    if ci == 0:
                        if z_free.get((zp, hl)):
                            w.append((sv, z_free[(zp, hl)]))
                        if ptr_done.get(b["pk"]):
                            w.append((sp, ptr_done[b["pk"]]))
                    zc = ZB(zp, hl, nz)
                    nz += 1
                    self.emit("scalar",
                              lambda e, P=P_sb, ch=ch, e0=e0, e1=e1, zc=zc, sk=scale:
                              e.activation(P[:, e0:e1],
                                           psS[:, ch["base"] + e0 - ch["c0"]:
                                               ch["base"] + e1 - ch["c0"]],
                                           AF.Exp, scale=sk,
                                           accum_out=stat[:, zc:zc + 1]),
                              waits=w, inc=sa)
                    b["exp"].append(sa.n)
                    psS_free[ch["p"]] = (sa, sa.n)
                b["nz"] = nz

            def emit_T(b):
                ti = b["ti"]
                P_sb, PT_sb = b["P"], b["PT"]
                nt = ti + 1
                b["evacs"] = []
                for g0 in range(0, nt, 8):
                    gn = min(8, nt - g0)
                    tp = tpp[0] & 1
                    tpp[0] += 1
                    tbase = tp * 1024
                    glast = (nt - 1) // 8
                    for j in range(gn):
                        ct = g0 + j
                        w = []
                        if j == 0:
                            ei = -1 if g0 // 8 == glast else g0 // 8
                            w.append((sa, b["exp"][ei]))
                            if ptrb_free[tp]:
                                w.append(ptrb_free[tp])
                        self.tr(ptrb[:, tbase + j * 128:tbase + (j + 1) * 128],
                                P_sb[:, ct * 128:(ct + 1) * 128], identb_s[:],
                                waits=w, inc=sp if j == gn - 1 else None)
                    p_tr = sp.n
                    self.emit("vector",
                              lambda e, PT=PT_sb, g0=g0, gn=gn, tbase=tbase:
                              e.tensor_copy(PT[:, g0 * 128:(g0 + gn) * 128],
                                            ptrb[:, tbase:tbase + gn * 128]),
                              waits=[(sp, p_tr)], inc=sv)
                    ptrb_free[tp] = (sv, sv.n)
                    b["evacs"].append((sv, sv.n))
                ptr_done[b["pk"]] = sp.n

            def emit_AV(b):
                ti, hl, par = b["ti"], b["hl"], b["par"]
                PT_sb = b["PT"]
                nt = ti + 1
                ob = par * 256 + hl * 128
                for ct in range(nt):
                    w = []
                    if ct % 8 == 0:
                        w.append(b["evacs"][ct // 8])
                    if ct == 0:
                        w.append((sv, v_vtm))
                        if psO_free[par]:
                            w.append(psO_free[par])
                    self.mm(psO[:, ob:ob + 128],
                            PT_sb[:, ct * 128:(ct + 1) * 128], v_tm3[:, ct, :],
                            start=(ct == 0), stop=(ct == nt - 1),
                            waits=w, inc=sp if ct == nt - 1 else None)
                b["av"] = sp.n

            def emit_combine(tb):
                ti, par, zp = tb[0]["ti"], tb[0]["par"], tb[0]["zp"]
                wz = [(sa, tb[-1]["exp"][-1])]
                for b in tb:
                    z0 = ZB(zp, b["hl"], 0)
                    for k in range(1, b["nz"]):
                        self.emit("vector",
                                  lambda e, z0=z0, zk=ZB(zp, b["hl"], k):
                                  e.tensor_tensor(stat[:, z0:z0 + 1],
                                                  stat[:, z0:z0 + 1],
                                                  stat[:, zk:zk + 1], ALU.add),
                                  waits=wz)
                        wz = []
                    self.emit("vector",
                              lambda e, z0=z0: e.reciprocal(stat[:, z0 + 2:z0 + 3],
                                                            stat[:, z0:z0 + 1]),
                              waits=wz)
                    wz = []
                    z_free[(zp, b["hl"])] = sv.n
                ob = par * 256
                if is_diff:
                    r1 = ZB(zp, 0, 2)
                    r2 = ZB(zp, 1, 2)
                    cf = ZB(zp, 1, 3)
                    self.emit("vector",
                              lambda e, r2=r2, cf=cf: e.tensor_scalar(
                                  stat[:, cf:cf + 1], stat[:, r2:r2 + 1],
                                  lam_bc[:], -1.0, ALU.mult, ALU.mult))
                    w = [(sp, tb[-1]["av"])]
                    if ocmb_free[0]:
                        w.append(ocmb_free[0])
                    self.emit("vector",
                              lambda e, ob=ob, r1=r1: e.tensor_scalar(
                                  Ocmb[:], psO[:, ob:ob + 128],
                                  stat[:, r1:r1 + 1], None, ALU.mult),
                              waits=w)
                    self.emit("vector",
                              lambda e, ob=ob, cf=cf: e.scalar_tensor_tensor(
                                  Ocmb[:], psO[:, ob + 128:ob + 256],
                                  stat[:, cf:cf + 1], Ocmb[:], ALU.mult, ALU.add),
                              inc=sv)
                    psO_free[par] = (sv, sv.n)
                    v_oc = sv.n
                    self.emit("scalar",
                              lambda e: e.activation(Osqt[:], Ocmb[:], AF.Square,
                                                     accum_out=stat[:, SSQ:SSQ + 1]),
                              waits=[(sv, v_oc)])
                    self.emit("scalar",
                              lambda e: e.activation(stat[:, LNV:LNV + 1],
                                                     stat[:, SSQ:SSQ + 1], AF.Ln,
                                                     bias=self.epsv[:, 0:1],
                                                     scale=1.0 / 128.0))
                    w = [rstd_free[par]] if rstd_free[par] else []
                    self.emit("scalar",
                              lambda e, par=par: e.activation(
                                  stat[:, RSTD(par):RSTD(par) + 1],
                                  stat[:, LNV:LNV + 1], AF.Exp, scale=-0.5),
                              waits=w, inc=sa)
                    a_rstd = sa.n
                    ocmb_free[0] = (sa, a_rstd)
                    w = [(sa, a_rstd)]
                    if On_free[par]:
                        w.append(On_free[par])
                    self.emit("vector",
                              lambda e, par=par: e.scalar_tensor_tensor(
                                  On[par][:], Ocmb[:], stat[:, RSTD(par):RSTD(par) + 1],
                                  g_bc[:], ALU.mult, ALU.mult),
                              waits=w, inc=sv)
                    rstd_free[par] = (sv, sv.n)
                else:
                    r1 = ZB(zp, 0, 2)
                    w = [(sp, tb[0]["av"])]
                    if On_free[par]:
                        w.append(On_free[par])
                    self.emit("vector",
                              lambda e, ob=ob, r1=r1, par=par: e.tensor_scalar(
                                  On[par][:], psO[:, ob:ob + 128],
                                  stat[:, r1:r1 + 1], None, ALU.mult),
                              waits=w, inc=sv)
                    psO_free[par] = (sv, sv.n)
                v_on = sv.n
                w = [(sv, v_on)]
                if ptr2_free[0]:
                    w.append(ptr2_free[0])
                self.tr(ptr2[:, 0:128], On[par][:], identb_s[:], waits=w, inc=sp)
                p_ot = sp.n
                On_free[par] = (sp, p_ot)
                self.emit("vector",
                          lambda e, h=h, ti=ti: e.tensor_copy(
                              aoT3[:, h, ti * 128:(ti + 1) * 128], ptr2[:, 0:128]),
                          waits=[(sp, p_ot)], inc=sv)
                ptr2_free[0] = (sv, sv.n)

            pend = []
            avq = []
            tile_blocks = []

            def flush_T():
                b0 = pend.pop(0)
                emit_T(b0)
                avq.append(b0)

            def flush_AV():
                b0 = avq.pop(0)
                emit_AV(b0)
                tile_blocks.append(b0)
                if b0["hl"] == last_hl:
                    emit_combine(tile_blocks)
                    tile_blocks.clear()

            for ti in range(TT):
                for (hl, pbase) in halves:
                    if is_diff:
                        P_sb = (Pa, Pb)[hl]
                        PT_sb = PT4[hl * 2 + (ti & 1)]
                    else:
                        P_sb = (Pa, Pb)[ti & 1]
                        PT_sb = PT4[ti & 1]
                    b = dict(ti=ti, hl=hl, pbase=pbase, P=P_sb, PT=PT_sb, par=ti & 1,
                             zp=ti % 3, pk=(hl if is_diff else ti & 1))
                    emit_S(b)
                    while len(pend) >= 2:
                        flush_T()
                    while len(avq) >= 1:
                        flush_AV()
                    emit_exp(b)
                    pend.append(b)
            while pend:
                flush_T()
            while avq:
                flush_AV()

        self.barrier()
        for cm in reversed(cms + pcms):
            cm.__exit__(None, None, None)

    def proj_residual_epilogue(self, mode, aoT3, w_bf, cast_key, resid_src, spill,
                               lnw_bf, lnw_key, xT3, identb_s, identf_s, onesb):
        c = self.cfg
        T, E, EC, TAU, NTAU = c["T"], c["E"], c["EC"], c["TAU"], c["NTAU"]
        sp, sv, sa = (self.S[k] for k in ("sp", "sv", "sa"))
        cms = []
        def sba(name, shape, dt):
            h, cm = self.alloc_sbuf(mode + name, shape, dt)
            cms.append(cm)
            return h
        wob = [sba(f"wob{j}", [128, EC * 128], BF16) for j in range(3)]
        tmpy = [sba(f"tmpy{j}", [128, 512], F32) for j in range(2)]
        h1blk = sba("h1blk", [128, 2 * E], F32)
        xload = [sba(f"xload{j}", [128, E], F32) for j in range(2)]
        h1n = sba("h1n", [128, E], F32)
        xlnt = sba("xlnt", [128, E], BF16)
        nq = E // 512
        stats = sba("stats", [128, 6 * nq + 8], F32)
        lnwbc = sba("lnwbc", [128, E], BF16)
        psA, psA_cm = self.alloc_psum(mode + "psA", [128, 2048], F32)
        psT, psT_cm = self.alloc_psum(mode + "psT", [128, 1024], F32)
        ptrb, ptrb_cm = self.alloc_psum(mode + "ptrb", [128, 1024], BF16)
        pcms = [psA_cm, psT_cm, ptrb_cm]
        S = 6 * nq
        NT2 = TAU // 128
        psT_free = [None, None]
        ptrb_free = None

        wosem = [self.sem(f"{mode}wo{j}") for j in range(3)]
        xlsem = [self.sem(f"{mode}xl{j}") for j in range(2)]
        spsem = [self.sem(f"{mode}sp{j}") for j in range(2)]

        for nb in range(nq):
            w = []
            if nb == 0:
                w.append(self.cast_thr[lnw_key])
                w.append((sv, self.v_ones))
            self.mm(psA[:, nb * 512:(nb + 1) * 512], onesb[:],
                    lnw_bf[:, nb * 512:(nb + 1) * 512], True, True,
                    waits=w, inc=sp if nb == nq - 1 else None)
        p_lnw = sp.n
        for nb in range(nq):
            self.emit("vector",
                      lambda e, nb=nb: e.tensor_copy(lnwbc[:, nb * 512:(nb + 1) * 512],
                                                     psA[:, nb * 512:(nb + 1) * 512]),
                      waits=[(sp, p_lnw)] if nb == 0 else (),
                      inc=sv if nb == nq - 1 else None)
        psA_free = (sv, sv.n)

        wo_done = [None] * 3
        wo_free = [0] * 3
        xl_free = [0, 0]
        xl_done = [None, None]
        tmpy_free = [0, 0]
        h1n_free = None
        hb_free = [None, None]
        xlnt_free = 0

        pairs = [(tau, m) for tau in range(NTAU) for m in range(EC)]

        def load_wo(idx):
            tau, m = pairs[idx]
            slot = idx % 3
            waits = [self.cast_thr[cast_key]]
            if wo_free[slot]:
                waits.append((sp, wo_free[slot]))
            wo_done[slot] = self.dma(
                "sync", wob[slot][:], w_bf[m], wosem[slot], waits=waits)

        load_wo(0)
        load_wo(1)
        for tau in range(NTAU):
            for tt in range(NT2):
                slot = tt % 2
                r0 = tau * TAU + tt * 128
                w = [(sv, xl_free[slot])] if xl_free[slot] else []
                xl_done[slot] = self.dma("sync", xload[slot][:],
                                         resid_src[r0:r0 + 128, :], xlsem[slot], waits=w)

            for pi in range(EC // 2):
                m0 = 2 * pi
                m1 = m0 + 1
                idx0 = tau * EC + m0
                s0, s1 = idx0 % 3, (idx0 + 1) % 3
                wb30 = wob[s0].ap().rearrange("p (c r) -> p c r", c=EC)
                wb31 = wob[s1].ap().rearrange("p (c r) -> p c r", c=EC)
                t0 = tau * TAU
                for kc in range(EC):
                    w0, w1 = [], []
                    if kc == 0:
                        w0.append(wo_done[s0])
                        w1.append(wo_done[s1])
                        if psA_free:
                            w0.append(psA_free)
                    self.mm(psA[:, 0:TAU], wb30[:, kc, :], aoT3[:, kc, t0:t0 + TAU],
                            start=(kc == 0), stop=(kc == EC - 1),
                            waits=w0, inc=sp if kc == EC - 1 else None)
                    self.mm(psA[:, 512:512 + TAU], wb31[:, kc, :], aoT3[:, kc, t0:t0 + TAU],
                            start=(kc == 0), stop=(kc == EC - 1),
                            waits=w1, inc=sp if kc == EC - 1 else None)
                p_y1 = sp.n
                wo_free[s0] = p_y1 - 1
                wo_free[s1] = p_y1
                for pf in (idx0 + 2, idx0 + 3):
                    if pf < len(pairs):
                        load_wo(pf)
                for mi, (pv, off) in enumerate(((p_y1 - 1, 0), (p_y1, 512))):
                    w = [(sp, pv)]
                    if tmpy_free[mi]:
                        w.append((sv, tmpy_free[mi]))
                    self.emit("vector",
                              lambda e, sl=mi, off=off: e.tensor_copy(
                                  tmpy[sl][:, 0:TAU], psA[:, off:off + TAU]),
                              waits=w, inc=sv)
                v_y = sv.n
                psA_free = (sv, v_y)
                for tt in range(NT2):
                    for mi, m in enumerate((m0, m1)):
                        tp = (tt * 2 + mi) & 1
                        wtr = [(sv, v_y)] if (tt == 0 and mi == 0) else []
                        if psT_free[tp]:
                            wtr.append(psT_free[tp])
                        self.tr(psT[:, tp * 512:tp * 512 + 128],
                                tmpy[mi][:, tt * 128:(tt + 1) * 128],
                                identf_s[:], waits=wtr, inc=sp)
                        p_tr = sp.n
                        wadd = [(sp, p_tr), xl_done[tt % 2]]
                        if hb_free[tt % 2]:
                            wadd.append(hb_free[tt % 2])
                        self.emit("vector",
                                  lambda e, tt=tt, m=m, sl=tt % 2, tp=tp:
                                  e.tensor_tensor(h1blk[:, tt * E + m * 128:tt * E + (m + 1) * 128],
                                                  psT[:, tp * 512:tp * 512 + 128],
                                                  xload[sl][:, m * 128:(m + 1) * 128],
                                                  ALU.add),
                                  waits=wadd, inc=sv)
                        psT_free[tp] = (sv, sv.n)
                tmpy_free[0] = sv.n
                tmpy_free[1] = sv.n
            v_h1 = sv.n

            for tt in range(NT2):
                hb = h1blk.ap()[:, tt * E:(tt + 1) * E]
                r0 = tau * TAU + tt * 128

                if mode == "p1":
                    for q in range(nq):
                        self.emit("vector",
                                  lambda e, q=q, hb=hb: e.bn_stats(stats[:, q * 6:(q + 1) * 6],
                                                                   hb[:, q * 512:(q + 1) * 512]),
                                  waits=[(sv, v_h1)] if q == 0 else ())
                    self.emit("vector", lambda e: e.bn_aggr(stats[:, S:S + 2], stats[:, 0:S]),
                              inc=sv)
                    v_mv = sv.n
                    self.emit("scalar", lambda e: e.activation(stats[:, S + 2:S + 3],
                                                               stats[:, S + 1:S + 2], AF.Ln,
                                                               bias=self.epsv[:, 0:1]),
                              waits=[(sv, v_mv)])
                    self.emit("scalar", lambda e: e.activation(stats[:, S + 3:S + 4],
                                                               stats[:, S + 2:S + 3], AF.Exp,
                                                               scale=-0.5),
                              inc=sa)
                    a_rstd = sa.n
                    w = [(sa, a_rstd)]
                    if h1n_free:
                        w.append(h1n_free)
                    self.emit("vector",
                              lambda e, hb=hb: e.tensor_scalar(h1n[:], hb, stats[:, S:S + 1],
                                                               stats[:, S + 3:S + 4],
                                                               ALU.subtract, ALU.mult),
                              waits=w, inc=sv)
                    v_h1n = sv.n
                    h1n_free = self.dma("sync", spill[r0:r0 + 128, :], h1n[:],
                                        spsem[0], waits=[(sv, v_h1n)])
                    rms_src = h1n.ap()[:, :]
                    v_src = v_h1n
                else:
                    hb_free[tt] = self.dma("sync", spill[r0:r0 + 128, :], hb,
                                           spsem[tt % 2], waits=[(sv, v_h1)])
                    rms_src = hb
                    v_src = v_h1

                for q in range(nq):
                    self.emit("vector",
                              lambda e, q=q, rs=rms_src: e.bn_stats(stats[:, q * 6:(q + 1) * 6],
                                                                    rs[:, q * 512:(q + 1) * 512]),
                              waits=[(sv, v_src)] if q == 0 else ())
                self.emit("vector", lambda e: e.bn_aggr(stats[:, S:S + 2], stats[:, 0:S]))
                self.emit("vector", lambda e: e.tensor_tensor(stats[:, S + 4:S + 5],
                                                              stats[:, S:S + 1],
                                                              stats[:, S:S + 1], ALU.mult))
                self.emit("vector", lambda e: e.tensor_tensor(stats[:, S + 5:S + 6],
                                                              stats[:, S + 4:S + 5],
                                                              stats[:, S + 1:S + 2], ALU.add),
                          inc=sv)
                v_msq = sv.n
                self.emit("scalar", lambda e: e.activation(stats[:, S + 6:S + 7],
                                                           stats[:, S + 5:S + 6], AF.Ln,
                                                           bias=self.epsv[:, 1:2]),
                          waits=[(sv, v_msq)])
                self.emit("scalar", lambda e: e.activation(stats[:, S + 7:S + 8],
                                                           stats[:, S + 6:S + 7], AF.Exp,
                                                           scale=-0.5),
                          inc=sa)
                a_rms = sa.n
                w = [(sa, a_rms)]
                if xlnt_free:
                    w.append((sv, xlnt_free))
                self.emit("vector",
                          lambda e, rs=rms_src: e.scalar_tensor_tensor(xlnt[:], rs,
                                                                       stats[:, S + 7:S + 8],
                                                                       lnwbc[:], ALU.mult, ALU.mult),
                          waits=w, inc=sv)
                v_xln = sv.n
                for g0 in range(0, EC, 8):
                    gn = min(8, EC - g0)
                    for j in range(gn):
                        cc = g0 + j
                        w = []
                        if j == 0:
                            w.append((sv, v_xln))
                            if ptrb_free:
                                w.append(ptrb_free)
                        self.tr(ptrb[:, j * 128:(j + 1) * 128], xlnt[:, cc * 128:(cc + 1) * 128],
                                identb_s[:], waits=w, inc=sp if j == gn - 1 else None)
                    p_tr = sp.n
                    for j in range(gn):
                        cc = g0 + j
                        self.emit("vector",
                                  lambda e, cc=cc, j=j, r0=r0:
                                  e.tensor_copy(xT3[:, cc, r0:r0 + 128],
                                                ptrb[:, j * 128:(j + 1) * 128]),
                                  waits=[(sp, p_tr)] if j == 0 else (),
                                  inc=sv if j == gn - 1 else None)
                    ptrb_free = (sv, sv.n)
                xlnt_free = sv.n
                xl_free[tt % 2] = sv.n

        self.barrier()
        for cm in reversed(cms + pcms):
            cm.__exit__(None, None, None)

    def gate_phase(self, xT3, gw_bf, wfull_d, identf_s):
        c = self.cfg
        T, E, EC, TT, NE, TOPK = c["T"], c["E"], c["EC"], c["TT"], c["NE"], c["TOPK"]
        sp, sv, sa = (self.S[k] for k in ("sp", "sv", "sa"))
        cms = []
        def sba(name, shape, dt):
            h, cm = self.alloc_sbuf(name, shape, dt)
            cms.append(cm)
            return h
        gwT = sba("gwT", [128, EC * NE], BF16)
        logT = sba("logT", [NE, T], F32)
        probs = sba("probs", [128, NE], F32)
        work = sba("work", [128, NE], F32)
        eqm = sba("eqm", [128, NE], F32)
        gst = sba("gstat", [128, 8], F32)
        wfT = sba("wfT", [NE, T], F32)
        psT, psT_cm = self.alloc_psum("gpsT", [128, 512], F32)
        pcms = [psT_cm]

        s_gw = self.sem("g_gw")
        w_gw = self.dma("sync", gwT[:], gw_bf, s_gw, waits=[self.cast_thr["gw"]])
        gwT3 = gwT.ap().rearrange("p (c r) -> p c r", c=EC)

        psT_free = None
        for nb in range(T // 512):
            for kc in range(EC):
                w = []
                if kc == 0:
                    w.append(w_gw)
                    if psT_free:
                        w.append(psT_free)
                self.mm(psT[0:NE, 0:512], gwT3[:, kc, :],
                        xT3[:, kc, nb * 512:(nb + 1) * 512],
                        start=(kc == 0), stop=(kc == EC - 1),
                        waits=w, inc=sp if kc == EC - 1 else None)
            p_l = sp.n
            self.emit("vector",
                      lambda e, nb=nb: e.tensor_copy(logT[:, nb * 512:(nb + 1) * 512],
                                                     psT[0:NE, 0:512]),
                      waits=[(sp, p_l)], inc=sv)
            psT_free = (sv, sv.n)
        v_log = sv.n

        eqm_free = 0
        for ti in range(TT):
            w = [(sv, v_log)] if ti == 0 else []
            if psT_free:
                w.append(psT_free)
            self.tr(psT[:, 0:NE], logT[:, ti * 128:(ti + 1) * 128],
                    identf_s[0:NE, 0:NE], waits=w, inc=sp)
            p_t = sp.n
            self.emit("vector", lambda e: e.tensor_copy(work[:], psT[:, 0:NE]),
                      waits=[(sp, p_t)], inc=sv)
            psT_free = (sv, sv.n)
            self.emit("vector", lambda e: e.tensor_reduce(gst[:, 0:1], work[:], AX.X,
                                                          ALU.max, negate=True), inc=sv)
            v_m = sv.n
            self.emit("scalar", lambda e: e.activation(probs[:], work[:], AF.Exp,
                                                       bias=gst[:, 0:1],
                                                       accum_out=gst[:, 1:2]),
                      waits=[(sv, v_m)], inc=sa)
            a_e = sa.n
            self.emit("vector", lambda e: e.reciprocal(gst[:, 2:3], gst[:, 1:2]),
                      waits=[(sa, a_e)])
            self.emit("vector", lambda e: e.tensor_scalar(probs[:], probs[:], gst[:, 2:3],
                                                          None, ALU.mult))
            self.emit("vector", lambda e: e.tensor_copy(work[:], probs[:]))
            self.emit("vector", lambda e: e.memset(gst[:, 3:4], 0.0))
            first_eq = True
            for it in range(TOPK):
                self.emit("vector", lambda e: e.tensor_reduce(gst[:, 4:5], work[:], AX.X, ALU.max))
                self.emit("vector", lambda e: e.tensor_tensor(gst[:, 3:4], gst[:, 3:4],
                                                              gst[:, 4:5], ALU.add))
                if it < TOPK - 1:
                    w = [(sp, eqm_free)] if (first_eq and eqm_free) else []
                    first_eq = False
                    self.emit("vector", lambda e: e.tensor_scalar(eqm[:], work[:], gst[:, 4:5],
                                                                  None, ALU.is_equal), waits=w)
                    self.emit("vector", lambda e: e.scalar_tensor_tensor(work[:], eqm[:], -2.0,
                                                                         work[:], ALU.mult, ALU.add))
            w = [(sp, eqm_free)] if (first_eq and eqm_free) else []
            self.emit("vector", lambda e: e.tensor_scalar(eqm[:], probs[:], gst[:, 4:5],
                                                          None, ALU.is_ge), waits=w)
            self.emit("vector", lambda e: e.reciprocal(gst[:, 5:6], gst[:, 3:4]))
            self.emit("vector", lambda e: e.tensor_tensor(eqm[:], eqm[:], probs[:], ALU.mult))
            self.emit("vector", lambda e: e.tensor_scalar(eqm[:], eqm[:], gst[:, 5:6],
                                                          None, ALU.mult), inc=sv)
            v_wf = sv.n
            w = [(sv, v_wf)]
            if psT_free:
                w.append(psT_free)
            self.tr(psT[0:NE, 0:128], eqm[:], identf_s[:], waits=w, inc=sp)
            p_t2 = sp.n
            eqm_free = p_t2
            self.emit("vector",
                      lambda e, ti=ti: e.tensor_copy(wfT[:, ti * 128:(ti + 1) * 128],
                                                     psT[0:NE, 0:128]),
                      waits=[(sp, p_t2)], inc=sv)
            psT_free = (sv, sv.n)
        v_wfT = sv.n
        s_wf = self.sem("g_wf")
        self.wfull_done = self.dma("sync", wfull_d, wfT[:], s_wf, waits=[(sv, v_wfT)])

        self.barrier()
        for cm in reversed(cms + pcms):
            cm.__exit__(None, None, None)

    def moe_phase(self, xT3, w1_bf, w2_bf, shgu_bf, shdn_bf, wfull_d, res1, res2,
                  out, onesf, identf_s):
        c = self.cfg
        T, E, EC = c["T"], c["E"], c["EC"]
        NE, I1, IC, SHI = c["NE"], c["I1"], c["IC"], c["SHI"]
        TU, NU = c["TU"], c["NU"]
        sp, sv, sa = (self.S[k] for k in ("sp", "sv", "sa"))
        NB = TU // 512

        specs = []
        for e in range(NE):
            specs.append(dict(kind="reg", e=e, gt0=0, ut0=IC, wsrc=w1_bf[e],
                              w2src=w2_bf[e], ck1=f"w1_{e}", ck2=f"w2_{e}"))
        specs.append(dict(kind="sh", e=NE, gt0=0, ut0=2 * IC, wsrc=shgu_bf,
                          w2src=shdn_bf[0], ck1="shgu", ck2="shdn"))
        specs.append(dict(kind="sh", e=NE + 1, gt0=IC, ut0=3 * IC, wsrc=shgu_bf,
                          w2src=shdn_bf[1], ck1="shgu", ck2="shdn"))

        cms = []
        def sba(name, shape, dt):
            h, cm = self.alloc_sbuf(name, shape, dt)
            cms.append(cm)
            return h
        routed = sba("routed", [128, EC * TU], F32)
        psG, psG_cm = self.alloc_psum("psG", [128, 1024], F32)
        psU, psU_cm = self.alloc_psum("psU", [128, 1024], F32)
        psD, psD_cm = self.alloc_psum("psD", [128, 1024], F32)
        psT, psT_cm = self.alloc_psum("mpsT", [128, 512], F32)
        pcms = [psG_cm, psU_cm, psD_cm, psT_cm]

        routed3 = routed.ap().rearrange("p (c t) -> p c t", c=EC)

        sW1g = [self.sem(f"mw1g{j}") for j in range(2)]
        sW1u = [self.sem(f"mw1u{j}") for j in range(2)]
        sW2 = [self.sem(f"mw2{j}") for j in range(2)]
        sWfr = self.sem("mwfr")

        w1g_done = [None, None]; w1g_free = [0, 0]
        w1u_done = [None, None]; w1u_free = [0, 0]
        w2_done = [None, None]; w2_free = [0, 0]
        gst_free = [0, 0]; ust_free = [0, 0]
        wbc_free = [0, 0]
        wfrow_free = 0
        psG_free = None; psU_free = None; psD_free = None
        psT_free = None
        act_free = 0

        for u in range(NU):
            t0 = u * TU
            mcms = []
            def mba(name, shape, dt, u=u, mcms=mcms):
                h, cm = self.alloc_sbuf(f"u{u}" + name, shape, dt)
                mcms.append(cm)
                return h
            act = mba("act", [128, IC * TU], BF16)
            w1g = [mba(f"w1g{j}", [128, EC * 128], BF16) for j in range(2)]
            w1u = [mba(f"w1u{j}", [128, EC * 128], BF16) for j in range(2)]
            w2b = [mba(f"w2b{j}", [128, IC * 128], BF16) for j in range(2)]
            gstp = [mba(f"gstp{j}", [128, TU], BF16) for j in range(2)]
            ustp = [mba(f"ustp{j}", [128, TU], BF16) for j in range(2)]
            wbc = [mba(f"wbc{j}", [128, TU], F32) for j in range(2)]
            wfrow = mba("wfrow", [1, TU], F32)
            act3 = act.ap().rearrange("p (c t) -> p c t", c=IC)

            for si, spec in enumerate(specs):
                eslot = spec["e"] % 2
                v_wbc = None
                if spec["kind"] == "reg":
                    w = [self.wfull_done]
                    if wfrow_free:
                        w.append((sp, wfrow_free))
                    g_wfr = self.dma("sync", wfrow[:],
                                     wfull_d[spec["e"]:spec["e"] + 1, t0:t0 + TU],
                                     sWfr, waits=w)
                    for nb in range(NB):
                        w = [g_wfr]
                        if nb == 0 and wbc_free[eslot]:
                            w.append((sv, wbc_free[eslot]))
                        if psT_free:
                            w.append(psT_free)
                        self.mm(psT[:, 0:512], onesf[:], wfrow[:, nb * 512:(nb + 1) * 512],
                                True, True, waits=w, inc=sp)
                        p_b = sp.n
                        self.emit("vector",
                                  lambda e_, nb=nb, sl=eslot:
                                  e_.tensor_copy(wbc[sl][:, nb * 512:(nb + 1) * 512],
                                                 psT[:, 0:512]),
                                  waits=[(sp, p_b)], inc=sv)
                        psT_free = (sv, sv.n)
                    wfrow_free = sp.n
                    v_wbc = sv.n

                for cc in range(IC):
                    slotc = cc % 2
                    w = [self.cast_thr[spec["ck1"]]]
                    if w1g_free[slotc]:
                        w.append((sp, w1g_free[slotc]))
                    w1g_done[slotc] = self.dma(
                        "sync", w1g[slotc][:], spec["wsrc"][spec["gt0"] + cc],
                        sW1g[slotc], waits=w)
                    w = [self.cast_thr[spec["ck1"]]]
                    if w1u_free[slotc]:
                        w.append((sp, w1u_free[slotc]))
                    w1u_done[slotc] = self.dma(
                        "sync", w1u[slotc][:], spec["wsrc"][spec["ut0"] + cc],
                        sW1u[slotc], waits=w)

                    wg3 = w1g[slotc].ap().rearrange("p (c r) -> p c r", c=EC)
                    wu3 = w1u[slotc].ap().rearrange("p (c r) -> p c r", c=EC)
                    for kc in range(EC):
                        for ps, w3, done_, free_ in ((psG, wg3, w1g_done, psG_free),
                                                     (psU, wu3, w1u_done, psU_free)):
                            for nb in range(NB):
                                w = []
                                if kc == 0 and nb == 0:
                                    w.append(done_[slotc])
                                    if free_:
                                        w.append(free_)
                                    if ps is psG and act_free:
                                        w.append((sp, act_free))
                                last = (kc == EC - 1 and nb == NB - 1 and ps is psU)
                                self.mm(ps[:, nb * 512:(nb + 1) * 512], w3[:, kc, :],
                                        xT3[:, kc, t0 + nb * 512:t0 + (nb + 1) * 512],
                                        start=(kc == 0), stop=(kc == EC - 1),
                                        waits=w, inc=sp if last else None)
                    p_um = sp.n
                    p_gm = p_um
                    w1g_free[slotc] = p_um
                    w1u_free[slotc] = p_um

                    w = [(sp, p_gm)]
                    if gst_free[slotc]:
                        w.append((sv, gst_free[slotc]))
                    self.emit("scalar",
                              lambda e_, sl=slotc: e_.activation(gstp[sl][:], psG[:, 0:c['TU']],
                                                                 AF.Silu),
                              waits=w, inc=sa)
                    a_g = sa.n
                    psG_free = (sa, a_g)
                    w = [(sp, p_um)]
                    if v_wbc is not None:
                        w.append((sv, v_wbc))
                    if ust_free[slotc]:
                        w.append((sv, ust_free[slotc]))
                    if spec["kind"] == "reg":
                        self.emit("vector",
                                  lambda e_, sl=eslot, s2=slotc:
                                  e_.tensor_tensor(ustp[s2][:], psU[:, 0:c['TU']], wbc[sl][:],
                                                   ALU.mult),
                                  waits=w, inc=sv)
                    else:
                        self.emit("vector",
                                  lambda e_, s2=slotc: e_.tensor_copy(ustp[s2][:], psU[:, 0:c['TU']]),
                                  waits=w, inc=sv)
                    v_u = sv.n
                    psU_free = (sv, v_u)
                    w = [(sa, a_g), (sv, v_u)]
                    self.emit("vector",
                              lambda e_, cc=cc, s2=slotc:
                              e_.tensor_tensor(act3[:, cc, :], gstp[s2][:], ustp[s2][:], ALU.mult),
                              waits=w, inc=sv)
                    gst_free[slotc] = sv.n
                    ust_free[slotc] = sv.n
                act_done = sv.n
                if spec["kind"] == "reg":
                    wbc_free[eslot] = act_done

                for m in range(EC):
                    slotm = m % 2
                    w = [self.cast_thr[spec["ck2"]]]
                    if w2_free[slotm]:
                        w.append((sp, w2_free[slotm]))
                    w2_done[slotm] = self.dma(
                        "sync", w2b[slotm][:], spec["w2src"][m],
                        sW2[slotm], waits=w)
                    w23 = w2b[slotm].ap().rearrange("p (c r) -> p c r", c=IC)
                    for kc in range(IC):
                        for nb in range(NB):
                            w = []
                            if kc == 0 and nb == 0:
                                w.append(w2_done[slotm])
                                w.append((sv, act_done))
                                if psD_free:
                                    w.append(psD_free)
                            last = (kc == IC - 1 and nb == NB - 1)
                            self.mm(psD[:, nb * 512:(nb + 1) * 512], w23[:, kc, :],
                                    act3[:, kc, nb * 512:(nb + 1) * 512],
                                    start=(kc == 0), stop=(kc == IC - 1),
                                    waits=w, inc=sp if last else None)
                    p_d = sp.n
                    w2_free[slotm] = p_d
                    if m == EC - 1:
                        act_free = p_d
                    if si == 0:
                        self.emit("vector",
                                  lambda e_, m=m: e_.tensor_copy(routed3[:, m, :], psD[:, 0:c['TU']]),
                                  waits=[(sp, p_d)], inc=sv)
                    else:
                        self.emit("vector",
                                  lambda e_, m=m: e_.tensor_tensor(routed3[:, m, :],
                                                                   routed3[:, m, :],
                                                                   psD[:, 0:c['TU']],
                                                                   ALU.add),
                                  waits=[(sp, p_d)], inc=sv)
                    psD_free = (sv, sv.n)
            v_routed = sv.n

            self.barrier()
            for cm in reversed(mcms):
                cm.__exit__(None, None, None)
            psT_free = self.final_epilogue(u, routed3, res1, res2, out, identf_s,
                                           psT, psT_free, v_routed)

        self.barrier()
        for cm in reversed(cms + pcms):
            cm.__exit__(None, None, None)

    def final_epilogue(self, u, routed3, res1, res2, out, identf_s, psT,
                       psT_free, v_routed):
        c = self.cfg
        T, E, EC, TU = c["T"], c["E"], c["EC"], c["TU"]
        sp, sv, sa = (self.S[k] for k in ("sp", "sv", "sa"))
        cms = []
        def sba(name, shape, dt):
            h, cm = self.alloc_sbuf(f"f{u}" + name, shape, dt)
            cms.append(cm)
            return h
        r1l = [sba(f"r1l{j}", [128, E], F32) for j in range(2)]
        r2l = [sba(f"r2l{j}", [128, E], F32) for j in range(2)]
        ftile = sba("ftile", [128, E], F32)
        ot = [sba(f"ot{j}", [128, E], F32) for j in range(2)]
        nq = E // 512
        fstats = sba("fstats", [128, 6 * nq + 4], F32)
        S = 6 * nq

        sR1 = [self.sem(f"fr1{j}") for j in range(2)]
        sR2 = [self.sem(f"fr2{j}") for j in range(2)]
        sOut = [self.sem(f"out{j}") for j in range(2)]

        NT = TU // 128
        r1_done = [None, None]; r2_done = [None, None]
        r1_free = [0, 0]; r2_free = [0, 0]
        ot_free = [None, None]
        ftile_free = 0

        for tt in range(NT):
            slot = tt % 2
            r0 = u * TU + tt * 128
            w = [(sv, r1_free[slot])] if r1_free[slot] else []
            r1_done[slot] = self.dma("sync", r1l[slot][:], res1[r0:r0 + 128, :],
                                     sR1[slot], waits=w)
            w = [(sv, r2_free[slot])] if r2_free[slot] else []
            r2_done[slot] = self.dma("sync", r2l[slot][:], res2[r0:r0 + 128, :],
                                     sR2[slot], waits=w)

            for q in range(nq):
                w = []
                if q == 0:
                    w.append(r1_done[slot])
                    w.append(r2_done[slot])
                    if ftile_free:
                        w.append((sv, ftile_free))
                self.emit("vector",
                          lambda e, q=q, sl=slot:
                          e.tensor_tensor(ftile[:, q * 512:(q + 1) * 512],
                                          r1l[sl][:, q * 512:(q + 1) * 512],
                                          r2l[sl][:, q * 512:(q + 1) * 512], ALU.add),
                          waits=w, inc=sv if q == nq - 1 else None)
            r1_free[slot] = sv.n
            r2_free[slot] = sv.n
            v_f = sv.n

            for g0 in range(0, EC, 4):
                gn = min(4, EC - g0)
                for j in range(gn):
                    m = g0 + j
                    w = []
                    if j == 0:
                        w.append((sv, v_routed))
                        if psT_free:
                            w.append(psT_free)
                    self.tr(psT[:, j * 128:(j + 1) * 128],
                            routed3[:, m, tt * 128:(tt + 1) * 128],
                            identf_s[:], waits=w, inc=sp if j == gn - 1 else None)
                p_t = sp.n
                self.emit("vector",
                          lambda e, g0=g0, gn=gn:
                          e.tensor_tensor(ftile[:, g0 * 128:(g0 + gn) * 128],
                                          ftile[:, g0 * 128:(g0 + gn) * 128],
                                          psT[:, 0:gn * 128], ALU.add),
                          waits=[(sp, p_t), (sv, v_f)], inc=sv)
                psT_free = (sv, sv.n)
            v_sum = sv.n

            for q in range(nq):
                self.emit("vector",
                          lambda e, q=q: e.bn_stats(fstats[:, q * 6:(q + 1) * 6],
                                                    ftile[:, q * 512:(q + 1) * 512]),
                          waits=[(sv, v_sum)] if q == 0 else ())
            self.emit("vector", lambda e: e.bn_aggr(fstats[:, S:S + 2], fstats[:, 0:S]),
                      inc=sv)
            v_mv = sv.n
            self.emit("scalar", lambda e: e.activation(fstats[:, S + 2:S + 3],
                                                       fstats[:, S + 1:S + 2], AF.Ln,
                                                       bias=self.epsv[:, 0:1]),
                      waits=[(sv, v_mv)])
            self.emit("scalar", lambda e: e.activation(fstats[:, S + 3:S + 4],
                                                       fstats[:, S + 2:S + 3], AF.Exp,
                                                       scale=-0.5),
                      inc=sa)
            a_r = sa.n
            w = [(sa, a_r)]
            if ot_free[slot]:
                w.append(ot_free[slot])
            self.emit("vector",
                      lambda e, sl=slot: e.tensor_scalar(ot[sl][:], ftile[:],
                                                         fstats[:, S:S + 1],
                                                         fstats[:, S + 3:S + 4],
                                                         ALU.subtract, ALU.mult),
                      waits=w, inc=sv)
            v_ot = sv.n
            ftile_free = sv.n
            ot_free[slot] = self.dma("sync", out[r0:r0 + 128, :], ot[slot][:],
                                     sOut[slot], waits=[(sv, v_ot)])

        w0 = (sOut[0].h, sOut[0].n)
        w1 = (sOut[1].h, sOut[1].n)
        def fwait(e, w0=w0, w1=w1):
            if w0[1]:
                e.wait_ge(w0[0], w0[1])
            if w1[1]:
                e.wait_ge(w1[0], w1[1])
        self.acts["gpsimd"].append(fwait)

        self.barrier()
        for cm in reversed(cms):
            cm.__exit__(None, None, None)
        return psT_free


def make_consts(cfg):
    return dict(identb=np.eye(128, dtype=ml_dtypes.bfloat16),
                identf=np.eye(128, dtype=np.float32),
                cmask=np.triu(np.full((128, 128), NEG_BIG, np.float32), k=1))


def _tile_rows(w):
    R, C = w.shape
    return np.ascontiguousarray(
        w.reshape(R // 128, 128, C // 128, 128).transpose(0, 3, 2, 1).reshape(R // 128, 128, C))


def _tile_x(x):
    T, E = x.shape
    return np.ascontiguousarray(x.reshape(T, E // 128, 128).transpose(2, 1, 0).reshape(128, -1))


def prep_weights(inputs):
    c = REAL_CFG
    NH, QHD, NOPE, VD, NE, I1 = c["NH"], c["QHD"], c["NOPE"], c["VD"], c["NE"], c["I1"]
    qw = np.asarray(inputs["L0_ffn_q_w"])
    kvw = np.asarray(inputs["L0_ffn_kv_w"])
    qn = np.ascontiguousarray(qw.reshape(NH, QHD, -1)[:, :NOPE].reshape(NH * NOPE, -1))
    kw = np.ascontiguousarray(kvw[:NH * NOPE])
    vw = np.ascontiguousarray(kvw[NH * NOPE:NH * (NOPE + VD)])
    w1 = np.asarray(inputs["L0_ffn_w1"])
    w2 = np.asarray(inputs["L0_ffn_w2"])
    shdn = np.asarray(inputs["L0_ffn_sh_down_w"])
    gw = np.asarray(inputs["L0_ffn_gate_w"])
    E = 2048
    d = dict(
        wq_t=_tile_rows(np.asarray(inputs["L0_attn_Wq"])),
        wk_t=_tile_rows(np.asarray(inputs["L0_attn_Wk"])),
        wv_t=_tile_rows(np.asarray(inputs["L0_attn_Wv"])),
        wo_t=_tile_rows(np.asarray(inputs["L0_attn_Wo"])),
        qn_t=_tile_rows(qn), kw_t=_tile_rows(kw), vw_t=_tile_rows(vw),
        ow_t=_tile_rows(np.asarray(inputs["L0_ffn_o_w"])),
        gw_t=np.ascontiguousarray(gw.reshape(NE, E // 128, 128).transpose(2, 1, 0).reshape(128, -1)),
        w1_t=np.stack([_tile_rows(w1[e]) for e in range(NE)]),
        w2_t=np.stack([_tile_rows(w2[e]) for e in range(NE)]),
        shgu_t=_tile_rows(np.asarray(inputs["L0_ffn_sh_gu_w"])),
        shdn_t=np.stack([_tile_rows(np.ascontiguousarray(shdn[:, h * I1:(h + 1) * I1]))
                         for h in range(2)]),
    )
    return d


_BUILD_CACHE = {}


def build_nc(cfg=None, key="real"):
    if key in _BUILD_CACHE:
        return _BUILD_CACHE[key]
    nc = KB(cfg or REAL_CFG).build()
    _BUILD_CACHE[key] = nc
    return nc


def kernel(**inputs):
    _ = inputs.get("L0_ffn_positions")
    nc = build_nc()
    consts = make_consts(REAL_CFG)
    base = dict(
        lq1=inputs["L0_attn_lambda_q1"], lk1=inputs["L0_attn_lambda_k1"],
        lq2=inputs["L0_attn_lambda_q2"], lk2=inputs["L0_attn_lambda_k2"],
        subw=inputs["L0_attn_subln_weight"],
        ln1w=inputs["L0_ffn_ln1_w"], ln2w=inputs["L0_ffn_ln2_w"],
        **prep_weights(inputs), **consts)
    x = inputs["L0_attn_query"]
    in_maps = [dict(base, xq=np.ascontiguousarray(x[b]), xt=_tile_x(x[b]))
               for b in range(x.shape[0])]
    res = run_bass_kernel_spmd(nc, in_maps, core_ids=list(range(len(in_maps))))
    return np.stack([res.results[b]["out"] for b in range(len(in_maps))], axis=0)


# version 18
# speedup vs baseline: 1.0331x; 1.0252x over previous
import math
import numpy as np
import ml_dtypes
import concourse.bass as bass
import concourse.mybir as mybir
from concourse.bass_utils import run_bass_kernel_spmd

F32 = mybir.dt.float32
BF16 = mybir.dt.bfloat16
AF = mybir.ActivationFunctionType
ALU = mybir.AluOpType
AX = mybir.AxisListType

ENGINES = ("tensor", "vector", "scalar", "gpsimd", "sync")

REAL_CFG = dict(T=2048, E=2048, NH=16, HHD=64, VD=128, QHD=192, NOPE=128,
                NE=16, I1=1408, TOPK=4, TU=1024, TAU=256)

NEG_BIG = -1.0e30


class Sem:
    def __init__(self, h):
        self.h = h
        self.n = 0


class KB:
    def __init__(self, cfg):
        self.cfg = dict(cfg)
        c = self.cfg
        c["EC"] = c["E"] // 128
        c["TT"] = c["T"] // 128
        c["IC"] = c["I1"] // 128
        c["SHI"] = 2 * c["I1"]
        c["NU"] = c["T"] // c["TU"]
        c["NTAU"] = c["T"] // c["TAU"]
        c["LI"] = 0.8 - 0.6 * math.exp(-0.3 * 12)
        assert c["E"] == c["NH"] * 2 * c["HHD"] == c["NH"] * c["VD"]
        assert c["T"] % 512 == 0 and c["TU"] % 512 == 0 and c["TAU"] % 128 == 0
        self.acts = {e: [] for e in ENGINES}
        self._exits = []
        self.S = {}
        self.cast_thr = {}
        self.psA_evac = None
        self.ptr_free = None
        self.ptr2_free = None
        self.psO_free = 0
        self.v_ones = 0
        self.wfull_done = None

    def sem(self, name):
        if name not in self.S:
            cm = self.nc.semaphore(name)
            h = cm.__enter__()
            self._exits.append(cm)
            self.S[name] = Sem(h)
        return self.S[name]

    def emit(self, eng, fn, waits=(), inc=None, k=1):
        waits = list(waits)
        if eng == "vector":
            s = self.S["sv"]
            waits.append((s, s.n))
            assert inc is None or inc is s
            inc, k = s, 1
        elif eng == "scalar":
            s = self.S["sa"]
            waits.append((s, s.n))
            assert inc is None or inc is s
            inc, k = s, 1
        wl = [(s.h, v) for (s, v) in waits if v]
        tgt = None
        if inc is not None:
            inc.n += k
            tgt = inc.n

        def run(e, wl=wl, fn=fn, inc=inc, k=k):
            for h, v in wl:
                e.wait_ge(h, v)
            r = fn(e)
            if inc is not None:
                r.then_inc(inc.h, k)
        self.acts[eng].append(run)
        return tgt

    def mm(self, out, lhsT, rhs, start, stop, waits=(), inc=None):
        return self.emit(
            "tensor",
            lambda e, o=out, l=lhsT, r=rhs, s=start, p=stop:
                e.matmul(o, l, r, start=s, stop=p),
            waits, inc)

    def tr(self, out, in_, ident, waits=(), inc=None):
        return self.emit(
            "tensor",
            lambda e, o=out, i=in_, d=ident: e.transpose(o, i, d),
            waits, inc)

    def dma(self, eng, out, in_, sem, waits=(), transpose=False, **kw):
        def fn(e, o=out, i=in_, t=transpose, kw=dict(kw)):
            return e.dma_start(o, i, transpose=t, **kw)
        self.emit(eng, fn, waits, inc=sem, k=16)
        return (sem, sem.n)

    def barrier(self, exclude=()):
        waits = tuple((s.h, s.n) for nm, s in self.S.items()
                      if s.n > 0 and nm not in exclude)
        for eng in ENGINES:
            def fn(e, wl=waits):
                for h, v in wl:
                    e.wait_ge(h, v)
            self.acts[eng].append(fn)

    def alloc_sbuf(self, name, shape, dt):
        cm = self.nc.sbuf_tensor(name, shape, dt)
        return cm.__enter__(), cm

    def alloc_psum(self, name, shape, dt):
        cm = self.nc.psum_tensor(name, shape, dt)
        return cm.__enter__(), cm

    def build(self):
        c = self.cfg
        T, E, EC = c["T"], c["E"], c["EC"]
        NH, HHD, VD, QHD, NOPE = c["NH"], c["HHD"], c["VD"], c["QHD"], c["NOPE"]
        NE, I1, SHI = c["NE"], c["I1"], c["SHI"]

        IC = c["IC"]
        nc = bass.Bass("TRN2", target_bir_lowering=False)
        self.nc = nc
        dram = lambda n, sh, dt=F32, kind="ExternalInput": nc.dram_tensor(n, sh, dt, kind=kind).ap()

        xq = dram("xq", [T, E])
        xt = dram("xt", [128, EC * T])
        wq_t = dram("wq_t", [NH, 128, E]); wk_t = dram("wk_t", [NH, 128, E])
        wv_t = dram("wv_t", [NH, 128, E]); wo_t = dram("wo_t", [EC, 128, E])
        lq1 = dram("lq1", [HHD]); lk1 = dram("lk1", [HHD])
        lq2 = dram("lq2", [HHD]); lk2 = dram("lk2", [HHD])
        subw = dram("subw", [2 * HHD])
        ln1w = dram("ln1w", [E]); ln2w = dram("ln2w", [E])
        qn_t = dram("qn_t", [NH, 128, E]); kw_t = dram("kw_t", [NH, 128, E])
        vw_t = dram("vw_t", [NH, 128, E]); ow_t = dram("ow_t", [EC, 128, E])
        gw_t = dram("gw_t", [128, EC * NE])
        w1_t = dram("w1_t", [NE, 2 * IC, 128, E])
        w2_t = dram("w2_t", [NE, EC, 128, I1])
        shgu_t = dram("shgu_t", [4 * IC, 128, E])
        shdn_t = dram("shdn_t", [2, EC, 128, I1])
        identb = dram("identb", [128, 128], BF16)
        identf = dram("identf", [128, 128], F32)
        cmask = dram("cmask", [128, 128], F32)
        out = dram("out", [T, E], F32, kind="ExternalOutput")

        idram = lambda n, sh, dt: nc.dram_tensor(n, sh, dt, kind="Internal").ap()
        wq_bf = idram("wq_bf", [NH, 128, E], BF16)
        wk_bf = idram("wk_bf", [NH, 128, E], BF16)
        wv_bf = idram("wv_bf", [NH, 128, E], BF16)
        wo_bf = idram("wo_bf", [EC, 128, E], BF16)
        qn_bf = idram("qn_bf", [NH, 128, E], BF16)
        kw_bf = idram("kw_bf", [NH, 128, E], BF16)
        vw_bf = idram("vw_bf", [NH, 128, E], BF16)
        ow_bf = idram("ow_bf", [EC, 128, E], BF16)
        gw_bf = idram("gw_bf", [128, EC * NE], BF16)
        w1_bf = idram("w1_bf", [NE, 2 * IC, 128, E], BF16)
        w2_bf = idram("w2_bf", [NE, EC, 128, I1], BF16)
        shgu_bf = idram("shgu_bf", [4 * IC, 128, E], BF16)
        shdn_bf = idram("shdn_bf", [2, EC, 128, I1], BF16)
        res1 = idram("res1", [T, E], F32)
        res2 = idram("res2", [T, E], F32)
        wfull_d = idram("wfull_d", [NE, T], F32)

        sp = self.sem("sp"); sv = self.sem("sv"); sa = self.sem("sa")

        sb = self.alloc_sbuf
        xT, _ = sb("xT", [128, EC * T], BF16)
        identb_s, _ = sb("identb_s", [128, 128], BF16)
        identf_s, _ = sb("identf_s", [128, 128], F32)
        cmask_s, _ = sb("cmask_s", [128, 128], F32)
        onesb, _ = sb("onesb", [1, 128], BF16)
        onesf, _ = sb("onesf", [1, 128], F32)
        lam_bc, _ = sb("lam_bc", [128, 1], F32)
        g_bc, _ = sb("g_bc", [128, 2 * HHD], F32)
        ln1w_bf, _ = sb("ln1w_bf", [1, E], BF16)
        ln2w_bf, _ = sb("ln2w_bf", [1, E], BF16)
        lnv, _ = sb("lnv", [1, 6 * HHD], F32)
        epsv, _ = sb("epsv", [128, 4], F32)
        self.epsv = epsv
        lprod, _ = sb("lprod", [1, 2 * HHD], F32)
        sml, _ = sb("sml", [1, 16], F32)

        xT3 = xT.ap().rearrange("p (c t) -> p c t", c=EC)

        def flat8k(ap):
            sh = list(ap.shape)
            v = ap
            while len(v.shape) > 2:
                n = len(v.shape)
                names = " ".join(chr(97 + i) for i in range(n))
                tail = " ".join(chr(97 + i) for i in range(n - 2, n))
                head = " ".join(chr(97 + i) for i in range(n - 2))
                v = v.rearrange(f"{names} -> {head} ({tail})".strip())
            a, b = v.shape
            assert (a * b) % 8192 == 0, sh
            if b % 8192 == 0 and b > 8192:
                return v.rearrange("a (r q) -> (a r) q", q=8192)
            if b == 8192:
                return v
            return v.rearrange("a b -> (a b)").rearrange("(r q) -> r q", q=8192)

        s_xT = self.sem("c_xT")
        self.w_xT = self.dma(
            "gpsimd", xT.ap().rearrange("p (a q) -> p a q", q=8192),
            xt.rearrange("p (a q) -> p a q", q=8192), s_xT, max_dma_last_dim=8192)

        groups = dict(
            c_qkv=[("Wq", wq_bf, wq_t), ("Wk", wk_bf, wk_t), ("Wv", wv_bf, wv_t)],
            c_p2=[("Wo", wo_bf, wo_t), ("qw", qn_bf, qn_t), ("kw", kw_bf, kw_t),
                  ("vw", vw_bf, vw_t), ("ow", ow_bf, ow_t), ("gw", gw_bf, gw_t),
                  ("ln1w", ln1w_bf[:], ln1w.rearrange("(a e) -> a e", a=1)),
                  ("ln2w", ln2w_bf[:], ln2w.rearrange("(a e) -> a e", a=1))],
            c_moe1=[(f"w1_{e}", w1_bf[e], w1_t[e]) for e in range(NE // 2)] +
                   [(f"w2_{e}", w2_bf[e], w2_t[e]) for e in range(NE // 2)],
            c_moe2=[(f"w1_{e}", w1_bf[e], w1_t[e]) for e in range(NE // 2, NE)] +
                   [(f"w2_{e}", w2_bf[e], w2_t[e]) for e in range(NE // 2, NE)] +
                   [("shgu", shgu_bf, shgu_t), ("shdn", shdn_bf, shdn_t)],
        )
        for gname, items in groups.items():
            gs = self.sem(gname)
            for name, dst, src in items:
                if name in ("ln1w", "ln2w", "gw"):
                    self.dma("gpsimd", dst, src, gs, max_dma_last_dim=8192)
                else:
                    self.dma("gpsimd", flat8k(dst), flat8k(src), gs,
                             max_dma_last_dim=8192)
            for name, dst, src in items:
                self.cast_thr[name] = (gs, gs.n)

        r1f = lambda ap: ap.rearrange("(a h) -> a h", a=1)
        s_cn = self.sem("c_const")
        self.dma("sync", identb_s[:], identb, s_cn)
        self.dma("sync", identf_s[:], identf, s_cn)
        self.dma("sync", cmask_s[:], cmask, s_cn)
        self.dma("sync", lnv[:, 0:HHD], r1f(lq1), s_cn)
        self.dma("sync", lnv[:, HHD:2 * HHD], r1f(lk1), s_cn)
        self.dma("sync", lnv[:, 2 * HHD:3 * HHD], r1f(lq2), s_cn)
        self.dma("sync", lnv[:, 3 * HHD:4 * HHD], r1f(lk2), s_cn)
        self.dma("sync", lnv[:, 4 * HHD:6 * HHD], r1f(subw), s_cn)
        w_consts = (s_cn, s_cn.n)
        self.w_consts = w_consts

        self.emit("vector", lambda e: e.memset(onesb[:], 1.0))
        self.emit("vector", lambda e: e.memset(epsv[:, 0:1], 1e-5))
        self.emit("vector", lambda e: e.memset(epsv[:, 1:2], 1e-6))
        self.emit("vector", lambda e: e.memset(onesf[:], 1.0), inc=sv)
        self.v_ones = sv.n
        self.emit("vector", lambda e: e.tensor_tensor(lprod[:, 0:HHD], lnv[:, 0:HHD],
                                                      lnv[:, HHD:2 * HHD], ALU.mult),
                  waits=[w_consts])
        self.emit("vector", lambda e: e.tensor_tensor(lprod[:, HHD:2 * HHD], lnv[:, 2 * HHD:3 * HHD],
                                                      lnv[:, 3 * HHD:4 * HHD], ALU.mult))
        self.emit("vector", lambda e: e.tensor_reduce(sml[:, 0:1], lprod[:, 0:HHD], AX.X, ALU.add))
        self.emit("vector", lambda e: e.tensor_reduce(sml[:, 1:2], lprod[:, HHD:2 * HHD], AX.X, ALU.add),
                  inc=sv)
        v_ls = sv.n
        self.emit("scalar", lambda e: e.activation(sml[:, 2:3], sml[:, 0:1], AF.Exp),
                  waits=[(sv, v_ls)])
        self.emit("scalar", lambda e: e.activation(sml[:, 3:4], sml[:, 1:2], AF.Exp), inc=sa)
        a_le = sa.n
        self.emit("vector", lambda e: e.tensor_tensor(sml[:, 4:5], sml[:, 2:3], sml[:, 3:4], ALU.subtract),
                  waits=[(sa, a_le)])
        self.emit("vector", lambda e: e.tensor_scalar(sml[:, 5:6], sml[:, 4:5], c["LI"], None, ALU.add),
                  inc=sv)
        v_lam = sv.n

        psP, psP_cm = self.alloc_psum("psP", [128, 512], F32)
        self.mm(psP[:, 0:1], onesf[:], sml[:, 5:6], True, True,
                waits=[(sv, v_lam)], inc=sp)
        p_lam = sp.n
        self.emit("vector", lambda e: e.tensor_copy(lam_bc[:], psP[:, 0:1]),
                  waits=[(sp, p_lam)], inc=sv)
        v_lbc = sv.n
        self.mm(psP[:, 128:128 + 2 * HHD], onesf[:], lnv[:, 4 * HHD:6 * HHD], True, True,
                waits=[(sv, v_lbc)], inc=sp)
        p_g = sp.n
        self.emit("vector", lambda e: e.tensor_scalar(g_bc[:], psP[:, 128:128 + 2 * HHD],
                                                      1.0 - c["LI"], None, ALU.mult),
                  waits=[(sp, p_g)], inc=sv)
        psP_cm.__exit__(None, None, None)
        self.barrier(exclude=("c_xT", "c_qkv", "c_p2", "c_moe1", "c_moe2"))

        PH = c.get("PH", 6)
        aoT, aoT_cm = sb("aoT", [128, EC * T], BF16)
        aoT3 = aoT.ap().rearrange("p (c t) -> p c t", c=EC)

        if PH >= 1:
            self.attn_phase(True, xT3, aoT3, (wq_bf, wk_bf, wv_bf), ("Wq", "Wk", "Wv"),
                            HHD ** -0.5, identb_s, identf_s, cmask_s, g_bc, lam_bc)
        if PH >= 2:
            self.proj_residual_epilogue("p1", aoT3, wo_bf, "Wo", xq, res1,
                                        ln1w_bf, "ln1w", xT3, identb_s, identf_s, onesb)
        if PH >= 3:
            self.attn_phase(False, xT3, aoT3, (qn_bf, kw_bf, vw_bf), ("qw", "kw", "vw"),
                            QHD ** -0.5, identb_s, identf_s, cmask_s, None, None)
        if PH >= 4:
            self.proj_residual_epilogue("p2", aoT3, ow_bf, "ow", res1, res2,
                                        ln2w_bf, "ln2w", xT3, identb_s, identf_s, onesb)
        aoT_cm.__exit__(None, None, None)

        if PH >= 5:
            self.gate_phase(xT3, gw_bf, wfull_d, identf_s)

        if PH >= 6:
            self.moe_phase(xT3, w1_bf, w2_bf, shgu_bf, shdn_bf, wfull_d, res1, res2,
                           out, onesf, identf_s)


        with nc.Block() as block:
            for eng in ENGINES:
                acts = self.acts[eng]
                if not acts:
                    continue
                def mk(acts):
                    def body(e):
                        for a in acts:
                            a(e)
                    return body
                getattr(block, eng)(mk(acts))
        for cm in self._exits:
            cm.__exit__(None, None, None)
        return nc

    def attn_phase(self, is_diff, xT3, aoT3, w_srcs, cast_keys,
                   scale, identb_s, identf_s, cmask_s, g_bc, lam_bc):
        c = self.cfg
        T, EC, TT, NH = c["T"], c["EC"], c["TT"], c["NH"]
        sp, sv, sa = (self.S[k] for k in ("sp", "sv", "sa"))
        pre = "d" if is_diff else "m"
        cms = []
        def sba(name, shape, dt):
            h, cm = self.alloc_sbuf(pre + name, shape, dt)
            cms.append(cm)
            return h
        qT = sba("qT", [128, T], BF16)
        kT = sba("kT", [128, T], BF16)
        vfm = sba("vfm", [128, T], BF16)
        v_tm = sba("v_tm", [128, T], BF16)
        Pa = sba("Pa", [128, T], BF16)
        Pb = sba("Pb", [128, T], BF16)
        PT4 = [sba(f"PT{j}", [128, T], BF16) for j in range(4 if is_diff else 2)]
        wb = [[sba(f"w{w}b{j}", [128, EC * 128], BF16) for j in range(2)] for w in range(3)]
        stat = sba("stat", [128, 64], F32)
        Ocmb = sba("Ocmb", [128, 128], F32)
        Osqt = sba("Osqt", [128, 128], F32)
        On = [sba(f"On{j}", [128, 128], BF16) for j in range(2)]

        psS, psS_cm = self.alloc_psum(pre + "psS", [128, 2048], F32)
        psO, psO_cm = self.alloc_psum(pre + "psO", [128, 512], F32)
        ptrb, ptrb_cm = self.alloc_psum(pre + "ptrb", [128, 2048], BF16)
        ptr2, ptr2_cm = self.alloc_psum(pre + "ptr2", [128, 1024], BF16)
        pcms = [psS_cm, psO_cm, ptrb_cm, ptr2_cm]

        v_tm3 = v_tm.ap().rearrange("p (c d) -> p c d", c=TT)
        NB = T // 512

        wsem = [[self.sem(f"{pre}wl{w}{j}") for j in range(2)] for w in range(3)]
        wload = [[None, None] for _ in range(3)]
        wfree = [[0, 0] for _ in range(3)]

        def issue_load(h):
            slot = h % 2
            for widx in range(3):
                waits = [self.cast_thr[cast_keys[widx]]]
                if wfree[widx][slot]:
                    waits.append((sp, wfree[widx][slot]))
                wload[widx][slot] = self.dma(
                    "sync", wb[widx][slot][:], w_srcs[widx][h], wsem[widx][slot],
                    waits=waits)

        halves = ((0, 0), (1, 64)) if is_diff else ((0, 0),)
        kdim = 64 if is_diff else 128
        last_hl = halves[-1][0]
        ZB = lambda zp, hl, k: (zp * 2 + hl) * 4 + k
        SSQ, LNV = 60, 61
        RSTD = lambda par: 56 + par

        psS_free = [None, None]
        ptrb_free = [None, None]
        ptr2_free = [None]
        psO_free = [None, None]
        z_free = {}
        ptr_done = {}
        On_free = [None, None]
        ocmb_free = [None]
        rstd_free = [None, None]
        ckp = [0]
        tpp = [0]

        issue_load(0)
        for h in range(NH):
            slot = h % 2
            if h + 1 < NH:
                issue_load(h + 1)

            evs = []
            proj_evac = [None] * NB
            for widx, dst in ((0, qT), (1, kT), (2, vfm)):
                wb3 = wb[widx][slot].ap().rearrange("p (c r) -> p c r", c=EC)
                for kc in range(EC):
                    for nb in range(NB):
                        w = []
                        if kc == 0:
                            if nb == 0:
                                w.append(wload[widx][slot])
                                if widx == 0:
                                    w.append(self.w_xT)
                                    for f in psS_free:
                                        if f:
                                            w.append(f)
                            if proj_evac[nb]:
                                w.append(proj_evac[nb])
                        self.mm(psS[:, nb * 512:(nb + 1) * 512], wb3[:, kc, :],
                                xT3[:, kc, nb * 512:(nb + 1) * 512],
                                start=(kc == 0), stop=(kc == EC - 1),
                                waits=w,
                                inc=sp if kc == EC - 1 else None)
                p_all = sp.n
                a_last = 0
                for nb in range(NB):
                    if nb & 1:
                        w = [(sp, p_all - NB + 1 + nb)]
                        if nb == NB - 1 and a_last:
                            w.append((sa, a_last))
                        self.emit("vector",
                                  lambda e, d=dst, nb=nb:
                                  e.tensor_copy(d[:, nb * 512:(nb + 1) * 512],
                                                psS[:, nb * 512:(nb + 1) * 512]),
                                  waits=w, inc=sv)
                        proj_evac[nb] = (sv, sv.n)
                    else:
                        self.emit("scalar",
                                  lambda e, d=dst, nb=nb:
                                  e.copy(d[:, nb * 512:(nb + 1) * 512],
                                         psS[:, nb * 512:(nb + 1) * 512]),
                                  waits=[(sp, p_all - NB + 1 + nb)], inc=sa)
                        proj_evac[nb] = (sa, sa.n)
                        a_last = sa.n
                wfree[widx][slot] = sp.n
                evs.append(sv.n)
            v_qk = evs[1]
            v_vfm = evs[2]
            psS_free = [(sv, evs[2]), (sv, evs[2])]

            for grp in range((TT + 7) // 8):
                tiles = list(range(grp * 8, min(TT, grp * 8 + 8)))
                for j, ct in enumerate(tiles):
                    w = []
                    if j == 0:
                        w.append((sv, v_vfm))
                        if ptr2_free[0]:
                            w.append(ptr2_free[0])
                    self.tr(ptr2[:, j * 128:(j + 1) * 128], vfm[:, ct * 128:(ct + 1) * 128],
                            identb_s[:], waits=w, inc=sp if j == len(tiles) - 1 else None)
                p_tr = sp.n
                g0, gn = tiles[0], len(tiles)
                self.emit("scalar",
                          lambda e, g0=g0, gn=gn:
                          e.copy(v_tm[:, g0 * 128:(g0 + gn) * 128], ptr2[:, 0:gn * 128]),
                          waits=[(sp, p_tr)], inc=sa)
                ptr2_free[0] = (sa, sa.n)
            v_vtm = (sa, sa.n)

            def emit_S(b):
                ti, pbase = b["ti"], b["pbase"]
                L = 128 * (ti + 1)
                b["chunks"] = []
                for c0 in range(0, L, 1024):
                    c1 = min(L, c0 + 1024)
                    p = ckp[0] & 1
                    ckp[0] += 1
                    base = p * 1024
                    diag = c1 == L
                    first = True

                    def fw():
                        nonlocal first
                        w = []
                        if first:
                            first = False
                            if psS_free[p]:
                                w.append(psS_free[p])
                            if ti == 0:
                                w.append((sv, v_qk))
                        return w
                    for n0 in range(c0, c1, 512):
                        n1 = min(c1, n0 + 512)
                        self.mm(psS[:, base + n0 - c0:base + n1 - c0],
                                qT[pbase:pbase + kdim, ti * 128:(ti + 1) * 128],
                                kT[pbase:pbase + kdim, n0:n1],
                                True, True, waits=fw(),
                                inc=sp if n1 == c1 else None)
                    p_sc = sp.n
                    v_mask = 0
                    if diag:
                        off = base + ti * 128 - c0
                        self.emit("vector",
                                  lambda e, off=off: e.tensor_tensor(
                                      psS[:, off:off + 128], psS[:, off:off + 128],
                                      cmask_s[:], ALU.add),
                                  waits=[(sp, p_sc)])
                        v_mask = sv.n
                    b["chunks"].append(dict(p=p, base=base, c0=c0, c1=c1,
                                            p_sc=p_sc, v_mask=v_mask))

            def emit_exp(b):
                hl, zp, P_sb = b["hl"], b["zp"], b["P"]
                L = 128 * (b["ti"] + 1)
                parts = []
                for ch in b["chunks"]:
                    e1 = min(ch["c1"], L - 128)
                    if e1 > ch["c0"]:
                        parts.append((ch, ch["c0"], e1, 0))
                dch = b["chunks"][-1]
                parts.append((dch, L - 128, L, dch["v_mask"]))
                b["exp"] = []
                nz = 0
                for ci, (ch, e0, e1, vm) in enumerate(parts):
                    w = [(sp, ch["p_sc"])]
                    if vm:
                        w.append((sv, vm))
                    if ci == 0:
                        if z_free.get((zp, hl)):
                            w.append((sv, z_free[(zp, hl)]))
                        if ptr_done.get(b["pk"]):
                            w.append((sp, ptr_done[b["pk"]]))
                    zc = ZB(zp, hl, nz)
                    nz += 1
                    self.emit("scalar",
                              lambda e, P=P_sb, ch=ch, e0=e0, e1=e1, zc=zc, sk=scale:
                              e.activation(P[:, e0:e1],
                                           psS[:, ch["base"] + e0 - ch["c0"]:
                                               ch["base"] + e1 - ch["c0"]],
                                           AF.Exp, scale=sk,
                                           accum_out=stat[:, zc:zc + 1]),
                              waits=w, inc=sa)
                    b["exp"].append(sa.n)
                    psS_free[ch["p"]] = (sa, sa.n)
                b["nz"] = nz

            def emit_T(b):
                ti = b["ti"]
                P_sb, PT_sb = b["P"], b["PT"]
                nt = ti + 1
                b["evacs"] = []
                for g0 in range(0, nt, 8):
                    gn = min(8, nt - g0)
                    tp = tpp[0] & 1
                    tpp[0] += 1
                    tbase = tp * 1024
                    glast = (nt - 1) // 8
                    for j in range(gn):
                        ct = g0 + j
                        w = []
                        if j == 0:
                            ei = -1 if g0 // 8 == glast else g0 // 8
                            w.append((sa, b["exp"][ei]))
                            if ptrb_free[tp]:
                                w.append(ptrb_free[tp])
                        self.tr(ptrb[:, tbase + j * 128:tbase + (j + 1) * 128],
                                P_sb[:, ct * 128:(ct + 1) * 128], identb_s[:],
                                waits=w, inc=sp if j == gn - 1 else None)
                    p_tr = sp.n
                    self.emit("vector",
                              lambda e, PT=PT_sb, g0=g0, gn=gn, tbase=tbase:
                              e.tensor_copy(PT[:, g0 * 128:(g0 + gn) * 128],
                                            ptrb[:, tbase:tbase + gn * 128]),
                              waits=[(sp, p_tr)], inc=sv)
                    ptrb_free[tp] = (sv, sv.n)
                    b["evacs"].append((sv, sv.n))
                ptr_done[b["pk"]] = sp.n

            def emit_AV(b):
                ti, hl, par = b["ti"], b["hl"], b["par"]
                PT_sb = b["PT"]
                nt = ti + 1
                ob = par * 256 + hl * 128
                for ct in range(nt):
                    w = []
                    if ct % 8 == 0:
                        w.append(b["evacs"][ct // 8])
                    if ct == 0:
                        w.append(v_vtm)
                        if psO_free[par]:
                            w.append(psO_free[par])
                    self.mm(psO[:, ob:ob + 128],
                            PT_sb[:, ct * 128:(ct + 1) * 128], v_tm3[:, ct, :],
                            start=(ct == 0), stop=(ct == nt - 1),
                            waits=w, inc=sp if ct == nt - 1 else None)
                b["av"] = sp.n

            def emit_combine(tb):
                ti, par, zp = tb[0]["ti"], tb[0]["par"], tb[0]["zp"]
                wz = [(sa, tb[-1]["exp"][-1])]
                for b in tb:
                    z0 = ZB(zp, b["hl"], 0)
                    for k in range(1, b["nz"]):
                        self.emit("vector",
                                  lambda e, z0=z0, zk=ZB(zp, b["hl"], k):
                                  e.tensor_tensor(stat[:, z0:z0 + 1],
                                                  stat[:, z0:z0 + 1],
                                                  stat[:, zk:zk + 1], ALU.add),
                                  waits=wz)
                        wz = []
                    self.emit("vector",
                              lambda e, z0=z0: e.reciprocal(stat[:, z0 + 2:z0 + 3],
                                                            stat[:, z0:z0 + 1]),
                              waits=wz)
                    wz = []
                    z_free[(zp, b["hl"])] = sv.n
                ob = par * 256
                if is_diff:
                    r1 = ZB(zp, 0, 2)
                    r2 = ZB(zp, 1, 2)
                    cf = ZB(zp, 1, 3)
                    self.emit("vector",
                              lambda e, r2=r2, cf=cf: e.tensor_scalar(
                                  stat[:, cf:cf + 1], stat[:, r2:r2 + 1],
                                  lam_bc[:], -1.0, ALU.mult, ALU.mult))
                    w = [(sp, tb[-1]["av"])]
                    if ocmb_free[0]:
                        w.append(ocmb_free[0])
                    self.emit("vector",
                              lambda e, ob=ob, r1=r1: e.tensor_scalar(
                                  Ocmb[:], psO[:, ob:ob + 128],
                                  stat[:, r1:r1 + 1], None, ALU.mult),
                              waits=w)
                    self.emit("vector",
                              lambda e, ob=ob, cf=cf: e.scalar_tensor_tensor(
                                  Ocmb[:], psO[:, ob + 128:ob + 256],
                                  stat[:, cf:cf + 1], Ocmb[:], ALU.mult, ALU.add),
                              inc=sv)
                    psO_free[par] = (sv, sv.n)
                    v_oc = sv.n
                    self.emit("scalar",
                              lambda e: e.activation(Osqt[:], Ocmb[:], AF.Square,
                                                     accum_out=stat[:, SSQ:SSQ + 1]),
                              waits=[(sv, v_oc)])
                    self.emit("scalar",
                              lambda e: e.activation(stat[:, LNV:LNV + 1],
                                                     stat[:, SSQ:SSQ + 1], AF.Ln,
                                                     bias=self.epsv[:, 0:1],
                                                     scale=1.0 / 128.0))
                    w = [rstd_free[par]] if rstd_free[par] else []
                    self.emit("scalar",
                              lambda e, par=par: e.activation(
                                  stat[:, RSTD(par):RSTD(par) + 1],
                                  stat[:, LNV:LNV + 1], AF.Exp, scale=-0.5),
                              waits=w, inc=sa)
                    a_rstd = sa.n
                    ocmb_free[0] = (sa, a_rstd)
                    w = [(sa, a_rstd)]
                    if On_free[par]:
                        w.append(On_free[par])
                    self.emit("vector",
                              lambda e, par=par: e.scalar_tensor_tensor(
                                  On[par][:], Ocmb[:], stat[:, RSTD(par):RSTD(par) + 1],
                                  g_bc[:], ALU.mult, ALU.mult),
                              waits=w, inc=sv)
                    rstd_free[par] = (sv, sv.n)
                else:
                    r1 = ZB(zp, 0, 2)
                    w = [(sp, tb[0]["av"])]
                    if On_free[par]:
                        w.append(On_free[par])
                    self.emit("vector",
                              lambda e, ob=ob, r1=r1, par=par: e.tensor_scalar(
                                  On[par][:], psO[:, ob:ob + 128],
                                  stat[:, r1:r1 + 1], None, ALU.mult),
                              waits=w, inc=sv)
                    psO_free[par] = (sv, sv.n)
                v_on = sv.n
                w = [(sv, v_on)]
                if ptr2_free[0]:
                    w.append(ptr2_free[0])
                self.tr(ptr2[:, 0:128], On[par][:], identb_s[:], waits=w, inc=sp)
                p_ot = sp.n
                On_free[par] = (sp, p_ot)
                self.emit("scalar",
                          lambda e, h=h, ti=ti: e.copy(
                              aoT3[:, h, ti * 128:(ti + 1) * 128], ptr2[:, 0:128]),
                          waits=[(sp, p_ot)], inc=sa)
                ptr2_free[0] = (sa, sa.n)

            pend = []
            avq = []
            tile_blocks = []

            def flush_T():
                b0 = pend.pop(0)
                emit_T(b0)
                avq.append(b0)

            def flush_AV():
                b0 = avq.pop(0)
                emit_AV(b0)
                tile_blocks.append(b0)
                if b0["hl"] == last_hl:
                    emit_combine(tile_blocks)
                    tile_blocks.clear()

            for ti in range(TT):
                for (hl, pbase) in halves:
                    if is_diff:
                        P_sb = (Pa, Pb)[hl]
                        PT_sb = PT4[hl * 2 + (ti & 1)]
                    else:
                        P_sb = (Pa, Pb)[ti & 1]
                        PT_sb = PT4[ti & 1]
                    b = dict(ti=ti, hl=hl, pbase=pbase, P=P_sb, PT=PT_sb, par=ti & 1,
                             zp=ti % 3, pk=(hl if is_diff else ti & 1))
                    emit_S(b)
                    while len(pend) >= 2:
                        flush_T()
                    while len(avq) >= 1:
                        flush_AV()
                    emit_exp(b)
                    pend.append(b)
            while pend:
                flush_T()
            while avq:
                flush_AV()

        self.barrier()
        for cm in reversed(cms + pcms):
            cm.__exit__(None, None, None)

    def proj_residual_epilogue(self, mode, aoT3, w_bf, cast_key, resid_src, spill,
                               lnw_bf, lnw_key, xT3, identb_s, identf_s, onesb):
        c = self.cfg
        T, E, EC, TAU, NTAU = c["T"], c["E"], c["EC"], c["TAU"], c["NTAU"]
        sp, sv, sa = (self.S[k] for k in ("sp", "sv", "sa"))
        cms = []
        def sba(name, shape, dt):
            h, cm = self.alloc_sbuf(mode + name, shape, dt)
            cms.append(cm)
            return h
        wob = [sba(f"wob{j}", [128, EC * 128], BF16) for j in range(3)]
        tmpy = [sba(f"tmpy{j}", [128, 512], F32) for j in range(2)]
        h1blk = sba("h1blk", [128, 2 * E], F32)
        xload = [sba(f"xload{j}", [128, E], F32) for j in range(2)]
        h1n = sba("h1n", [128, E], F32)
        xlnt = sba("xlnt", [128, E], BF16)
        nq = E // 512
        stats = sba("stats", [128, 6 * nq + 8], F32)
        lnwbc = sba("lnwbc", [128, E], BF16)
        psA, psA_cm = self.alloc_psum(mode + "psA", [128, 2048], F32)
        psT, psT_cm = self.alloc_psum(mode + "psT", [128, 1024], F32)
        ptrb, ptrb_cm = self.alloc_psum(mode + "ptrb", [128, 1024], BF16)
        pcms = [psA_cm, psT_cm, ptrb_cm]
        S = 6 * nq
        NT2 = TAU // 128
        psT_free = [None, None]
        ptrb_free = None

        wosem = [self.sem(f"{mode}wo{j}") for j in range(3)]
        xlsem = [self.sem(f"{mode}xl{j}") for j in range(2)]
        spsem = [self.sem(f"{mode}sp{j}") for j in range(2)]

        for nb in range(nq):
            w = []
            if nb == 0:
                w.append(self.cast_thr[lnw_key])
                w.append((sv, self.v_ones))
            self.mm(psA[:, nb * 512:(nb + 1) * 512], onesb[:],
                    lnw_bf[:, nb * 512:(nb + 1) * 512], True, True,
                    waits=w, inc=sp if nb == nq - 1 else None)
        p_lnw = sp.n
        for nb in range(nq):
            self.emit("vector",
                      lambda e, nb=nb: e.tensor_copy(lnwbc[:, nb * 512:(nb + 1) * 512],
                                                     psA[:, nb * 512:(nb + 1) * 512]),
                      waits=[(sp, p_lnw)] if nb == 0 else (),
                      inc=sv if nb == nq - 1 else None)
        psA_free = (sv, sv.n)

        wo_done = [None] * 3
        wo_free = [0] * 3
        xl_free = [0, 0]
        xl_done = [None, None]
        tmpy_free = [0, 0]
        h1n_free = None
        hb_free = [None, None]
        xlnt_free = 0

        pairs = [(tau, m) for tau in range(NTAU) for m in range(EC)]

        def load_wo(idx):
            tau, m = pairs[idx]
            slot = idx % 3
            waits = [self.cast_thr[cast_key]]
            if wo_free[slot]:
                waits.append((sp, wo_free[slot]))
            wo_done[slot] = self.dma(
                "sync", wob[slot][:], w_bf[m], wosem[slot], waits=waits)

        load_wo(0)
        load_wo(1)
        for tau in range(NTAU):
            for tt in range(NT2):
                slot = tt % 2
                r0 = tau * TAU + tt * 128
                w = [(sv, xl_free[slot])] if xl_free[slot] else []
                xl_done[slot] = self.dma("sync", xload[slot][:],
                                         resid_src[r0:r0 + 128, :], xlsem[slot], waits=w)

            for pi in range(EC // 2):
                m0 = 2 * pi
                m1 = m0 + 1
                idx0 = tau * EC + m0
                s0, s1 = idx0 % 3, (idx0 + 1) % 3
                wb30 = wob[s0].ap().rearrange("p (c r) -> p c r", c=EC)
                wb31 = wob[s1].ap().rearrange("p (c r) -> p c r", c=EC)
                t0 = tau * TAU
                for kc in range(EC):
                    w0, w1 = [], []
                    if kc == 0:
                        w0.append(wo_done[s0])
                        w1.append(wo_done[s1])
                        if psA_free:
                            w0.append(psA_free)
                    self.mm(psA[:, 0:TAU], wb30[:, kc, :], aoT3[:, kc, t0:t0 + TAU],
                            start=(kc == 0), stop=(kc == EC - 1),
                            waits=w0, inc=sp if kc == EC - 1 else None)
                    self.mm(psA[:, 512:512 + TAU], wb31[:, kc, :], aoT3[:, kc, t0:t0 + TAU],
                            start=(kc == 0), stop=(kc == EC - 1),
                            waits=w1, inc=sp if kc == EC - 1 else None)
                p_y1 = sp.n
                wo_free[s0] = p_y1 - 1
                wo_free[s1] = p_y1
                for pf in (idx0 + 2, idx0 + 3):
                    if pf < len(pairs):
                        load_wo(pf)
                for mi, (pv, off) in enumerate(((p_y1 - 1, 0), (p_y1, 512))):
                    w = [(sp, pv)]
                    if tmpy_free[mi]:
                        w.append((sv, tmpy_free[mi]))
                    self.emit("vector",
                              lambda e, sl=mi, off=off: e.tensor_copy(
                                  tmpy[sl][:, 0:TAU], psA[:, off:off + TAU]),
                              waits=w, inc=sv)
                v_y = sv.n
                psA_free = (sv, v_y)
                for tt in range(NT2):
                    for mi, m in enumerate((m0, m1)):
                        tp = (tt * 2 + mi) & 1
                        wtr = [(sv, v_y)] if (tt == 0 and mi == 0) else []
                        if psT_free[tp]:
                            wtr.append(psT_free[tp])
                        self.tr(psT[:, tp * 512:tp * 512 + 128],
                                tmpy[mi][:, tt * 128:(tt + 1) * 128],
                                identf_s[:], waits=wtr, inc=sp)
                        p_tr = sp.n
                        wadd = [(sp, p_tr), xl_done[tt % 2]]
                        if hb_free[tt % 2]:
                            wadd.append(hb_free[tt % 2])
                        self.emit("vector",
                                  lambda e, tt=tt, m=m, sl=tt % 2, tp=tp:
                                  e.tensor_tensor(h1blk[:, tt * E + m * 128:tt * E + (m + 1) * 128],
                                                  psT[:, tp * 512:tp * 512 + 128],
                                                  xload[sl][:, m * 128:(m + 1) * 128],
                                                  ALU.add),
                                  waits=wadd, inc=sv)
                        psT_free[tp] = (sv, sv.n)
                tmpy_free[0] = sv.n
                tmpy_free[1] = sv.n
            v_h1 = sv.n

            for tt in range(NT2):
                hb = h1blk.ap()[:, tt * E:(tt + 1) * E]
                r0 = tau * TAU + tt * 128

                if mode == "p1":
                    for q in range(nq):
                        self.emit("vector",
                                  lambda e, q=q, hb=hb: e.bn_stats(stats[:, q * 6:(q + 1) * 6],
                                                                   hb[:, q * 512:(q + 1) * 512]),
                                  waits=[(sv, v_h1)] if q == 0 else ())
                    self.emit("vector", lambda e: e.bn_aggr(stats[:, S:S + 2], stats[:, 0:S]),
                              inc=sv)
                    v_mv = sv.n
                    self.emit("scalar", lambda e: e.activation(stats[:, S + 2:S + 3],
                                                               stats[:, S + 1:S + 2], AF.Ln,
                                                               bias=self.epsv[:, 0:1]),
                              waits=[(sv, v_mv)])
                    self.emit("scalar", lambda e: e.activation(stats[:, S + 3:S + 4],
                                                               stats[:, S + 2:S + 3], AF.Exp,
                                                               scale=-0.5),
                              inc=sa)
                    a_rstd = sa.n
                    w = [(sa, a_rstd)]
                    if h1n_free:
                        w.append(h1n_free)
                    self.emit("vector",
                              lambda e, hb=hb: e.tensor_scalar(h1n[:], hb, stats[:, S:S + 1],
                                                               stats[:, S + 3:S + 4],
                                                               ALU.subtract, ALU.mult),
                              waits=w, inc=sv)
                    v_h1n = sv.n
                    h1n_free = self.dma("sync", spill[r0:r0 + 128, :], h1n[:],
                                        spsem[0], waits=[(sv, v_h1n)])
                    rms_src = h1n.ap()[:, :]
                    v_src = v_h1n
                else:
                    hb_free[tt] = self.dma("sync", spill[r0:r0 + 128, :], hb,
                                           spsem[tt % 2], waits=[(sv, v_h1)])
                    rms_src = hb
                    v_src = v_h1

                for q in range(nq):
                    self.emit("vector",
                              lambda e, q=q, rs=rms_src: e.bn_stats(stats[:, q * 6:(q + 1) * 6],
                                                                    rs[:, q * 512:(q + 1) * 512]),
                              waits=[(sv, v_src)] if q == 0 else ())
                self.emit("vector", lambda e: e.bn_aggr(stats[:, S:S + 2], stats[:, 0:S]))
                self.emit("vector", lambda e: e.tensor_tensor(stats[:, S + 4:S + 5],
                                                              stats[:, S:S + 1],
                                                              stats[:, S:S + 1], ALU.mult))
                self.emit("vector", lambda e: e.tensor_tensor(stats[:, S + 5:S + 6],
                                                              stats[:, S + 4:S + 5],
                                                              stats[:, S + 1:S + 2], ALU.add),
                          inc=sv)
                v_msq = sv.n
                self.emit("scalar", lambda e: e.activation(stats[:, S + 6:S + 7],
                                                           stats[:, S + 5:S + 6], AF.Ln,
                                                           bias=self.epsv[:, 1:2]),
                          waits=[(sv, v_msq)])
                self.emit("scalar", lambda e: e.activation(stats[:, S + 7:S + 8],
                                                           stats[:, S + 6:S + 7], AF.Exp,
                                                           scale=-0.5),
                          inc=sa)
                a_rms = sa.n
                w = [(sa, a_rms)]
                if xlnt_free:
                    w.append((sv, xlnt_free))
                self.emit("vector",
                          lambda e, rs=rms_src: e.scalar_tensor_tensor(xlnt[:], rs,
                                                                       stats[:, S + 7:S + 8],
                                                                       lnwbc[:], ALU.mult, ALU.mult),
                          waits=w, inc=sv)
                v_xln = sv.n
                for g0 in range(0, EC, 8):
                    gn = min(8, EC - g0)
                    for j in range(gn):
                        cc = g0 + j
                        w = []
                        if j == 0:
                            w.append((sv, v_xln))
                            if ptrb_free:
                                w.append(ptrb_free)
                        self.tr(ptrb[:, j * 128:(j + 1) * 128], xlnt[:, cc * 128:(cc + 1) * 128],
                                identb_s[:], waits=w, inc=sp if j == gn - 1 else None)
                    p_tr = sp.n
                    for j in range(gn):
                        cc = g0 + j
                        self.emit("vector",
                                  lambda e, cc=cc, j=j, r0=r0:
                                  e.tensor_copy(xT3[:, cc, r0:r0 + 128],
                                                ptrb[:, j * 128:(j + 1) * 128]),
                                  waits=[(sp, p_tr)] if j == 0 else (),
                                  inc=sv if j == gn - 1 else None)
                    ptrb_free = (sv, sv.n)
                xlnt_free = sv.n
                xl_free[tt % 2] = sv.n

        self.barrier()
        for cm in reversed(cms + pcms):
            cm.__exit__(None, None, None)

    def gate_phase(self, xT3, gw_bf, wfull_d, identf_s):
        c = self.cfg
        T, E, EC, TT, NE, TOPK = c["T"], c["E"], c["EC"], c["TT"], c["NE"], c["TOPK"]
        sp, sv, sa = (self.S[k] for k in ("sp", "sv", "sa"))
        cms = []
        def sba(name, shape, dt):
            h, cm = self.alloc_sbuf(name, shape, dt)
            cms.append(cm)
            return h
        gwT = sba("gwT", [128, EC * NE], BF16)
        logT = sba("logT", [NE, T], F32)
        probs = sba("probs", [128, NE], F32)
        work = sba("work", [128, NE], F32)
        eqm = sba("eqm", [128, NE], F32)
        gst = sba("gstat", [128, 8], F32)
        wfT = sba("wfT", [NE, T], F32)
        psT, psT_cm = self.alloc_psum("gpsT", [128, 512], F32)
        pcms = [psT_cm]

        s_gw = self.sem("g_gw")
        w_gw = self.dma("sync", gwT[:], gw_bf, s_gw, waits=[self.cast_thr["gw"]])
        gwT3 = gwT.ap().rearrange("p (c r) -> p c r", c=EC)

        psT_free = None
        for nb in range(T // 512):
            for kc in range(EC):
                w = []
                if kc == 0:
                    w.append(w_gw)
                    if psT_free:
                        w.append(psT_free)
                self.mm(psT[0:NE, 0:512], gwT3[:, kc, :],
                        xT3[:, kc, nb * 512:(nb + 1) * 512],
                        start=(kc == 0), stop=(kc == EC - 1),
                        waits=w, inc=sp if kc == EC - 1 else None)
            p_l = sp.n
            self.emit("vector",
                      lambda e, nb=nb: e.tensor_copy(logT[:, nb * 512:(nb + 1) * 512],
                                                     psT[0:NE, 0:512]),
                      waits=[(sp, p_l)], inc=sv)
            psT_free = (sv, sv.n)
        v_log = sv.n

        eqm_free = 0
        for ti in range(TT):
            w = [(sv, v_log)] if ti == 0 else []
            if psT_free:
                w.append(psT_free)
            self.tr(psT[:, 0:NE], logT[:, ti * 128:(ti + 1) * 128],
                    identf_s[0:NE, 0:NE], waits=w, inc=sp)
            p_t = sp.n
            self.emit("vector", lambda e: e.tensor_copy(work[:], psT[:, 0:NE]),
                      waits=[(sp, p_t)], inc=sv)
            psT_free = (sv, sv.n)
            self.emit("vector", lambda e: e.tensor_reduce(gst[:, 0:1], work[:], AX.X,
                                                          ALU.max, negate=True), inc=sv)
            v_m = sv.n
            self.emit("scalar", lambda e: e.activation(probs[:], work[:], AF.Exp,
                                                       bias=gst[:, 0:1],
                                                       accum_out=gst[:, 1:2]),
                      waits=[(sv, v_m)], inc=sa)
            a_e = sa.n
            self.emit("vector", lambda e: e.reciprocal(gst[:, 2:3], gst[:, 1:2]),
                      waits=[(sa, a_e)])
            self.emit("vector", lambda e: e.tensor_scalar(probs[:], probs[:], gst[:, 2:3],
                                                          None, ALU.mult))
            self.emit("vector", lambda e: e.tensor_copy(work[:], probs[:]))
            self.emit("vector", lambda e: e.memset(gst[:, 3:4], 0.0))
            first_eq = True
            for it in range(TOPK):
                self.emit("vector", lambda e: e.tensor_reduce(gst[:, 4:5], work[:], AX.X, ALU.max))
                self.emit("vector", lambda e: e.tensor_tensor(gst[:, 3:4], gst[:, 3:4],
                                                              gst[:, 4:5], ALU.add))
                if it < TOPK - 1:
                    w = [(sp, eqm_free)] if (first_eq and eqm_free) else []
                    first_eq = False
                    self.emit("vector", lambda e: e.tensor_scalar(eqm[:], work[:], gst[:, 4:5],
                                                                  None, ALU.is_equal), waits=w)
                    self.emit("vector", lambda e: e.scalar_tensor_tensor(work[:], eqm[:], -2.0,
                                                                         work[:], ALU.mult, ALU.add))
            w = [(sp, eqm_free)] if (first_eq and eqm_free) else []
            self.emit("vector", lambda e: e.tensor_scalar(eqm[:], probs[:], gst[:, 4:5],
                                                          None, ALU.is_ge), waits=w)
            self.emit("vector", lambda e: e.reciprocal(gst[:, 5:6], gst[:, 3:4]))
            self.emit("vector", lambda e: e.tensor_tensor(eqm[:], eqm[:], probs[:], ALU.mult))
            self.emit("vector", lambda e: e.tensor_scalar(eqm[:], eqm[:], gst[:, 5:6],
                                                          None, ALU.mult), inc=sv)
            v_wf = sv.n
            w = [(sv, v_wf)]
            if psT_free:
                w.append(psT_free)
            self.tr(psT[0:NE, 0:128], eqm[:], identf_s[:], waits=w, inc=sp)
            p_t2 = sp.n
            eqm_free = p_t2
            self.emit("vector",
                      lambda e, ti=ti: e.tensor_copy(wfT[:, ti * 128:(ti + 1) * 128],
                                                     psT[0:NE, 0:128]),
                      waits=[(sp, p_t2)], inc=sv)
            psT_free = (sv, sv.n)
        v_wfT = sv.n
        s_wf = self.sem("g_wf")
        self.wfull_done = self.dma("sync", wfull_d, wfT[:], s_wf, waits=[(sv, v_wfT)])

        self.barrier()
        for cm in reversed(cms + pcms):
            cm.__exit__(None, None, None)

    def moe_phase(self, xT3, w1_bf, w2_bf, shgu_bf, shdn_bf, wfull_d, res1, res2,
                  out, onesf, identf_s):
        c = self.cfg
        T, E, EC = c["T"], c["E"], c["EC"]
        NE, I1, IC, SHI = c["NE"], c["I1"], c["IC"], c["SHI"]
        TU, NU = c["TU"], c["NU"]
        sp, sv, sa = (self.S[k] for k in ("sp", "sv", "sa"))
        NB = TU // 512

        specs = []
        for e in range(NE):
            specs.append(dict(kind="reg", e=e, gt0=0, ut0=IC, wsrc=w1_bf[e],
                              w2src=w2_bf[e], ck1=f"w1_{e}", ck2=f"w2_{e}"))
        specs.append(dict(kind="sh", e=NE, gt0=0, ut0=2 * IC, wsrc=shgu_bf,
                          w2src=shdn_bf[0], ck1="shgu", ck2="shdn"))
        specs.append(dict(kind="sh", e=NE + 1, gt0=IC, ut0=3 * IC, wsrc=shgu_bf,
                          w2src=shdn_bf[1], ck1="shgu", ck2="shdn"))

        cms = []
        def sba(name, shape, dt):
            h, cm = self.alloc_sbuf(name, shape, dt)
            cms.append(cm)
            return h
        routed = sba("routed", [128, EC * TU], F32)
        psG, psG_cm = self.alloc_psum("psG", [128, 1024], F32)
        psU, psU_cm = self.alloc_psum("psU", [128, 1024], F32)
        psD, psD_cm = self.alloc_psum("psD", [128, 2048], F32)
        pcms = [psG_cm, psU_cm, psD_cm]

        routed3 = routed.ap().rearrange("p (c t) -> p c t", c=EC)

        sW1g = [self.sem(f"mw1g{j}") for j in range(2)]
        sW1u = [self.sem(f"mw1u{j}") for j in range(2)]
        sW2 = [self.sem(f"mw2{j}") for j in range(2)]
        sWfr = self.sem("mwfr")

        w1g_done = [None, None]; w1g_free = [0, 0]
        w1u_done = [None, None]; w1u_free = [0, 0]
        w2_done = [None, None]; w2_free = [0, 0]
        gst_free = [0, 0]; ust_free = [0, 0]
        wbc_free = [0, 0]
        wfrow_free = 0
        psG_free = None; psU_free = None
        psD_free = [None, None]
        act_free = 0

        for u in range(NU):
            t0 = u * TU
            mcms = []
            def mba(name, shape, dt, u=u, mcms=mcms):
                h, cm = self.alloc_sbuf(f"u{u}" + name, shape, dt)
                mcms.append(cm)
                return h
            act = mba("act", [128, IC * TU], BF16)
            w1g = [mba(f"w1g{j}", [128, EC * 128], BF16) for j in range(2)]
            w1u = [mba(f"w1u{j}", [128, EC * 128], BF16) for j in range(2)]
            w2b = [mba(f"w2b{j}", [128, IC * 128], BF16) for j in range(2)]
            gstp = [mba(f"gstp{j}", [128, TU], BF16) for j in range(2)]
            ustp = [mba(f"ustp{j}", [128, TU], BF16) for j in range(2)]
            wbc = [mba(f"wbc{j}", [128, TU], F32) for j in range(2)]
            wfrow = mba("wfrow", [1, TU], F32)
            act3 = act.ap().rearrange("p (c t) -> p c t", c=IC)

            for si, spec in enumerate(specs):
                eslot = spec["e"] % 2
                v_wbc = None
                if spec["kind"] == "reg":
                    w = [self.wfull_done]
                    if wfrow_free:
                        w.append((sp, wfrow_free))
                    g_wfr = self.dma("sync", wfrow[:],
                                     wfull_d[spec["e"]:spec["e"] + 1, t0:t0 + TU],
                                     sWfr, waits=w)
                    for nb in range(NB):
                        w = [g_wfr]
                        if nb == 0:
                            if wbc_free[eslot]:
                                w.append((sv, wbc_free[eslot]))
                            if psG_free:
                                w.append(psG_free)
                        self.mm(psG[:, nb * 512:(nb + 1) * 512], onesf[:],
                                wfrow[:, nb * 512:(nb + 1) * 512],
                                True, True, waits=w, inc=sp)
                        p_b = sp.n
                        self.emit("vector",
                                  lambda e_, nb=nb, sl=eslot:
                                  e_.tensor_copy(wbc[sl][:, nb * 512:(nb + 1) * 512],
                                                 psG[:, nb * 512:(nb + 1) * 512]),
                                  waits=[(sp, p_b)], inc=sv)
                    psG_free = (sv, sv.n)
                    wfrow_free = sp.n
                    v_wbc = sv.n

                for cc in range(IC):
                    slotc = cc % 2
                    w = [self.cast_thr[spec["ck1"]]]
                    if w1g_free[slotc]:
                        w.append((sp, w1g_free[slotc]))
                    w1g_done[slotc] = self.dma(
                        "sync", w1g[slotc][:], spec["wsrc"][spec["gt0"] + cc],
                        sW1g[slotc], waits=w)
                    w = [self.cast_thr[spec["ck1"]]]
                    if w1u_free[slotc]:
                        w.append((sp, w1u_free[slotc]))
                    w1u_done[slotc] = self.dma(
                        "sync", w1u[slotc][:], spec["wsrc"][spec["ut0"] + cc],
                        sW1u[slotc], waits=w)

                    wg3 = w1g[slotc].ap().rearrange("p (c r) -> p c r", c=EC)
                    wu3 = w1u[slotc].ap().rearrange("p (c r) -> p c r", c=EC)
                    for kc in range(EC):
                        for ps, w3, done_, free_ in ((psG, wg3, w1g_done, psG_free),
                                                     (psU, wu3, w1u_done, psU_free)):
                            for nb in range(NB):
                                w = []
                                if kc == 0 and nb == 0:
                                    w.append(done_[slotc])
                                    if free_:
                                        w.append(free_)
                                    if ps is psG and act_free:
                                        w.append((sp, act_free))
                                last = (kc == EC - 1 and nb == NB - 1 and ps is psU)
                                self.mm(ps[:, nb * 512:(nb + 1) * 512], w3[:, kc, :],
                                        xT3[:, kc, t0 + nb * 512:t0 + (nb + 1) * 512],
                                        start=(kc == 0), stop=(kc == EC - 1),
                                        waits=w, inc=sp if last else None)
                    p_um = sp.n
                    p_gm = p_um
                    w1g_free[slotc] = p_um
                    w1u_free[slotc] = p_um

                    w = [(sp, p_gm)]
                    if gst_free[slotc]:
                        w.append((sv, gst_free[slotc]))
                    self.emit("scalar",
                              lambda e_, sl=slotc: e_.activation(gstp[sl][:], psG[:, 0:c['TU']],
                                                                 AF.Silu),
                              waits=w, inc=sa)
                    a_g = sa.n
                    psG_free = (sa, a_g)
                    w = [(sp, p_um)]
                    if v_wbc is not None:
                        w.append((sv, v_wbc))
                    if ust_free[slotc]:
                        w.append((sv, ust_free[slotc]))
                    if spec["kind"] == "reg":
                        self.emit("vector",
                                  lambda e_, sl=eslot, s2=slotc:
                                  e_.tensor_tensor(ustp[s2][:], psU[:, 0:c['TU']], wbc[sl][:],
                                                   ALU.mult),
                                  waits=w, inc=sv)
                    else:
                        self.emit("vector",
                                  lambda e_, s2=slotc: e_.tensor_copy(ustp[s2][:], psU[:, 0:c['TU']]),
                                  waits=w, inc=sv)
                    v_u = sv.n
                    psU_free = (sv, v_u)
                    w = [(sa, a_g), (sv, v_u)]
                    self.emit("vector",
                              lambda e_, cc=cc, s2=slotc:
                              e_.tensor_tensor(act3[:, cc, :], gstp[s2][:], ustp[s2][:], ALU.mult),
                              waits=w, inc=sv)
                    gst_free[slotc] = sv.n
                    ust_free[slotc] = sv.n
                act_done = sv.n
                if spec["kind"] == "reg":
                    wbc_free[eslot] = act_done

                for m in range(EC):
                    slotm = m % 2
                    w = [self.cast_thr[spec["ck2"]]]
                    if w2_free[slotm]:
                        w.append((sp, w2_free[slotm]))
                    w2_done[slotm] = self.dma(
                        "sync", w2b[slotm][:], spec["w2src"][m],
                        sW2[slotm], waits=w)
                    w23 = w2b[slotm].ap().rearrange("p (c r) -> p c r", c=IC)
                    half = (m & 1) * 1024
                    for kc in range(IC):
                        for nb in range(NB):
                            w = []
                            if kc == 0 and nb == 0:
                                w.append(w2_done[slotm])
                                w.append((sv, act_done))
                                if psD_free[m & 1]:
                                    w.append(psD_free[m & 1])
                            last = (kc == IC - 1 and nb == NB - 1)
                            self.mm(psD[:, half + nb * 512:half + (nb + 1) * 512],
                                    w23[:, kc, :],
                                    act3[:, kc, nb * 512:(nb + 1) * 512],
                                    start=(kc == 0), stop=(kc == IC - 1),
                                    waits=w, inc=sp if last else None)
                    p_d = sp.n
                    w2_free[slotm] = p_d
                    if m == EC - 1:
                        act_free = p_d
                    if si == 0:
                        self.emit("vector",
                                  lambda e_, m=m, half=half: e_.tensor_copy(
                                      routed3[:, m, :], psD[:, half:half + c['TU']]),
                                  waits=[(sp, p_d)], inc=sv)
                    else:
                        self.emit("vector",
                                  lambda e_, m=m, half=half: e_.tensor_tensor(
                                      routed3[:, m, :], routed3[:, m, :],
                                      psD[:, half:half + c['TU']], ALU.add),
                                  waits=[(sp, p_d)], inc=sv)
                    psD_free[m & 1] = (sv, sv.n)
            v_routed = sv.n

            self.barrier()
            for cm in reversed(mcms):
                cm.__exit__(None, None, None)
            pf = self.final_epilogue(u, routed3, res1, res2, out, identf_s,
                                     psD[:, 0:512], psD_free[0], v_routed)
            psD_free = [pf, pf]

        self.barrier()
        for cm in reversed(cms + pcms):
            cm.__exit__(None, None, None)

    def final_epilogue(self, u, routed3, res1, res2, out, identf_s, psT,
                       psT_free, v_routed):
        c = self.cfg
        T, E, EC, TU = c["T"], c["E"], c["EC"], c["TU"]
        sp, sv, sa = (self.S[k] for k in ("sp", "sv", "sa"))
        cms = []
        def sba(name, shape, dt):
            h, cm = self.alloc_sbuf(f"f{u}" + name, shape, dt)
            cms.append(cm)
            return h
        r1l = [sba(f"r1l{j}", [128, E], F32) for j in range(2)]
        r2l = [sba(f"r2l{j}", [128, E], F32) for j in range(2)]
        ftile = [sba(f"ftile{j}", [128, E], F32) for j in range(2)]
        ot = [sba(f"ot{j}", [128, E], F32) for j in range(2)]
        nq = E // 512
        fstats = [sba(f"fstats{j}", [128, 6 * nq + 4], F32) for j in range(2)]
        S = 6 * nq

        sR1 = [self.sem(f"fr1{j}") for j in range(2)]
        sR2 = [self.sem(f"fr2{j}") for j in range(2)]
        sOut = [self.sem(f"out{j}") for j in range(2)]

        NT = TU // 128
        r1_done = [None, None]; r2_done = [None, None]
        r1_free = [0, 0]; r2_free = [0, 0]
        ot_free = [None, None]
        ftile_free = [0, 0]
        a_r_prev = [0, 0]

        for tt in range(NT):
            slot = tt % 2
            ft = ftile[slot]
            fs = fstats[slot]
            r0 = u * TU + tt * 128
            w = [(sv, r1_free[slot])] if r1_free[slot] else []
            r1_done[slot] = self.dma("sync", r1l[slot][:], res1[r0:r0 + 128, :],
                                     sR1[slot], waits=w)
            w = [(sv, r2_free[slot])] if r2_free[slot] else []
            r2_done[slot] = self.dma("sync", r2l[slot][:], res2[r0:r0 + 128, :],
                                     sR2[slot], waits=w)

            for q in range(nq):
                w = []
                if q == 0:
                    w.append(r1_done[slot])
                    w.append(r2_done[slot])
                    if ftile_free[slot]:
                        w.append((sv, ftile_free[slot]))
                self.emit("vector",
                          lambda e, q=q, sl=slot, ft=ft:
                          e.tensor_tensor(ft[:, q * 512:(q + 1) * 512],
                                          r1l[sl][:, q * 512:(q + 1) * 512],
                                          r2l[sl][:, q * 512:(q + 1) * 512], ALU.add),
                          waits=w, inc=sv if q == nq - 1 else None)
            r1_free[slot] = sv.n
            r2_free[slot] = sv.n
            v_f = sv.n

            for g0 in range(0, EC, 4):
                gn = min(4, EC - g0)
                for j in range(gn):
                    m = g0 + j
                    w = []
                    if j == 0:
                        w.append((sv, v_routed))
                        if psT_free:
                            w.append(psT_free)
                    self.tr(psT[:, j * 128:(j + 1) * 128],
                            routed3[:, m, tt * 128:(tt + 1) * 128],
                            identf_s[:], waits=w, inc=sp if j == gn - 1 else None)
                p_t = sp.n
                self.emit("vector",
                          lambda e, g0=g0, gn=gn, ft=ft:
                          e.tensor_tensor(ft[:, g0 * 128:(g0 + gn) * 128],
                                          ft[:, g0 * 128:(g0 + gn) * 128],
                                          psT[:, 0:gn * 128], ALU.add),
                          waits=[(sp, p_t), (sv, v_f)], inc=sv)
                psT_free = (sv, sv.n)
            v_sum = sv.n

            for q in range(nq):
                w = [] if q else [(sv, v_sum)]
                if q == 0 and a_r_prev[slot]:
                    w.append((sa, a_r_prev[slot]))
                self.emit("vector",
                          lambda e, q=q, ft=ft, fs=fs:
                          e.bn_stats(fs[:, q * 6:(q + 1) * 6],
                                     ft[:, q * 512:(q + 1) * 512]),
                          waits=w)
            self.emit("vector", lambda e, fs=fs: e.bn_aggr(fs[:, S:S + 2], fs[:, 0:S]),
                      inc=sv)
            v_mv = sv.n
            self.emit("scalar", lambda e, fs=fs: e.activation(fs[:, S + 2:S + 3],
                                                              fs[:, S + 1:S + 2], AF.Ln,
                                                              bias=self.epsv[:, 0:1]),
                      waits=[(sv, v_mv)])
            self.emit("scalar", lambda e, fs=fs: e.activation(fs[:, S + 3:S + 4],
                                                              fs[:, S + 2:S + 3], AF.Exp,
                                                              scale=-0.5),
                      inc=sa)
            a_r = sa.n
            a_r_prev[slot] = a_r
            w = [(sa, a_r)]
            if ot_free[slot]:
                w.append(ot_free[slot])
            self.emit("vector",
                      lambda e, sl=slot, ft=ft, fs=fs:
                      e.tensor_scalar(ot[sl][:], ft[:],
                                      fs[:, S:S + 1], fs[:, S + 3:S + 4],
                                      ALU.subtract, ALU.mult),
                      waits=w, inc=sv)
            v_ot = sv.n
            ftile_free[slot] = sv.n
            ot_free[slot] = self.dma("sync", out[r0:r0 + 128, :], ot[slot][:],
                                     sOut[slot], waits=[(sv, v_ot)])

        w0 = (sOut[0].h, sOut[0].n)
        w1 = (sOut[1].h, sOut[1].n)
        def fwait(e, w0=w0, w1=w1):
            if w0[1]:
                e.wait_ge(w0[0], w0[1])
            if w1[1]:
                e.wait_ge(w1[0], w1[1])
        self.acts["gpsimd"].append(fwait)

        self.barrier()
        for cm in reversed(cms):
            cm.__exit__(None, None, None)
        return psT_free


def make_consts(cfg):
    return dict(identb=np.eye(128, dtype=ml_dtypes.bfloat16),
                identf=np.eye(128, dtype=np.float32),
                cmask=np.triu(np.full((128, 128), NEG_BIG, np.float32), k=1))


def _tile_rows(w):
    R, C = w.shape
    return np.ascontiguousarray(
        w.reshape(R // 128, 128, C // 128, 128).transpose(0, 3, 2, 1).reshape(R // 128, 128, C))


def _tile_x(x):
    T, E = x.shape
    return np.ascontiguousarray(x.reshape(T, E // 128, 128).transpose(2, 1, 0).reshape(128, -1))


def prep_weights(inputs):
    c = REAL_CFG
    NH, QHD, NOPE, VD, NE, I1 = c["NH"], c["QHD"], c["NOPE"], c["VD"], c["NE"], c["I1"]
    qw = np.asarray(inputs["L0_ffn_q_w"])
    kvw = np.asarray(inputs["L0_ffn_kv_w"])
    qn = np.ascontiguousarray(qw.reshape(NH, QHD, -1)[:, :NOPE].reshape(NH * NOPE, -1))
    kw = np.ascontiguousarray(kvw[:NH * NOPE])
    vw = np.ascontiguousarray(kvw[NH * NOPE:NH * (NOPE + VD)])
    w1 = np.asarray(inputs["L0_ffn_w1"])
    w2 = np.asarray(inputs["L0_ffn_w2"])
    shdn = np.asarray(inputs["L0_ffn_sh_down_w"])
    gw = np.asarray(inputs["L0_ffn_gate_w"])
    E = 2048
    d = dict(
        wq_t=_tile_rows(np.asarray(inputs["L0_attn_Wq"])),
        wk_t=_tile_rows(np.asarray(inputs["L0_attn_Wk"])),
        wv_t=_tile_rows(np.asarray(inputs["L0_attn_Wv"])),
        wo_t=_tile_rows(np.asarray(inputs["L0_attn_Wo"])),
        qn_t=_tile_rows(qn), kw_t=_tile_rows(kw), vw_t=_tile_rows(vw),
        ow_t=_tile_rows(np.asarray(inputs["L0_ffn_o_w"])),
        gw_t=np.ascontiguousarray(gw.reshape(NE, E // 128, 128).transpose(2, 1, 0).reshape(128, -1)),
        w1_t=np.stack([_tile_rows(w1[e]) for e in range(NE)]),
        w2_t=np.stack([_tile_rows(w2[e]) for e in range(NE)]),
        shgu_t=_tile_rows(np.asarray(inputs["L0_ffn_sh_gu_w"])),
        shdn_t=np.stack([_tile_rows(np.ascontiguousarray(shdn[:, h * I1:(h + 1) * I1]))
                         for h in range(2)]),
    )
    return d


_BUILD_CACHE = {}


def build_nc(cfg=None, key="real"):
    if key in _BUILD_CACHE:
        return _BUILD_CACHE[key]
    nc = KB(cfg or REAL_CFG).build()
    _BUILD_CACHE[key] = nc
    return nc


def kernel(**inputs):
    _ = inputs.get("L0_ffn_positions")
    nc = build_nc()
    consts = make_consts(REAL_CFG)
    base = dict(
        lq1=inputs["L0_attn_lambda_q1"], lk1=inputs["L0_attn_lambda_k1"],
        lq2=inputs["L0_attn_lambda_q2"], lk2=inputs["L0_attn_lambda_k2"],
        subw=inputs["L0_attn_subln_weight"],
        ln1w=inputs["L0_ffn_ln1_w"], ln2w=inputs["L0_ffn_ln2_w"],
        **prep_weights(inputs), **consts)
    x = inputs["L0_attn_query"]
    in_maps = [dict(base, xq=np.ascontiguousarray(x[b]), xt=_tile_x(x[b]))
               for b in range(x.shape[0])]
    res = run_bass_kernel_spmd(nc, in_maps, core_ids=list(range(len(in_maps))))
    return np.stack([res.results[b]["out"] for b in range(len(in_maps))], axis=0)
